# version 13
# speedup vs baseline: 1.2144x; 1.0131x over previous
import contextlib
from collections import defaultdict

import numpy as np
import ml_dtypes

import concourse.bass as bass
import concourse.mybir as mybir
from concourse.bass_utils import run_bass_kernel_spmd

F32 = mybir.dt.float32
BF16 = mybir.dt.bfloat16
AF = mybir.ActivationFunctionType
ALU = mybir.AluOpType
AX = mybir.AxisListType

C = 1024
NCT = 8
MAC = 512
MIC = 128
EPS = 1e-5

ENGINES = ("tensor", "vector", "scalar", "gpsimd", "sync")


class _Op:
    __slots__ = ("idx", "engine", "fn", "deps", "seq", "dma")

    def __init__(self, idx, engine, fn, deps, seq, dma):
        self.idx = idx
        self.engine = engine
        self.fn = fn
        self.deps = deps
        self.seq = seq
        self.dma = dma


class Prog:
    def __init__(self, nc):
        self.nc = nc
        self.ops = []
        self.counts = {e: 0 for e in ENGINES}
        self.dma_counts = defaultdict(int)
        self.dma_sem_names = []

    def op(self, engine, fn, deps=()):
        self.counts[engine] += 1
        o = _Op(len(self.ops), engine, fn, self._norm(deps), self.counts[engine], None)
        self.ops.append(o)
        return (engine, self.counts[engine])

    def dma(self, queue, dma_sem, fn, deps=()):
        if dma_sem not in self.dma_counts:
            self.dma_sem_names.append(dma_sem)
        self.dma_counts[dma_sem] += 16
        o = _Op(len(self.ops), queue, fn, self._norm(deps), self.dma_counts[dma_sem], dma_sem)
        self.ops.append(o)
        return (dma_sem, self.dma_counts[dma_sem])

    @staticmethod
    def _norm(deps):
        out = []
        seen = set()
        for d in deps:
            if d is None or d in seen:
                continue
            seen.add(d)
            out.append(d)
        return out

    def barrier_tokens(self):
        toks = [(e, self.counts[e]) for e in ENGINES if self.counts[e] > 0]
        toks += [(s, self.dma_counts[s]) for s in self.dma_sem_names]
        return toks

    def hard_barrier(self):
        toks = self.barrier_tokens()
        for e in ENGINES:
            if self.counts[e] > 0:
                self.op(e, lambda eng: eng.nop(), deps=toks)
        return self.barrier_tokens()

    def emit(self, block, sems, dma_sems):
        waited = defaultdict(set)
        for o in self.ops:
            for (s, v) in o.deps:
                waited[s].add(v)
        streams = {e: [o for o in self.ops if o.engine == e] for e in ENGINES}
        dq = {}
        for o in self.ops:
            if o.dma:
                q = dq.setdefault(o.dma, o.engine)
                assert q == o.engine, f"dma sem {o.dma} used from two queues"
        dcross = {s: sorted(waited[s]) for s in self.dma_sem_names}

        def emit_stream(eng_api, engine):
            max_w = defaultdict(int)
            crossing = sorted(waited[engine])
            ci = 0
            count = 0
            dcnt = defaultdict(int)
            dci = defaultdict(int)
            for o in streams[engine]:
                if not o.dma:
                    while ci < len(crossing) and crossing[ci] <= count:
                        v = crossing[ci]
                        if max_w[engine] < v < o.seq:
                            eng_api.wait_ge(sems[engine], v)
                            max_w[engine] = v
                        ci += 1
                else:
                    s = o.dma
                    cr = dcross[s]
                    while dci[s] < len(cr) and cr[dci[s]] <= dcnt[s]:
                        v = cr[dci[s]]
                        if max_w[s] < v:
                            eng_api.wait_ge(dma_sems[s], v)
                            max_w[s] = v
                        dci[s] += 1
                    dcnt[s] += 16
                for (s, v) in o.deps:
                    if v > max_w[s]:
                        eng_api.wait_ge(sems[s] if s in ENGINES else dma_sems[s], v)
                        max_w[s] = v
                inst = o.fn(eng_api)
                if o.dma:
                    inst.then_inc(dma_sems[o.dma], 16)
                else:
                    inst.then_inc(sems[engine], 1)
                    count += 1

        for engine in ENGINES:
            if not streams[engine]:
                continue
            section = getattr(block, engine)

            def make(engine):
                def _(eng_api):
                    emit_stream(eng_api, engine)
                return _
            section(make(engine))

    def build_and_emit(self, stack):
        nc = self.nc
        sems = {}
        for e in ENGINES:
            if any(o.engine == e and not o.dma for o in self.ops):
                sems[e] = stack.enter_context(nc.semaphore(f"sem_{e}"))
        dma_sems = {}
        for name in self.dma_sem_names:
            dma_sems[name] = stack.enter_context(nc.semaphore(f"ds_{name}"))
        block = stack.enter_context(nc.Block())
        self.emit(block, sems, dma_sems)


class Buf:
    __slots__ = ("w", "rs", "ro")

    def __init__(self, readonly=False):
        self.w = None
        self.rs = []
        self.ro = readonly

    def r(self):
        return [self.w] if self.w is not None else []

    def rc(self, tok):
        if not self.ro:
            self.rs.append(tok)

    def wdeps(self):
        d = list(self.rs)
        if self.w is not None:
            d.append(self.w)
        return d

    def wc(self, tok):
        self.w = tok
        self.rs = []


def build(T=8192, dbg=False, stop_after=None, a1_parts=127, b_parts=255):
    nc = bass.Bass("TRN2", target_bir_lowering=False)
    NMAC = T // MAC

    X = nc.dram_tensor("X", [T, C], F32, kind="ExternalInput")
    OUT = nc.dram_tensor("OUT", [T, C], F32, kind="ExternalOutput")
    WRT = nc.dram_tensor("WRT", [C, C], F32, kind="ExternalInput")
    WKT = nc.dram_tensor("WKT", [C, C], F32, kind="ExternalInput")
    WVT = nc.dram_tensor("WVT", [C, C], F32, kind="ExternalInput")
    WG = nc.dram_tensor("WG", [C, 3 * C], F32, kind="ExternalInput")
    LRW = nc.dram_tensor("LRW", [1, C], F32, kind="ExternalInput")
    OUTW = nc.dram_tensor("OUTW", [C, C], F32, kind="ExternalInput")
    W1 = nc.dram_tensor("W1", [C, C], F32, kind="ExternalInput")
    W2 = nc.dram_tensor("W2", [C, C], F32, kind="ExternalInput")
    rows_in = {}
    for nm, n in [("DNG", C), ("DNB", C), ("MNG", C), ("MNB", C), ("MUR", C),
                  ("MUK", C), ("GATESB", 3 * C), ("OUTB", C), ("B1", C), ("B2", C),
                  ("LRB", 1)]:
        rows_in[nm] = nc.dram_tensor(nm, [1, n], F32, kind="ExternalInput")
    IDENT_F = nc.dram_tensor("IDENT_F", [128, 128], F32, kind="ExternalInput")
    IDENT_B = nc.dram_tensor("IDENT_B", [128, 128], BF16, kind="ExternalInput")
    ONESB = nc.dram_tensor("ONESB", [1, 128], BF16, kind="ExternalInput")
    LMASK = nc.dram_tensor("LMASK", [128, 128], F32, kind="ExternalInput")
    TWOI = nc.dram_tensor("TWOI", [128, 128], BF16, kind="ExternalInput")
    skind = {"kind": "ExternalOutput"} if dbg else {}
    X1S = nc.dram_tensor("X1S", [T, C], F32, **skind)
    QTS = nc.dram_tensor("QTS", [NCT, 128, T], BF16, **skind)
    if dbg:
        SD = nc.dram_tensor("SD", [128, NCT * C], F32, kind="ExternalOutput")
        GD = nc.dram_tensor("GD", [128, NCT * C], F32, kind="ExternalOutput")
        A1D = nc.dram_tensor("A1D", [T, C], F32, kind="ExternalOutput")
        X2D = nc.dram_tensor("X2D", [T, C], F32, kind="ExternalOutput")
        YD = nc.dram_tensor("YD", [NCT, 128, T], BF16, kind="ExternalOutput")
        A2D = nc.dram_tensor("A2D", [T, C], F32, kind="ExternalOutput")
        HD = nc.dram_tensor("HD", [NCT, 128, T], BF16, kind="ExternalOutput")

    p = Prog(nc)
    root = contextlib.ExitStack()

    def SB(stk, n, sh, dt):
        return stk.enter_context(nc.sbuf_tensor(n, sh, dt))

    with root:
        ident_f = SB(root, "ident_f", [128, 128], F32)
        ident_b = SB(root, "ident_b", [128, 128], BF16)
        onesb = SB(root, "onesb", [1, 128], BF16)
        lmask = SB(root, "lmask", [128, 128], F32)
        twoi = SB(root, "twoi", [128, 128], BF16)
        twoi4 = SB(root, "twoi4", [128, 512], BF16)
        ccols = SB(root, "ccols", [128, 8], F32)
        onef = SB(root, "onef", [1, 1], F32)
        dng_c = SB(root, "dng_c", [128, NCT], F32)
        mng_c = SB(root, "mng_c", [128, NCT], F32)
        dnb_cb = SB(root, "dnb_cb", [128, NCT], BF16)
        mnb_cb = SB(root, "mnb_cb", [128, NCT], BF16)
        mur_c = SB(root, "mur_c", [128, NCT], F32)
        muk_c = SB(root, "muk_c", [128, NCT], F32)
        qb_c = SB(root, "qb_c", [128, NCT], F32)
        kb_c = SB(root, "kb_c", [128, NCT], F32)
        b1_c = SB(root, "b1_c", [128, NCT], F32)
        lrb_bc = SB(root, "lrb_bc", [128, 1], F32)
        lrw_b = SB(root, "lrw_b", [128, NCT], BF16)
        browv_b = SB(root, "browv_b", [1, C], BF16)
        outb_rb = SB(root, "outb_rb", [1, C], BF16)
        b2_rb = SB(root, "b2_rb", [1, C], BF16)
        stt = SB(root, "stt", [128, 32], F32)
        sqd = SB(root, "sqd", [128, 2 * C], BF16)

        PA = root.enter_context(nc.psum_tensor("PA", [128, 1024], F32))
        PB = root.enter_context(nc.psum_tensor("PB", [128, 1024], F32))
        PW = root.enter_context(nc.psum_tensor("PW", [128, 512], F32))
        PX = root.enter_context(nc.psum_tensor("PX", [128, 512], F32))
        PT = root.enter_context(nc.psum_tensor("PT", [128, 1024], BF16))
        PS = root.enter_context(nc.psum_tensor("PS", [128, 512], F32))
        bPA, bPB, bPW, bPX, bPT, bPS = (Buf() for _ in range(6))

        bstt = [Buf() for _ in range(4)]
        bsqd = [Buf(), Buf()]

        bccols = Buf()
        btwoi4 = Buf()
        bonef = Buf()
        bdng, bmng, bdnbb, bmnbb, bmur, bmuk = (Buf() for _ in range(6))
        bqb, bkb, bb1c, blrb, blrwb, bbrowv, boutb, bb2 = (Buf() for _ in range(8))

        gstk = contextlib.ExitStack()
        wg_s = SB(gstk, "wg_s", [128, NCT * 3 * C], BF16)
        bwg = Buf(readonly=True)
        fwstk = contextlib.ExitStack()
        wr_s = SB(fwstk, "wr_s", [128, NCT * C], BF16)
        wk_s = SB(fwstk, "wk_s", [128, NCT * C], BF16)
        wv_s = SB(fwstk, "wv_s", [128, NCT * C], BF16)
        bwr, bwk, bwv = (Buf(readonly=True) for _ in range(3))
        initstk = contextlib.ExitStack()
        rowbuf = SB(initstk, "rowbuf", [1, 3 * C], F32)
        brow = Buf()
        brow2t = SB(initstk, "brow2", [1, 3 * C], F32)
        bbrow2 = Buf()
        colf = SB(initstk, "colf", [128, NCT], F32)
        bcolf = Buf()
        wstage = SB(initstk, "wstage", [128, 3 * C], F32)
        wstage2 = SB(initstk, "wstage2", [128, 3 * C], F32)
        wraw = SB(initstk, "wraw", [128, 3 * C], BF16)
        bwstage, bwstage2, bwraw = Buf(), Buf(), Buf()
        wstages = [(wstage, bwstage), (wstage2, bwstage2)]

        ld = {}
        for nm, dst, src in [("IDENT_F", ident_f, IDENT_F), ("IDENT_B", ident_b, IDENT_B),
                             ("ONESB", onesb, ONESB), ("LMASK", lmask, LMASK),
                             ("TWOI", twoi, TWOI)]:
            ld[nm] = p.dma("sync", "cl", (lambda dst, src: lambda e: e.dma_start(
                out=dst[:, :], in_=src[:, :]))(dst, src))
        cc_toks = []
        for i, v in [(0, EPS), (1, -0.5), (2, 0.5), (3, -1.0), (4, 1.0 + EPS),
                     (5, EPS * EPS)]:
            cc_toks.append(p.op("vector", (lambda i, v: lambda e: e.memset(
                ccols[:, i:i + 1], v))(i, v)))
        bccols.wc(cc_toks[-1])
        cc = cc_toks[-1]
        t_onef = p.op("vector", lambda e: e.memset(onef[0:1, 0:1], 1.0))
        bonef.wc(t_onef)
        tw_toks = []
        for i in range(4):
            tw_toks.append(p.op("vector", (lambda i: lambda e: e.tensor_copy(
                twoi4[:, 128 * i:128 * (i + 1)], twoi[:, :]))(i),
                deps=[ld["TWOI"]] + (btwoi4.wdeps() if i == 0 else [])))
        btwoi4.wc(tw_toks[-1])

        def load_row(nm, n):
            tl = p.dma("sync", "cl", (lambda nm, n: lambda e: e.dma_start(
                out=rowbuf[0:1, 0:n], in_=rows_in[nm][0:1, 0:n]))(nm, n),
                deps=brow.wdeps())
            brow.wc(tl)
            return tl

        def row_to_cols(row_ap, row_tok, n, dst, bdst):
            mms = []
            for k in range(n // 128):
                deps = [row_tok, t_onef]
                if k == 0:
                    deps += bPS.wdeps()
                mms.append(p.op("tensor", (lambda k, row_ap: lambda e: e.matmul(
                    PS[:, k:k + 1], row_ap[0:1, 128 * k:128 * (k + 1)],
                    onef[0:1, 0:1], start=True, stop=True))(k, row_ap), deps=deps))
            bPS.wc(mms[-1])
            t_e = p.op("vector", (lambda n: lambda e: e.tensor_copy(
                dst[:, 0:n // 128], PS[:, 0:n // 128]))(n),
                deps=[mms[-1]] + bdst.wdeps())
            bPS.rc(t_e)
            bdst.wc(t_e)
            return t_e

        for nm, dst, bdst in [("DNG", dng_c, bdng), ("MNG", mng_c, bmng),
                              ("MUR", mur_c, bmur), ("MUK", muk_c, bmuk)]:
            tl = load_row(nm, C)
            te = row_to_cols(rowbuf, tl, C, dst, bdst)
            brow.rc(te)
        for nm, dstb, bdstb in [("DNB", dnb_cb, bdnbb), ("MNB", mnb_cb, bmnbb)]:
            tl = load_row(nm, C)
            te = row_to_cols(rowbuf, tl, C, colf, bcolf)
            brow.rc(te)
            tc = p.op("scalar", (lambda dstb: lambda e: e.activation(
                out=dstb[:, :], in_=colf[:, :], func=AF.Identity))(dstb),
                deps=[te] + bdstb.wdeps())
            bcolf.rc(tc)
            bdstb.wc(tc)

        fold_regions = [PA[0:1, 0:512], PA[0:1, 512:1024], PB[0:1, 0:512],
                        PB[0:1, 512:1024], PW[0:1, 0:512], PX[0:1, 0:512]]
        fold_banks = [bPA, bPA, bPB, bPB, bPW, bPX]
        gate_fold = [None] * 6
        wg_last = None
        for ck in range(NCT):
            wst, bwst = wstages[ck % 2]
            dl = p.dma("scalar", "wl", (lambda ck, wst: lambda e: e.dma_start(
                out=wst[:, :], in_=WG[128 * ck:128 * (ck + 1), :]))(ck, wst),
                deps=bwst.wdeps())
            bwst.wc(dl)
            tr = p.op("scalar", (lambda wst: lambda e: e.activation(
                out=wraw[:, :], in_=wst[:, :], func=AF.Identity))(wst),
                      deps=[dl] + bwraw.wdeps())
            bwraw.wc(tr)
            bwst.rc(tr)
            for h in range(6):
                deps = [tr, bdnbb.w]
                if ck == 0:
                    deps += fold_banks[h].wdeps()
                mm = p.op("tensor", (lambda h, ck: lambda e: e.matmul(
                    fold_regions[h], dnb_cb[:, ck:ck + 1], wraw[:, 512 * h:512 * (h + 1)],
                    start=(ck == 0), stop=(ck == NCT - 1)))(h, ck), deps=deps)
                bwraw.rc(mm)
                if ck == NCT - 1:
                    fold_banks[h].wc(mm)
                    gate_fold[h] = mm
            ts = p.op("scalar", (lambda ck, wst: lambda e: e.activation(
                out=wg_s[:, 3 * C * ck:3 * C * (ck + 1)], in_=wst[:, :],
                func=AF.Identity, scale=dng_c[:, ck:ck + 1]))(ck, wst),
                deps=[dl, bdng.w])
            bwst.rc(ts)
            bdng.rc(ts)
            wg_last = ts
        bwg.w = wg_last
        t_gb = load_row("GATESB", 3 * C)
        ev_last = None
        for h in range(6):
            ev = p.op("vector", (lambda h: lambda e: e.tensor_add(
                brow2t[0:1, 512 * h:512 * (h + 1)], rowbuf[0:1, 512 * h:512 * (h + 1)],
                fold_regions[h]))(h),
                deps=[t_gb, gate_fold[h]] + (bbrow2.wdeps() if h == 0 else []))
            fold_banks[h].rc(ev)
            brow.rc(ev)
            ev_last = ev
        bbrow2.wc(ev_last)
        t_bv = p.op("scalar", lambda e: e.activation(out=browv_b[0:1, :],
                                                     in_=brow2t[0:1, 2 * C:3 * C],
                                                     func=AF.Identity), deps=[ev_last])
        bbrow2.rc(t_bv)
        bbrowv.wc(t_bv)
        t_qbc = row_to_cols(brow2t, ev_last, C, qb_c, bqb)
        bbrow2.rc(t_qbc)
        mms = []
        for k in range(NCT):
            deps = [ev_last, t_onef]
            if k == 0:
                deps += bPS.wdeps()
            mms.append(p.op("tensor", (lambda k: lambda e: e.matmul(
                PS[:, k:k + 1], brow2t[0:1, C + 128 * k:C + 128 * (k + 1)],
                onef[0:1, 0:1], start=True, stop=True))(k), deps=deps))
        bPS.wc(mms[-1])
        t_kbc = p.op("vector", lambda e: e.tensor_copy(kb_c[:, 0:NCT], PS[:, 0:NCT]),
                     deps=[mms[-1]] + bkb.wdeps())
        bPS.rc(t_kbc)
        bkb.wc(t_kbc)
        bbrow2.rc(t_kbc)

        tl = p.dma("sync", "cl", lambda e: e.dma_start(out=rowbuf[0:1, 0:C],
                                                       in_=LRW[0:1, 0:C]),
                   deps=brow.wdeps())
        brow.wc(tl)
        t_lrwf = row_to_cols(rowbuf, tl, C, colf, bcolf)
        brow.rc(t_lrwf)
        t_lrwb = p.op("vector", lambda e: e.tensor_mul(lrw_b[:, :], colf[:, :],
                                                       dng_c[:, :]),
                      deps=[t_lrwf, bdng.w] + blrwb.wdeps())
        bcolf.rc(t_lrwb)
        bdng.rc(t_lrwb)
        blrwb.wc(t_lrwb)
        lrwr_b = SB(initstk, "lrwr_b", [128, NCT], BF16)
        blrwr = Buf()
        t_lrwr = p.op("scalar", lambda e: e.activation(out=lrwr_b[:, :], in_=colf[:, :],
                                                       func=AF.Identity), deps=[t_lrwf])
        bcolf.rc(t_lrwr)
        blrwr.wc(t_lrwr)
        lrf_mm = None
        for ck in range(NCT):
            deps = [bdnbb.w, t_lrwr]
            if ck == 0:
                deps += bPS.wdeps()
            lrf_mm = p.op("tensor", (lambda ck: lambda e: e.matmul(
                PS[0:1, 0:1], dnb_cb[:, ck:ck + 1], lrwr_b[:, ck:ck + 1],
                start=(ck == 0), stop=(ck == NCT - 1)))(ck), deps=deps)
        bPS.wc(lrf_mm)
        t_lrb = load_row("LRB", 1)
        lrb_row = SB(initstk, "lrb_row", [1, 2], F32)
        lrb_rowb = SB(initstk, "lrb_rowb", [1, 1], BF16)
        blrr = Buf()
        t1 = p.op("vector", lambda e: e.tensor_add(lrb_row[0:1, 0:1], rowbuf[0:1, 0:1],
                                                   PS[0:1, 0:1]),
                  deps=[t_lrb, lrf_mm] + blrr.wdeps())
        bPS.rc(t1)
        brow.rc(t1)
        blrr.wc(t1)
        t2 = p.op("scalar", lambda e: e.activation(out=lrb_rowb[0:1, 0:1],
                                                   in_=lrb_row[0:1, 0:1], func=AF.Identity),
                  deps=[t1])
        blrr.rc(t2)
        mmb = p.op("tensor", lambda e: e.matmul(PS[:, 0:1], onesb[0:1, :],
                                                lrb_rowb[0:1, 0:1], start=True, stop=True),
                   deps=[t2, ld["ONESB"]] + bPS.wdeps())
        bPS.wc(mmb)
        t_lrbb = p.op("scalar", lambda e: e.activation(out=lrb_bc[:, :], in_=PS[:, 0:1],
                                                       func=AF.Identity),
                      deps=[mmb] + blrb.wdeps())
        bPS.rc(t_lrbb)
        blrb.wc(t_lrbb)

        t_ob = load_row("OUTB", C)
        t_obb = p.op("scalar", lambda e: e.activation(out=outb_rb[0:1, :],
                                                      in_=rowbuf[0:1, 0:C], func=AF.Identity),
                     deps=[t_ob])
        brow.rc(t_obb)
        boutb.wc(t_obb)
        t_b2 = load_row("B2", C)
        t_b2b = p.op("scalar", lambda e: e.activation(out=b2_rb[0:1, :],
                                                      in_=rowbuf[0:1, 0:C], func=AF.Identity),
                     deps=[t_b2])
        brow.rc(t_b2b)
        bb2.wc(t_b2b)

        b1f = [None, None]
        for ck in range(NCT):
            wst, bwst = wstages[ck % 2]
            dl = p.dma("scalar", "wl", (lambda ck, wst: lambda e: e.dma_start(
                out=wst[:, 0:C], in_=W1[128 * ck:128 * (ck + 1), :]))(ck, wst),
                deps=bwst.wdeps())
            bwst.wc(dl)
            tr = p.op("scalar", (lambda wst: lambda e: e.activation(
                out=wraw[:, 0:C], in_=wst[:, 0:C], func=AF.Identity))(wst),
                      deps=[dl] + bwraw.wdeps())
            bwraw.wc(tr)
            bwst.rc(tr)
            for h in range(2):
                deps = [tr, bmnbb.w]
                if ck == 0:
                    deps += (bPA.wdeps() if h == 0 else bPB.wdeps())
                fm = p.op("tensor", (lambda h, ck: lambda e: e.matmul(
                    (PA if h == 0 else PB)[0:1, 0:512], mnb_cb[:, ck:ck + 1],
                    wraw[:, 512 * h:512 * (h + 1)], start=(ck == 0),
                    stop=(ck == NCT - 1)))(h, ck), deps=deps)
                bwraw.rc(fm)
                if ck == NCT - 1:
                    (bPA if h == 0 else bPB).wc(fm)
                    b1f[h] = fm
        t_b1 = load_row("B1", C)
        t_bf0 = p.op("vector", lambda e: e.tensor_add(
            brow2t[0:1, 0:512], rowbuf[0:1, 0:512], PA[0:1, 0:512]),
            deps=[t_b1, b1f[0]] + bbrow2.wdeps())
        bPA.rc(t_bf0)
        brow.rc(t_bf0)
        t_bf1 = p.op("vector", lambda e: e.tensor_add(
            brow2t[0:1, 512:1024], rowbuf[0:1, 512:1024], PB[0:1, 0:512]),
            deps=[t_b1, b1f[1], t_bf0])
        bPB.rc(t_bf1)
        brow.rc(t_bf1)
        bbrow2.wc(t_bf1)
        t_b1c = row_to_cols(brow2t, t_bf1, C, b1_c, bb1c)
        bbrow2.rc(t_b1c)

        def load_ffn(dst, bdst, srcd, half=False):
            last = None
            for ck in range(NCT):
                wst, bwst = wstages[ck % 2]
                dl = p.dma("scalar", "wl", (lambda ck, srcd, wst: lambda e: e.dma_start(
                    out=wst[:, 0:C],
                    in_=srcd[128 * ck:128 * (ck + 1), :]))(ck, srcd, wst),
                    deps=bwst.wdeps())
                bwst.wc(dl)
                if half:
                    tc = p.op("scalar", (lambda ck, dst, wst: lambda e: e.activation(
                        out=dst[:, C * ck:C * (ck + 1)], in_=wst[:, 0:C],
                        func=AF.Identity, scale=ccols[:, 2:3]))(ck, dst, wst),
                        deps=[dl, cc])
                else:
                    tc = p.op("scalar", (lambda ck, dst, wst: lambda e: e.activation(
                        out=dst[:, C * ck:C * (ck + 1)], in_=wst[:, 0:C],
                        func=AF.Identity))(ck, dst, wst),
                        deps=[dl])
                bwst.rc(tc)
                last = tc
            bdst.w = last
            return last

        t_wr = load_ffn(wr_s, bwr, WRT)
        t_wk = load_ffn(wk_s, bwk, WKT)
        t_wv = load_ffn(wv_s, bwv, WVT, half=True)

        bar_p0 = p.hard_barrier()
        initstk.close()
        a0stk = contextlib.ExitStack()
        dscr = SB(a0stk, "dscr", [128, 1024], BF16)
        bdscr = Buf()
        krelu = SB(a0stk, "krelu", [128, 1024], F32)
        bkrelu = Buf()
        r_sb = SB(a0stk, "r_sb", [128, 2 * C], F32)
        br_sb = [Buf(), Buf()]

        x0b = SB(a0stk, "x0b", [128, 2 * C], F32)
        bx0 = [Buf(), Buf()]
        x0add = SB(a0stk, "x0add", [128, 2 * C], F32)
        bx0add = [Buf(), Buf()]
        lnb = SB(a0stk, "lnb", [128, 4 * C], BF16)
        blnb = [Buf(), Buf(), Buf(), Buf()]
        lnT = SB(a0stk, "lnT", [128, NCT * 513], BF16)
        blnT = Buf()
        xrT = SB(a0stk, "xrT", [128, 2 * NCT * 512], BF16)
        xkT = SB(a0stk, "xkT", [128, 2 * NCT * 512], BF16)
        bxr = [Buf(), Buf()]
        bxk = [Buf(), Buf()]
        kactT = SB(a0stk, "kactT", [128, NCT * 512], BF16)
        bkact = Buf()
        x1b = SB(a0stk, "x1b", [128, 2 * C], F32)
        bx1 = [Buf(), Buf()]

        def transpose8(src_fn, dst_fn, deps_src, bdst, dst_view=None):
            trs = []
            wd = bPT.wdeps()
            for k in range(8):
                deps = list(deps_src) + [ld["IDENT_B"]]
                if k == 0:
                    deps += wd
                trs.append(p.op("tensor", (lambda k: lambda e: e.transpose(
                    PT[:, 128 * k:128 * (k + 1)], src_fn(k), ident_b[:, :]))(k),
                    deps=deps))
            bPT.wc(trs[-1])
            wd2 = bdst.wdeps()
            if dst_view is not None:
                evs_last = p.op("vector", (lambda dst_view: lambda e: e.tensor_copy(
                    dst_view, PT.rearrange("p (k q) -> p k q", q=128)))(dst_view),
                    deps=[trs[-1]] + wd2)
            else:
                evs_last = None
                for k in range(8):
                    deps = [trs[-1]]
                    if k == 0:
                        deps += wd2
                    evs_last = p.op("vector", (lambda k: lambda e: e.tensor_copy(
                        dst_fn(k), PT[:, 128 * k:128 * (k + 1)]))(k), deps=deps)
            bPT.rc(evs_last)
            bdst.wc(evs_last)
            return evs_last

        def ln_stats(src_ap, deps_x, slot, second_ln=False):
            s0 = 8 * slot
            bs = bstt[slot]
            t_sum = p.op("vector", (lambda src_ap, s0: lambda e: e.reduce_sum(
                stt[:, s0:s0 + 1], src_ap, axis=AX.X))(src_ap, s0),
                deps=deps_x + bs.wdeps())
            t_sq = p.op("scalar", (lambda src_ap, s0: lambda e: e.activation(
                out=sqd[:, 0:C], in_=src_ap, func=AF.Square,
                accum_out=stt[:, s0 + 1:s0 + 2]))(src_ap, s0),
                deps=deps_x + bsqd[0].wdeps() + [t_sum])
            bsqd[0].wc(t_sq)
            t_mean = p.op("vector", (lambda s0: lambda e: e.tensor_scalar(
                out=stt[:, s0 + 2:s0 + 3], in0=stt[:, s0:s0 + 1], scalar1=1.0 / C,
                scalar2=None, op0=ALU.mult))(s0), deps=[t_sum])
            t_mm2 = p.op("vector", (lambda s0: lambda e: e.tensor_mul(
                stt[:, s0 + 3:s0 + 4], stt[:, s0 + 2:s0 + 3], stt[:, s0 + 2:s0 + 3]))(s0),
                deps=[t_mean])
            t_var = p.op("vector", (lambda s0: lambda e: e.tensor_scalar(
                out=stt[:, s0 + 4:s0 + 5], in0=stt[:, s0 + 1:s0 + 2], scalar1=1.0 / C,
                scalar2=stt[:, s0 + 3:s0 + 4], op0=ALU.mult, op1=ALU.subtract))(s0),
                deps=[t_sq, t_mm2])
            if not second_ln:
                t_ln = p.op("scalar", (lambda s0: lambda e: e.activation(
                    out=stt[:, s0 + 5:s0 + 6], in_=stt[:, s0 + 4:s0 + 5], func=AF.Ln,
                    bias=ccols[:, 0:1]))(s0), deps=[t_var, cc])
            else:
                t_ln = p.op("scalar", (lambda s0: lambda e: e.activation(
                    out=stt[:, s0 + 5:s0 + 6], in_=stt[:, s0 + 4:s0 + 5], func=AF.Ln,
                    scale=ccols[:, 4:5], bias=ccols[:, 5:6]))(s0), deps=[t_var, cc])
            t_sc = p.op("scalar", (lambda s0: lambda e: e.activation(
                out=stt[:, s0 + 6:s0 + 7], in_=stt[:, s0 + 5:s0 + 6], func=AF.Exp,
                scale=ccols[:, 1:2]))(s0), deps=[t_ln])
            t_nb = p.op("vector", (lambda s0: lambda e: e.tensor_scalar(
                out=stt[:, s0 + 7:s0 + 8], in0=stt[:, s0 + 2:s0 + 3],
                scalar1=stt[:, s0 + 6:s0 + 7], scalar2=-1.0, op0=ALU.mult,
                op1=ALU.mult))(s0), deps=[t_sc, t_mean])
            bs.wc(t_nb)
            return [t_sc, t_nb], stt[:, s0 + 6:s0 + 7], stt[:, s0 + 7:s0 + 8]

        def stats_norm(src_ap, deps_x, j, lnb_t, blnb_t, sl, second_ln=False):
            stoks, sc_ap, nb_ap = ln_stats(src_ap, deps_x, j % 4, second_ln)
            t_n = p.op("scalar", (lambda src_ap, sc_ap, nb_ap, sl: lambda e:
                       e.activation(out=lnb_t[:, C * sl:C * (sl + 1)], in_=src_ap,
                                    func=AF.Identity, scale=sc_ap, bias=nb_ap))(
                           src_ap, sc_ap, nb_ap, sl),
                       deps=stoks + deps_x + blnb_t[sl].wdeps())
            blnb_t[sl].wc(t_n)
            bstt[j % 4].rc(t_n)
            return t_n

        def lnb_to_lnT(j, lnT_t, blnT_t, lnb_t, blnb_t, sl, t_n):
            transpose8(
                (lambda sl: lambda k: lnb_t[:, C * sl + 128 * k:C * sl + 128 * (k + 1)])(sl),
                None, [t_n], blnT_t,
                dst_view=lnT_t.rearrange("p (k t) -> p k t", t=513)[:, :, 1 + 128 * j:1 + 128 * (j + 1)])
            blnb_t[sl].rc(blnT_t.w)

        def emit_lnT(src_ap, deps_x, j, lnT_t, blnT_t, lnb_t, blnb_t, second_ln=False):
            sl = j % 2
            t_n = stats_norm(src_ap, deps_x, j, lnb_t, blnb_t, sl, second_ln)
            lnb_to_lnT(j, lnT_t, blnT_t, lnb_t, blnb_t, sl, t_n)
            return t_n

        def emit_ffn_mix(lnT_t, blnT_t, xrT_t, bxr_t, xkT_t, bxk_t,
                         dscr_t, bdscr_t):
            mix_last = None
            xr_last = None
            for k in range(NCT):
                cur = lnT_t[:, 513 * k + 1:513 * k + 513]
                shf = lnT_t[:, 513 * k:513 * k + 512]
                t_d = p.op("vector", (lambda cur, shf, k: lambda e: e.tensor_sub(
                    dscr_t[:, 512 * (k % 2):512 * (k % 2 + 1)], cur, shf))(cur, shf, k),
                    deps=blnT_t.r() + bdscr_t.wdeps())
                bdscr_t.wc(t_d)
                blnT_t.rc(t_d)
                t_xr = p.op("vector", (lambda k, shf: lambda e: e.scalar_tensor_tensor(
                    out=xrT_t[:, 512 * k:512 * (k + 1)],
                    in0=dscr_t[:, 512 * (k % 2):512 * (k % 2 + 1)],
                    scalar=mur_c[:, k:k + 1], in1=shf, op0=ALU.mult, op1=ALU.add))(k, shf),
                    deps=[t_d, bmur.w] + (bxr_t.wdeps() if k == 0 else []))
                t_xk = p.op("vector", (lambda k, shf: lambda e: e.scalar_tensor_tensor(
                    out=xkT_t[:, 512 * k:512 * (k + 1)],
                    in0=dscr_t[:, 512 * (k % 2):512 * (k % 2 + 1)],
                    scalar=muk_c[:, k:k + 1], in1=shf, op0=ALU.mult, op1=ALU.add))(k, shf),
                    deps=[t_d, bmuk.w] + (bxk_t.wdeps() if k == 0 else []))
                bdscr_t.rc(t_xr)
                bdscr_t.rc(t_xk)
                blnT_t.rc(t_xr)
                blnT_t.rc(t_xk)
                xr_last = t_xr
                mix_last = t_xk
            bxr_t.wc(xr_last)
            bxk_t.wc(mix_last)
            bc_last = None
            for k in range(NCT):
                bc_last = p.op("vector", (lambda k: lambda e: e.tensor_copy(
                    lnT_t[:, 513 * k:513 * k + 1],
                    lnT_t[:, 513 * k + 512:513 * k + 513]))(k),
                    deps=[mix_last, xr_last] + blnT_t.r())
            blnT_t.wc(bc_last)

        def emit_ffn_mms(xrT_t, bxr_t, xkT_t, bxk_t, kactT_t, bkact_t,
                         wr_t, wk_t, wv_t, bwr_t, bwk_t, bwv_t,
                         out_action, krelu_t, bkrelu_t, rsb_t, brsb_t,
                         pre_dk=None, post_dk=None, pre_jloop=None):
            ka_last = None
            for dk in range(NCT):
                if pre_dk is not None and dk % 2 == 0:
                    pre_dk(dk)
                bank, bankb = (PW, bPW) if dk % 2 == 0 else (PX, bPX)
                mms = []
                for ck in range(NCT):
                    deps = [bxk_t.w, bwk_t.w]
                    if ck == 0:
                        deps += bankb.wdeps()
                    mms.append(p.op("tensor", (lambda bank, dk, ck: lambda e: e.matmul(
                        bank[:, :], wk_t[:, C * ck + 128 * dk:C * ck + 128 * (dk + 1)],
                        xkT_t[:, 512 * ck:512 * (ck + 1)],
                        start=(ck == 0), stop=(ck == NCT - 1)))(bank, dk, ck), deps=deps))
                bankb.wc(mms[-1])
                bxk_t.rc(mms[-1])
                t_r = p.op("scalar", (lambda bank, dk: lambda e: e.activation(
                    out=krelu_t[:, 512 * (dk % 2):512 * (dk % 2 + 1)], in_=bank[:, :],
                    func=AF.Relu))(bank, dk),
                    deps=[mms[-1]] + bkrelu_t.wdeps())
                bankb.rc(t_r)
                bkrelu_t.wc(t_r)
                t_s = p.op("scalar", (lambda dk: lambda e: e.activation(
                    out=kactT_t[:, 512 * dk:512 * (dk + 1)],
                    in_=krelu_t[:, 512 * (dk % 2):512 * (dk % 2 + 1)], func=AF.Square))(dk),
                    deps=[t_r] + (bkact_t.wdeps() if dk == 0 else []))
                bkrelu_t.rc(t_s)
                ka_last = t_s
                if post_dk is not None and dk % 2 == 1:
                    post_dk(dk)
            bkact_t.wc(ka_last)
            if pre_jloop is not None:
                pre_jloop()
            for j in range(4):
                rmms = []
                for h in range(2):
                    for ck in range(NCT):
                        deps = [bxr_t.w, bwr_t.w]
                        if h == 0 and ck == 0:
                            deps += bPA.wdeps()
                        rmms.append(p.op("tensor", (lambda j, h, ck: lambda e: e.matmul(
                            PA[:, 512 * h:512 * (h + 1)],
                            xrT_t[:, 512 * ck + 128 * j:512 * ck + 128 * (j + 1)],
                            wr_t[:, C * ck + 512 * h:C * ck + 512 * (h + 1)],
                            start=(ck == 0), stop=(ck == NCT - 1)))(j, h, ck),
                            deps=deps))
                bPA.wc(rmms[-1])
                bxr_t.rc(rmms[-1])
                rsl = j % 2
                sig = p.op("scalar", (lambda rsl: lambda e: e.activation(
                    out=rsb_t[:, C * rsl:C * (rsl + 1)], in_=PA[:, :],
                    func=AF.Sigmoid))(rsl),
                    deps=[rmms[-1]] + brsb_t[rsl].wdeps())
                bPA.rc(sig)
                brsb_t[rsl].wc(sig)
                vmms = []
                for h in range(2):
                    for dk in range(NCT):
                        deps = [bkact_t.w, bwv_t.w]
                        if h == 0 and dk == 0:
                            deps += bPB.wdeps()
                        vmms.append(p.op("tensor", (lambda j, h, dk: lambda e: e.matmul(
                            PB[:, 512 * h:512 * (h + 1)],
                            kactT_t[:, 512 * dk + 128 * j:512 * dk + 128 * (j + 1)],
                            wv_t[:, C * dk + 512 * h:C * dk + 512 * (h + 1)],
                            start=(dk == 0), stop=(dk == NCT - 1)))(j, h, dk),
                            deps=deps))
                bPB.wc(vmms[-1])
                bkact_t.rc(vmms[-1])
                out_action(j, rsl, vmms[-1], sig,
                           rsb_t[:, C * rsl:C * (rsl + 1)], brsb_t[rsl])

        tz = None
        for k in range(NCT):
            tz = p.op("vector", (lambda k: lambda e: e.memset(
                lnT[:, 513 * k:513 * k + 1], 0.0))(k),
                deps=(blnT.wdeps() + bar_p0) if k == 0 else [])
        blnT.wc(tz)

        SLOT = NCT * 512

        def a0_mm_stage(m, pre_dk=None, post_dk=None, pre_jloop=None):
            ms = m % 2
            xr_v = xrT[:, SLOT * ms:SLOT * (ms + 1)]
            xk_v = xkT[:, SLOT * ms:SLOT * (ms + 1)]

            def a0_out(j, rsl, vtok, sig_tok, rsb_ap, brsb_slot, m=m):
                xsl = j % 2
                t_xl = p.dma("sync", "x0a", (lambda m, j, xsl: lambda e: e.dma_start(
                    out=x0add[:, C * xsl:C * (xsl + 1)],
                    in_=X[MAC * m + MIC * j:MAC * m + MIC * (j + 1), :]))(m, j, xsl),
                    deps=bx0add[xsl].wdeps())
                bx0add[xsl].wc(t_xl)
                t_t3 = p.op("vector", (lambda rsb_ap, xsl: lambda e: e.scalar_tensor_tensor(
                    out=x1b[:, C * xsl:C * (xsl + 1)], in0=PB[:, :], scalar=1.0,
                    in1=rsb_ap, op0=ALU.mult, op1=ALU.mult))(rsb_ap, xsl),
                    deps=[vtok, sig_tok] + bx1[xsl].wdeps())
                bPB.rc(t_t3)
                brsb_slot.rc(t_t3)
                bx1[xsl].wc(t_t3)
                t_x1 = p.op("vector", (lambda xsl: lambda e: e.tensor_add(
                    x1b[:, C * xsl:C * (xsl + 1)], x1b[:, C * xsl:C * (xsl + 1)],
                    x0add[:, C * xsl:C * (xsl + 1)]))(xsl),
                    deps=[t_t3, t_xl] + bx1[xsl].wdeps())
                bx0add[xsl].rc(t_x1)
                bx1[xsl].wc(t_x1)
                t_sp = p.dma("sync", "x1o", (lambda m, j, xsl: lambda e: e.dma_start(
                    out=X1S[MAC * m + MIC * j:MAC * m + MIC * (j + 1), :],
                    in_=x1b[:, C * xsl:C * (xsl + 1)]))(m, j, xsl),
                    deps=[t_x1])
                bx1[xsl].rc(t_sp)

            emit_ffn_mms(xr_v, bxr[ms], xk_v, bxk[ms], kactT, bkact,
                         wr_s, wk_s, wv_s, bwr, bwk, bwv, a0_out,
                         krelu, bkrelu, r_sb, br_sb,
                         pre_dk=pre_dk, post_dk=post_dk, pre_jloop=pre_jloop)

        def a0_stats_micro(m, j):
            xs = j % 2
            t_x0 = p.dma("sync", "x0i", (lambda m, j, xs: lambda e: e.dma_start(
                out=x0b[:, C * xs:C * (xs + 1)],
                in_=X[MAC * m + MIC * j:MAC * m + MIC * (j + 1), :]))(m, j, xs),
                deps=bx0[xs].wdeps() + (bar_p0 if m == 0 and j < 2 else []))
            bx0[xs].wc(t_x0)
            t_n = stats_norm(x0b[:, C * xs:C * (xs + 1)], [bx0[xs].w],
                             j, lnb, blnb, j)
            bx0[xs].rc(t_n)
            return t_n

        for m in range(NMAC):
            ms = m % 2
            if m == 0:
                for j in range(4):
                    t_n = a0_stats_micro(0, j)
                    lnb_to_lnT(j, lnT, blnT, lnb, blnb, j, t_n)
                emit_ffn_mix(lnT, blnT, xrT[:, 0:SLOT], bxr[0],
                             xkT[:, 0:SLOT], bxk[0], dscr, bdscr)
                continue
            ntoks = {}

            def pre_dk(dk, m=m, ntoks=ntoks):
                ntoks[dk // 2] = a0_stats_micro(m, dk // 2)

            def post_dk(dk, m=m, ntoks=ntoks):
                jj = (dk - 1) // 2
                lnb_to_lnT(jj, lnT, blnT, lnb, blnb, jj, ntoks[jj])

            def pre_j(m=m, ms=ms):
                emit_ffn_mix(lnT, blnT, xrT[:, SLOT * ms:SLOT * (ms + 1)], bxr[ms],
                             xkT[:, SLOT * ms:SLOT * (ms + 1)], bxk[ms], dscr, bdscr)

            a0_mm_stage(m - 1, pre_dk=pre_dk, post_dk=post_dk, pre_jloop=pre_j)
        a0_mm_stage(NMAC - 1)

        bar_a0 = p.hard_barrier()
        a0stk.close()
        fwstk.close()
        if stop_after == "a0":
            t_cp = p.dma("sync", "oo", lambda e: e.dma_start(out=OUT[:, :], in_=X1S[:, :]),
                         deps=bar_a0)
            p.op("sync", lambda e: e.nop(), deps=[t_cp])
            gstk.close()
            p.build_and_emit(root)
            return nc

        gfstk = contextlib.ExitStack()
        G_f = gfstk.enter_context(nc.sbuf_tensor("G_f", [128, NCT * C], F32, side="right"))
        bG = Buf()
        a1stk = contextlib.ExitStack()
        S_f = SB(a1stk, "S_f", [128, NCT * C], F32)
        S_b = SB(a1stk, "S_b", [128, NCT * C], BF16)
        bS = Buf()
        bSb_t = [Buf() for _ in range(NCT)]
        x1i = SB(a1stk, "x1i", [128, 2 * C], F32)
        bx1i = [Buf(), Buf()]
        a0nb = SB(a1stk, "a0nb", [128, 2 * C], BF16)
        ba0nb = [Buf(), Buf()]
        a0nT = SB(a1stk, "a0nT", [128, NCT * 512], BF16)
        ba0nT = Buf()
        kT = SB(a1stk, "kT", [128, NCT * 512], BF16)
        bkT = Buf()
        khat = SB(a1stk, "khat", [128, 4 * C], BF16)
        bkhat = Buf()
        qsp = khat
        bqsp = bkhat
        ut = SB(a1stk, "ut", [128, 4 * C], BF16)
        but = [Buf(), Buf(), Buf(), Buf()]
        wt = SB(a1stk, "wt", [128, 4 * C], BF16)
        bwt = Buf()
        b_sb = SB(a1stk, "b_sb", [128, C], BF16)
        bbsb = Buf()
        lr_c = SB(a1stk, "lr_c", [128, 4], F32)
        blr = Buf()
        nb_A = SB(a1stk, "nb_A", [128, 4 * 128], BF16)
        nb_AT = SB(a1stk, "nb_AT", [128, 4 * 128], BF16)
        nb_M = SB(a1stk, "nb_M", [128, 4 * 128], BF16)
        nb_MT = SB(a1stk, "nb_MT", [128, 4 * 128], BF16)
        nb_Q = nb_A
        bnA, bnAT, bnM, bnMT = (Buf() for _ in range(4))
        bnQ = bnA
        g_sc = SB(a1stk, "g_sc", [128, 4 * 128], F32)
        bg_sc = Buf()
        B_sb = SB(a1stk, "B_sb", [128, 4 * 128], F32)
        bB_sb = Buf()
        rr_b = SB(a1stk, "rr_b", [1, 128], BF16)
        brr = Buf()
        af_sc = SB(a1stk, "af_sc", [128, 128], F32)
        baf = Buf()

        tz1 = p.op("vector", lambda e: e.memset(S_f[:, :], 0.0), deps=bar_a0)
        bS.wc(tz1)
        tz2 = p.op("scalar", lambda e: e.activation(out=S_b[:, :], in_=S_f[:, :],
                                                    func=AF.Identity), deps=[tz1] + bar_a0)
        for bb in bSb_t:
            bb.wc(tz2)
        tz3 = p.op("vector", lambda e: e.memset(G_f[:, :], 0.0), deps=bar_a0)
        bG.wc(tz3)

        for m in range(NMAC):
            for j in range(4):
                sl = j % 2
                t_x1 = p.dma("sync", "x1i", (lambda m, j, sl: lambda e: e.dma_start(
                    out=x1i[:, C * sl:C * (sl + 1)],
                    in_=X1S[MAC * m + MIC * j:MAC * m + MIC * (j + 1), :]))(m, j, sl),
                    deps=bx1i[sl].wdeps() + (bar_a0 if m == 0 and j < 2 else []))
                bx1i[sl].wc(t_x1)
                stoks, sc_ap, nb_ap = ln_stats(x1i[:, C * sl:C * (sl + 1)],
                                               [bx1i[sl].w], j % 4, second_ln=True)
                t_n = p.op("scalar", (lambda sl, sc_ap, nb_ap: lambda e: e.activation(
                    out=a0nb[:, C * sl:C * (sl + 1)], in_=x1i[:, C * sl:C * (sl + 1)],
                    func=AF.Identity, scale=sc_ap, bias=nb_ap))(sl, sc_ap, nb_ap),
                    deps=stoks + [bx1i[sl].w] + ba0nb[sl].wdeps())
                ba0nb[sl].wc(t_n)
                bstt[j % 4].rc(t_n)
                bx1i[sl].rc(t_n)
                transpose8(
                    (lambda sl: lambda k: a0nb[:, C * sl + 128 * k:C * sl + 128 * (k + 1)])(sl),
                    (lambda j: lambda k: a0nT[:, 512 * k + 128 * j:512 * k + 128 * (j + 1)])(j),
                    [t_n], ba0nT)
                ba0nb[sl].rc(ba0nT.w)
            for which in ("q", "k"):
                off = 0 if which == "q" else C
                dstb, bdst = (qsp, bqsp) if which == "q" else (kT, bkT)
                bias_c = qb_c if which == "q" else kb_c
                bias_tok = bqb.w if which == "q" else bkb.w
                e_last = None
                for dk in range(NCT):
                    bank, bankb = (PW, bPW) if dk % 2 == 0 else (PX, bPX)
                    mms = []
                    for ck in range(NCT):
                        deps = [ba0nT.w, bwg.w]
                        if ck == 0:
                            deps += bankb.wdeps()
                        mms.append(p.op("tensor", (lambda bank, dk, ck, off: lambda e: e.matmul(
                            bank[:, :],
                            wg_s[:, 3 * C * ck + off + 128 * dk:3 * C * ck + off + 128 * (dk + 1)],
                            a0nT[:, 512 * ck:512 * (ck + 1)],
                            start=(ck == 0), stop=(ck == NCT - 1)))(bank, dk, ck, off),
                            deps=deps))
                    bankb.wc(mms[-1])
                    ba0nT.rc(mms[-1])
                    if which == "q":
                        dst_ap = qsp.bitcast(BF16)[:, 512 * dk:512 * (dk + 1)]
                    else:
                        dst_ap = kT[:, 512 * dk:512 * (dk + 1)]
                    deps_e = [mms[-1], bias_tok]
                    if dk == 0:
                        deps_e += bdst.wdeps()
                    e_last = p.op("scalar", (lambda bank, dst_ap, bias_c, dk: lambda e:
                                  e.activation(out=dst_ap, in_=bank[:, :], func=AF.Identity,
                                               bias=bias_c[:, dk:dk + 1]))(bank, dst_ap, bias_c, dk),
                                  deps=deps_e)
                    bankb.rc(e_last)
                bdst.wc(e_last)
                if which == "q":
                    t_qo = p.dma("scalar", "qo", (lambda m: lambda e: e.dma_start(
                        out=QTS.rearrange("ik p t -> p ik t")[:, :, MAC * m:MAC * (m + 1)],
                        in_=qsp.rearrange("p (ik t) -> p ik t", ik=NCT)))(m),
                        deps=bqsp.r())
                    bqsp.rc(t_qo)
            lr_last = None
            for j in range(4):
                mms = []
                for ck in range(NCT):
                    deps = [ba0nT.w, blrwb.w]
                    if ck == 0:
                        deps += bPS.wdeps()
                    mms.append(p.op("tensor", (lambda j, ck: lambda e: e.matmul(
                        PS[:, 0:1], a0nT[:, 512 * ck + 128 * j:512 * ck + 128 * (j + 1)],
                        lrw_b[:, ck:ck + 1],
                        start=(ck == 0), stop=(ck == NCT - 1)))(j, ck), deps=deps))
                bPS.wc(mms[-1])
                ba0nT.rc(mms[-1])
                lr_last = p.op("scalar", (lambda j: lambda e: e.activation(
                    out=lr_c[:, j:j + 1], in_=PS[:, 0:1], func=AF.Identity,
                    bias=lrb_bc[:, :]))(j),
                    deps=[mms[-1], blrb.w] + (blr.wdeps() if j == 0 else []))
                bPS.rc(lr_last)
            blr.wc(lr_last)
            for j in range(4 if a1_parts & 1 else 0):
                mms = []
                for ik in range(NCT):
                    deps = [bkT.w]
                    if ik == 0:
                        deps += bPS.wdeps()
                    mms.append(p.op("tensor", (lambda j, ik: lambda e: e.matmul(
                        PS[:, 0:128], kT[:, 512 * ik + 128 * j:512 * ik + 128 * (j + 1)],
                        kT[:, 512 * ik + 128 * j:512 * ik + 128 * (j + 1)],
                        start=(ik == 0), stop=(ik == NCT - 1)))(j, ik), deps=deps))
                bPS.wc(mms[-1])
                bkT.rc(mms[-1])
                s0 = 8 * (j % 4)
                t_dm = p.op("vector", lambda e: e.tensor_mul(
                    af_sc[:, :], PS[:, 0:128], ident_f[:, :]),
                    deps=[mms[-1], ld["IDENT_F"]] + baf.wdeps() + bstt[j % 4].wdeps())
                bPS.rc(t_dm)
                baf.wc(t_dm)
                t_d2 = p.op("vector", (lambda s0: lambda e: e.reduce_sum(
                    stt[:, s0:s0 + 1], af_sc[:, :], axis=AX.X))(s0), deps=[t_dm])
                baf.rc(t_d2)
                t_ln2 = p.op("scalar", (lambda s0: lambda e: e.activation(
                    out=stt[:, s0 + 1:s0 + 2], in_=stt[:, s0:s0 + 1], func=AF.Ln))(s0),
                    deps=[t_d2])
                t_rs = p.op("scalar", (lambda s0: lambda e: e.activation(
                    out=stt[:, s0 + 2:s0 + 3], in_=stt[:, s0 + 1:s0 + 2], func=AF.Exp,
                    scale=ccols[:, 1:2]))(s0), deps=[t_ln2, cc])
                bstt[j % 4].wc(t_rs)
                t_t1 = p.op("vector", (lambda j, s0: lambda e: e.tensor_scalar(
                    out=g_sc[:, 128 * j:128 * (j + 1)], in0=PS[:, 0:128],
                    scalar1=stt[:, s0 + 2:s0 + 3], scalar2=None, op0=ALU.mult))(j, s0),
                    deps=[t_rs, mms[-1]] + (bg_sc.wdeps() if j == 0 else []))
                bPS.rc(t_t1)
                bstt[j % 4].rc(t_t1)
                t_rr = p.op("tensor", (lambda s0: lambda e: e.transpose(
                    PS[0:1, 128:256], stt[:, s0 + 2:s0 + 3], ident_f[:, :]))(s0),
                    deps=[t_rs, t_t1, ld["IDENT_F"]] + bPS.wdeps())
                bPS.wc(t_rr)
                bstt[j % 4].rc(t_rr)
                t_rrb = p.op("scalar", lambda e: e.activation(
                    out=rr_b[0:1, :], in_=PS[0:1, 128:256], func=AF.Identity),
                    deps=[t_rr] + brr.wdeps())
                bPS.rc(t_rrb)
                brr.wc(t_rrb)
                t_bc = p.op("tensor", lambda e: e.matmul(
                    PS[:, 256:384], onesb[0:1, :], rr_b[0:1, :], start=True, stop=True),
                    deps=[t_rrb, ld["ONESB"]] + bPS.wdeps())
                bPS.wc(t_bc)
                brr.rc(t_bc)
                t_Bs = p.op("vector", (lambda j: lambda e: e.tensor_copy(
                    B_sb[:, 128 * j:128 * (j + 1)], PS[:, 256:384]))(j),
                    deps=[t_bc] + (bB_sb.wdeps() if j == 0 else []))
                bPS.rc(t_Bs)
                t_l2 = p.op("vector", (lambda j: lambda e: e.tensor_mul(
                    g_sc[:, 128 * j:128 * (j + 1)], g_sc[:, 128 * j:128 * (j + 1)],
                    B_sb[:, 128 * j:128 * (j + 1)]))(j), deps=[t_t1, t_Bs])
                t_l3 = p.op("vector", (lambda j: lambda e: e.tensor_mul(
                    g_sc[:, 128 * j:128 * (j + 1)], g_sc[:, 128 * j:128 * (j + 1)],
                    lmask[:, :]))(j), deps=[t_l2, ld["LMASK"]])
                t_A = p.op("vector", (lambda j: lambda e: e.tensor_tensor(
                    nb_A[:, 128 * j:128 * (j + 1)], g_sc[:, 128 * j:128 * (j + 1)],
                    ident_f[:, :], op=ALU.add))(j),
                    deps=[t_l3, ld["IDENT_F"]] + (bnA.wdeps() if j == 0 else []))
                t_M0 = p.op("vector", (lambda j: lambda e: e.tensor_tensor(
                    nb_M[:, 128 * j:128 * (j + 1)], g_sc[:, 128 * j:128 * (j + 1)],
                    ident_f[:, :], op=ALU.subtract))(j),
                    deps=[t_l3, ld["IDENT_F"]] + (bnM.wdeps() if j == 0 else []))
                if j == 3:
                    bg_sc.wc(t_l3)
                    bB_sb.wc(t_Bs)
                tt1 = p.op("tensor", (lambda j: lambda e: e.transpose(
                    PT[:, 0:128], nb_A[:, 128 * j:128 * (j + 1)], ident_b[:, :]))(j),
                    deps=[t_A, ld["IDENT_B"]] + bPT.wdeps())
                tt2 = p.op("tensor", (lambda j: lambda e: e.transpose(
                    PT[:, 128:256], nb_M[:, 128 * j:128 * (j + 1)], ident_b[:, :]))(j),
                    deps=[t_M0, ld["IDENT_B"]])
                bPT.wc(tt2)
                te1 = p.op("vector", (lambda j: lambda e: e.tensor_copy(
                    nb_AT[:, 128 * j:128 * (j + 1)], PT[:, 0:128]))(j),
                    deps=[tt2] + (bnAT.wdeps() if j == 0 else []))
                te2 = p.op("vector", (lambda j: lambda e: e.tensor_copy(
                    nb_MT[:, 128 * j:128 * (j + 1)], PT[:, 128:256]))(j),
                    deps=[tt2] + (bnMT.wdeps() if j == 0 else []))
                bPT.rc(te1)
                bPT.rc(te2)
                if j == 3:
                    bnA.wc(t_A)
                    bnM.wc(t_M0)
                    bnAT.wc(te1)
                    bnMT.wc(te2)
            for it in range(6 if a1_parts & 1 else 0):
                pms = []
                for j in range(4):
                    deps = [bnAT.w, bnM.w]
                    if j == 0:
                        deps += bPW.wdeps()
                    pms.append(p.op("tensor", (lambda j: lambda e: e.matmul(
                        PW[:, 128 * j:128 * (j + 1)], nb_AT[:, 128 * j:128 * (j + 1)],
                        nb_M[:, 128 * j:128 * (j + 1)], start=True, stop=True))(j),
                        deps=deps))
                bPW.wc(pms[-1])
                bnAT.rc(pms[-1])
                bnM.rc(pms[-1])
                t_Q = p.op("vector", lambda e: e.tensor_tensor(
                    nb_Q[:, 0:512], PW[:, 0:512], twoi4[:, :], op=ALU.add),
                    deps=[pms[-1], btwoi4.w] + bnQ.wdeps())
                bPW.rc(t_Q)
                bnQ.wc(t_Q)
                mms2 = []
                for j in range(4):
                    deps = [bnMT.w, bnQ.w]
                    if j == 0:
                        deps += bPX.wdeps()
                    mms2.append(p.op("tensor", (lambda j: lambda e: e.matmul(
                        PX[:, 128 * j:128 * (j + 1)], nb_MT[:, 128 * j:128 * (j + 1)],
                        nb_Q[:, 128 * j:128 * (j + 1)], start=True, stop=True))(j),
                        deps=deps))
                bPX.wc(mms2[-1])
                bnMT.rc(mms2[-1])
                bnQ.rc(mms2[-1])
                mms3 = []
                for j in range(4):
                    deps = [bnQ.w, bnMT.w]
                    if j == 0:
                        deps += bPW.wdeps()
                    mms3.append(p.op("tensor", (lambda j: lambda e: e.matmul(
                        PW[:, 128 * j:128 * (j + 1)], nb_Q[:, 128 * j:128 * (j + 1)],
                        nb_MT[:, 128 * j:128 * (j + 1)], start=True, stop=True))(j),
                        deps=deps))
                bPW.wc(mms3[-1])
                bnQ.rc(mms3[-1])
                bnMT.rc(mms3[-1])
                t_M = p.op("vector", lambda e: e.tensor_copy(nb_M[:, 0:512], PX[:, 0:512]),
                           deps=[mms2[-1]] + bnM.wdeps())
                bPX.rc(t_M)
                bnM.wc(t_M)
                t_MT = p.op("vector", lambda e: e.tensor_copy(nb_MT[:, 0:512], PW[:, 0:512]),
                            deps=[mms3[-1]] + bnMT.wdeps())
                bPW.rc(t_MT)
                bnMT.wc(t_MT)
            for j in range(4 if a1_parts & 1 else 0):
                last_sc = None
                for ik in range(NCT):
                    deps = [bkT.w, bB_sb.w] + (bkT.rs[-1:] if bkT.rs else [])
                    last_sc = p.op("vector", (lambda j, ik: lambda e: e.tensor_mul(
                        kT[:, 512 * ik + 128 * j:512 * ik + 128 * (j + 1)],
                        kT[:, 512 * ik + 128 * j:512 * ik + 128 * (j + 1)],
                        B_sb[:, 128 * j:128 * (j + 1)]))(j, ik), deps=deps)
                bkT.wc(last_sc)
                bB_sb.rc(last_sc)
                transpose8(
                    (lambda j: lambda ik: kT[:, 512 * ik + 128 * j:512 * ik + 128 * (j + 1)])(j),
                    None, [last_sc], bkhat,
                    dst_view=khat.rearrange("p (j2 ik q) -> p j2 ik q", j2=4, q=128)[:, j, :, :])
                bkT.rc(bkhat.w)
            inc_ks = bool(a1_parts & 8)
            inc_x = bool(a1_parts & 16)
            inc_u = bool(a1_parts & 32)
            inc_sf = bool(a1_parts & 64)

            def rec_vks(j, close_group):
                bank, bankb = (PA, bPA) if j % 2 == 0 else (PB, bPB)
                mms = []
                for h in range(2):
                    for ck in range(NCT):
                        deps = [ba0nT.w, bwg.w]
                        if h == 0 and ck == 0:
                            deps += bankb.wdeps()
                        mms.append(p.op("tensor", (lambda bank, j, h, ck: lambda e: e.matmul(
                            bank[:, 512 * h:512 * (h + 1)],
                            a0nT[:, 512 * ck + 128 * j:512 * ck + 128 * (j + 1)],
                            wg_s[:, 3 * C * ck + 2 * C + 512 * h:3 * C * ck + 2 * C + 512 * (h + 1)],
                            start=(ck == 0), stop=False))(bank, j, h, ck), deps=deps))
                ba0nT.rc(mms[-1])
                for h in range(2):
                    mms.append(p.op("tensor", (lambda bank, h, st2: lambda e: e.matmul(
                        bank[:, 512 * h:512 * (h + 1)], onesb[0:1, :],
                        browv_b[0:1, 512 * h:512 * (h + 1)], start=False,
                        stop=st2))(bank, h, (not inc_ks) and close_group),
                        deps=[bbrowv.w, ld["ONESB"]]))
                if inc_ks:
                    for h in range(2):
                        for ik in range(NCT):
                            is_stop = close_group and h == 1 and ik == NCT - 1 and False
                            is_stop = close_group and ik == NCT - 1
                            deps = [bkT.w, bSb_t[ik].w]
                            mms.append(p.op("tensor", (lambda bank, j, h, ik, is_stop: lambda e: e.matmul(
                                bank[:, 512 * h:512 * (h + 1)],
                                kT[:, 512 * ik + 128 * j:512 * ik + 128 * (j + 1)],
                                S_b[:, C * ik + 512 * h:C * ik + 512 * (h + 1)],
                                start=False, stop=is_stop))(bank, j, h, ik, is_stop),
                                deps=deps))
                            bSb_t[ik].rc(mms[-1])
                    bkT.rc(mms[-1])
                bankb.wc(mms[-1])

            def rec_corr(j):
                bank, bankb = (PA, bPA) if j % 2 == 0 else (PB, bPB)
                t_xu = None
                for jj in range(j):
                    xms = []
                    for ik in range(NCT):
                        deps = [bkT.w]
                        if ik == 0:
                            deps += bPS.wdeps()
                        xms.append(p.op("tensor", (lambda j, jj, ik: lambda e: e.matmul(
                            PS[:, 0:128],
                            kT[:, 512 * ik + 128 * jj:512 * ik + 128 * (jj + 1)],
                            kT[:, 512 * ik + 128 * j:512 * ik + 128 * (j + 1)],
                            start=(ik == 0), stop=(ik == NCT - 1)))(j, jj, ik), deps=deps))
                    bPS.wc(xms[-1])
                    bkT.rc(xms[-1])
                    t_xt = p.op("scalar", lambda e: e.activation(
                        out=af_sc.bitcast(BF16)[:, 0:128], in_=PS[:, 0:128],
                        func=AF.Identity), deps=[xms[-1]] + baf.wdeps())
                    bPS.rc(t_xt)
                    baf.wc(t_xt)
                    for h in range(2):
                        t_xu = p.op("tensor", (lambda h, jj, j: lambda e: e.matmul(
                            bank[:, 512 * h:512 * (h + 1)], af_sc.bitcast(BF16)[:, 0:128],
                            ut[:, C * jj + 512 * h:C * jj + 512 * (h + 1)],
                            start=False, stop=(jj == j - 1)))(h, jj, j),
                            deps=[t_xt, but[jj].w])
                    baf.rc(t_xu)
                    but[jj].rc(t_xu)
                if t_xu is not None:
                    bankb.wc(t_xu)

            def rec_bu(j):
                bank, bankb = (PA, bPA) if j % 2 == 0 else (PB, bPB)
                t_b2e = p.op("scalar", (lambda bank: lambda e: e.activation(
                    out=b_sb[:, 0:1024], in_=bank[:, 0:1024], func=AF.Identity))(bank),
                    deps=[bankb.w] + bbsb.wdeps())
                bankb.rc(t_b2e)
                bbsb.wc(t_b2e)
                if not inc_u:
                    return
                ums = []
                for h in range(2):
                    ums.append(p.op("tensor", (lambda bank, j, h: lambda e: e.matmul(
                        bank[:, 512 * h:512 * (h + 1)], nb_MT[:, 128 * j:128 * (j + 1)],
                        b_sb[:, 512 * h:512 * (h + 1)], start=True, stop=True))(bank, j, h),
                        deps=[bnMT.w, t_b2e] + (bankb.wdeps() if h == 0 else [])))
                bankb.wc(ums[-1])
                bnMT.rc(ums[-1])
                bbsb.rc(ums[-1])
                t_u2 = p.op("scalar", (lambda bank, j: lambda e: e.activation(
                    out=ut[:, C * j:C * (j + 1)], in_=bank[:, 0:1024],
                    func=AF.Identity))(bank, j),
                    deps=[ums[-1]] + but[j].wdeps())
                but[j].wc(t_u2)
                t_w2 = p.op("vector", (lambda j: lambda e: e.tensor_scalar(
                    out=wt[:, C * j:C * (j + 1)], in0=ut[:, C * j:C * (j + 1)],
                    scalar1=lr_c[:, j:j + 1], scalar2=None, op0=ALU.mult))(j),
                    deps=[t_u2, blr.w] + (bwt.wdeps() if j == 0 else []))
                but[j].rc(t_w2)
                blr.rc(t_w2)
                if j == 3:
                    bwt.wc(t_w2)
                bankb.rc(t_w2)

            if a1_parts & 2:
                for j in range(4):
                    close = (j == 0) or not inc_x
                    rec_vks(j, close_group=close)
                    if j > 0 and inc_x:
                        rec_corr(j)
                    rec_bu(j)
                if inc_sf:
                    for ik in range(NCT):
                        bank2, bank2b = (PA, bPA) if ik % 2 == 0 else (PB, bPB)
                        fms = []
                        for jj in range(4):
                            for h in range(2):
                                deps = [bkhat.w, but[jj].w]
                                if jj == 0 and h == 0:
                                    deps += bank2b.wdeps()
                                fms.append(p.op("tensor",
                                    (lambda bank2, ik, jj, h: lambda e: e.matmul(
                                        bank2[:, 512 * h:512 * (h + 1)],
                                        khat[:, C * jj + 128 * ik:C * jj + 128 * (ik + 1)],
                                        ut[:, C * jj + 512 * h:C * jj + 512 * (h + 1)],
                                        start=(jj == 0), stop=(jj == 3)))(bank2, ik, jj, h),
                                    deps=deps))
                        bank2b.wc(fms[-1])
                        bkhat.rc(fms[-1])
                        for jj in range(4):
                            but[jj].rc(fms[-1])
                        t_sa = p.op("vector", (lambda bank2, ik: lambda e: e.tensor_add(
                            S_f[:, C * ik:C * (ik + 1)], S_f[:, C * ik:C * (ik + 1)],
                            bank2[:, :]))(bank2, ik),
                            deps=[fms[-1]] + bS.wdeps())
                        bank2b.rc(t_sa)
                        bS.wc(t_sa)
                        t_sc2 = p.op("scalar", (lambda ik: lambda e: e.activation(
                            out=S_b[:, C * ik:C * (ik + 1)], in_=S_f[:, C * ik:C * (ik + 1)],
                            func=AF.Identity))(ik),
                            deps=[t_sa] + bSb_t[ik].wdeps())
                        bS.rc(t_sc2)
                        bSb_t[ik].wc(t_sc2)
            for ik in range(NCT if a1_parts & 4 else 0):
                bank2, bank2b = (PB, bPB) if ik % 2 == 0 else (PA, bPA)
                gms = []
                for jj in range(4):
                    for h in range(2):
                        deps = [bkhat.w, bwt.w]
                        if jj == 0 and h == 0:
                            deps += bank2b.wdeps()
                        gms.append(p.op("tensor", (lambda bank2, ik, jj, h: lambda e: e.matmul(
                            bank2[:, 512 * h:512 * (h + 1)],
                            khat[:, C * jj + 128 * ik:C * jj + 128 * (ik + 1)],
                            wt[:, C * jj + 512 * h:C * jj + 512 * (h + 1)],
                            start=(jj == 0), stop=(jj == 3)))(bank2, ik, jj, h),
                            deps=deps))
                bank2b.wc(gms[-1])
                bkhat.rc(gms[-1])
                bwt.rc(gms[-1])
                t_ga = p.op("vector", (lambda bank2, ik: lambda e: e.tensor_add(
                    G_f[:, C * ik:C * (ik + 1)], G_f[:, C * ik:C * (ik + 1)],
                    bank2[:, :]))(bank2, ik),
                    deps=[gms[-1]] + bG.wdeps())
                bank2b.rc(t_ga)
                bG.wc(t_ga)
        if dbg:
            t_sd = p.dma("sync", "dbg", lambda e: e.dma_start(out=SD[:, :], in_=S_f[:, :]),
                         deps=bS.r())
            bS.rc(t_sd)
            t_gd = p.dma("sync", "dbg", lambda e: e.dma_start(out=GD[:, :], in_=G_f[:, :]),
                         deps=bG.r())
            bG.rc(t_gd)
        if stop_after == "a1":
            toks = []
            for ik in range(NCT):
                toks.append(p.dma("sync", "oo", (lambda ik: lambda e: e.dma_start(
                    out=OUT[128 * ik:128 * (ik + 1), :],
                    in_=S_f[:, C * ik:C * (ik + 1)]))(ik), deps=bS.r() + bG.r()))
            for t in toks:
                bS.rc(t)
            p.op("sync", lambda e: e.nop(), deps=[toks[-1]] +
                 ([("dbg", p.dma_counts["dbg"])] if dbg else []))
            a1stk.close()
            gstk.close()
            gfstk.close()
            p.build_and_emit(root)
            return nc
        bar_a1 = p.hard_barrier()
        a1stk.close()
        gstk.close()

        bstk = contextlib.ExitStack()
        G_b = SB(bstk, "G_b", [128, NCT * C], BF16)
        bGb = Buf(readonly=True)
        t_gc = None
        for ik in range(2 * NCT):
            t_gc = p.op("scalar", (lambda ik: lambda e: e.activation(
                out=G_b[:, 512 * ik:512 * (ik + 1)], in_=G_f[:, 512 * ik:512 * (ik + 1)],
                func=AF.Identity))(ik), deps=[bG.w] + (bar_a1 if ik == 0 else []))
            bG.rc(t_gc)
        bGb.w = t_gc
        gbar = p.hard_barrier()
        gfstk.close()

        wr_s2 = SB(bstk, "wr_s2", [128, NCT * C], BF16)
        wk_s2 = SB(bstk, "wk_s2", [128, NCT * C], BF16)
        wv_s2 = SB(bstk, "wv_s2", [128, NCT * C], BF16)
        outw_s = SB(bstk, "outw_s", [128, NCT * C], BF16)
        w1_s = SB(bstk, "w1_s", [128, NCT * C], BF16)
        w2_s = SB(bstk, "w2_s", [128, NCT * C], BF16)
        bwstk = contextlib.ExitStack()
        wfstage2 = SB(bwstk, "wfstage2", [128, 2 * C], F32)
        bwf2 = [Buf(), Buf()]
        bwr2, bwk2, bwv2, boutw, bw1, bw2 = (Buf(readonly=True) for _ in range(6))

        def load_w2(dst, bdst, srcd, scale_col=None, scale_cols=None, bar=None,
                    scale_tok=None):
            last = None
            for ck in range(NCT):
                sl = ck % 2
                deps = bwf2[sl].wdeps()
                if bar is not None and ck < 2:
                    deps = deps + bar
                dl = p.dma("scalar", "wl", (lambda ck, sl, srcd: lambda e: e.dma_start(
                    out=wfstage2[:, C * sl:C * (sl + 1)],
                    in_=srcd[128 * ck:128 * (ck + 1), :]))(ck, sl, srcd),
                    deps=deps)
                bwf2[sl].wc(dl)
                if scale_cols is not None:
                    sc_ap = scale_cols[:, ck:ck + 1]
                elif scale_col is not None:
                    sc_ap = scale_col
                else:
                    sc_ap = None
                if sc_ap is None:
                    tc = p.op("scalar", (lambda ck, sl, dst: lambda e: e.activation(
                        out=dst[:, C * ck:C * (ck + 1)],
                        in_=wfstage2[:, C * sl:C * (sl + 1)], func=AF.Identity))(ck, sl, dst),
                        deps=[dl])
                else:
                    tc = p.op("scalar", (lambda ck, sl, dst, sc_ap: lambda e: e.activation(
                        out=dst[:, C * ck:C * (ck + 1)],
                        in_=wfstage2[:, C * sl:C * (sl + 1)], func=AF.Identity,
                        scale=sc_ap))(ck, sl, dst, sc_ap),
                        deps=[dl] + ([scale_tok] if scale_tok else [cc]))
                bwf2[sl].rc(tc)
                last = tc
            bdst.w = last
            return last

        load_w2(wr_s2, bwr2, WRT, bar=gbar)
        load_w2(wk_s2, bwk2, WKT)
        load_w2(wv_s2, bwv2, WVT, scale_col=ccols[:, 2:3])
        load_w2(outw_s, boutw, OUTW, scale_col=ccols[:, 3:4])
        load_w2(w1_s, bw1, W1, scale_cols=mng_c, scale_tok=bmng.w)
        load_w2(w2_s, bw2, W2)
        wbar = p.hard_barrier()
        bwstk.close()

        x1i2 = SB(bstk, "x1i2", [128, 4 * C], F32)
        bx1i2 = Buf()
        qTin = SB(bstk, "qTin", [128, NCT * 512], BF16)
        bqT = Buf()
        kactT2 = qTin
        bkact2 = bqT
        a0b = SB(bstk, "a0b", [128, 2 * C], F32)
        ba0 = [Buf(), Buf()]
        outb_f = a0b
        boutf = ba0
        yT = SB(bstk, "yT", [128, NCT * 512], BF16)
        byT = Buf()
        xrT2 = yT
        bxr2 = byT
        a1b = SB(bstk, "a1b", [128, 2 * C], F32)
        ba1 = [Buf(), Buf()]
        t3b2 = a1b
        bt32 = ba1
        a1nb = SB(bstk, "a1nb", [128, 2 * C], BF16)
        ba1nb = [Buf(), Buf()]
        ln3b = a1nb
        bln3b = ba1nb
        a1nT = SB(bstk, "a1nT", [128, NCT * 512], BF16)
        ba1nT = Buf()
        xkT2 = a1nT
        bxk2 = ba1nT
        hT = SB(bstk, "hT", [128, NCT * 512], BF16)
        bhT = Buf()
        ln3T = SB(bstk, "ln3T", [128, NCT * 513], BF16)
        bln3T = Buf()
        dscr_B = sqd.bitcast(BF16)[:, C:2 * C]
        bdscr_B = Buf()
        krelu_B = SB(bstk, "krelu2", [128, 1024], F32)
        bkrelu_B = Buf()
        rsb_B = hT.bitcast(F32)

        tz4 = None
        for k in range(NCT):
            tz4 = p.op("vector", (lambda k: lambda e: e.memset(
                ln3T[:, 513 * k:513 * k + 1], 0.0))(k),
                deps=(bln3T.wdeps() + wbar) if k == 0 else [])
        bln3T.wc(tz4)

        for m in range(NMAC if b_parts & 1 else 0):
            t_x1 = p.dma("sync", "x1i", (lambda m: lambda e: e.dma_start(
                out=x1i2.rearrange("p (j c) -> p j c", j=4),
                in_=X1S.rearrange("(mm j p) c -> p mm j c", p=128, j=4)[:, m, :, :]))(m),
                deps=bx1i2.wdeps() + (wbar if m == 0 else []))
            bx1i2.wc(t_x1)
            t_qi = p.dma("scalar", "qi", (lambda m: lambda e: e.dma_start(
                out=qTin.rearrange("p (ik t) -> p ik t", ik=NCT),
                in_=QTS.rearrange("ik p t -> p ik t")[:, :, MAC * m:MAC * (m + 1)]))(m),
                deps=bqT.wdeps() + (wbar if m == 0 else []))
            bqT.wc(t_qi)
            y_last = None
            for jq in range(NCT):
                bank, bankb = (PW, bPW) if jq % 2 == 0 else (PX, bPX)
                mms = []
                for ik in range(NCT):
                    deps = [bGb.w, bqT.w]
                    if ik == 0:
                        deps += bankb.wdeps()
                    mms.append(p.op("tensor", (lambda bank, jq, ik: lambda e: e.matmul(
                        bank[:, :], G_b[:, C * ik + 128 * jq:C * ik + 128 * (jq + 1)],
                        qTin[:, 512 * ik:512 * (ik + 1)],
                        start=(ik == 0), stop=(ik == NCT - 1)))(bank, jq, ik), deps=deps))
                bankb.wc(mms[-1])
                bqT.rc(mms[-1])
                deps_e = [mms[-1]]
                if jq == 0:
                    deps_e += byT.wdeps()
                y_last = p.op("scalar", (lambda bank, jq: lambda e: e.activation(
                    out=yT[:, 512 * jq:512 * (jq + 1)], in_=bank[:, :],
                    func=AF.Identity))(bank, jq), deps=deps_e)
                bankb.rc(y_last)
            byT.wc(y_last)
            if dbg:
                t_dmp3 = p.dma("sync", "dbg", (lambda m: lambda e: e.dma_start(
                    out=YD.rearrange("jq q t -> q jq t")[:, :, MAC * m:MAC * (m + 1)],
                    in_=yT.rearrange("p (jq t) -> p jq t", jq=NCT)))(m), deps=byT.r())
                byT.rc(t_dmp3)
            for half in range(2):
              for j in (2 * half, 2 * half + 1):
                asl = j % 2
                stoks, sc_ap, nb_ap = ln_stats(x1i2[:, C * j:C * (j + 1)], [bx1i2.w],
                                               j % 4)
                t_a0 = p.op("scalar", (lambda j, asl, sc_ap, nb_ap: lambda e: e.activation(
                    out=a0b[:, C * asl:C * (asl + 1)], in_=x1i2[:, C * j:C * (j + 1)],
                    func=AF.Identity, scale=sc_ap, bias=nb_ap))(j, asl, sc_ap, nb_ap),
                    deps=stoks + [bx1i2.w] + ba0[asl].wdeps())
                ba0[asl].wc(t_a0)
                bstt[j % 4].rc(t_a0)
                bx1i2.rc(t_a0)
                mms = []
                for h in range(2):
                    for jv in range(NCT):
                        deps = [byT.w, boutw.w]
                        if h == 0 and jv == 0:
                            deps += bPA.wdeps()
                        mms.append(p.op("tensor", (lambda j, h, jv: lambda e: e.matmul(
                            PA[:, 512 * h:512 * (h + 1)],
                            yT[:, 512 * jv + 128 * j:512 * jv + 128 * (j + 1)],
                            outw_s[:, C * jv + 512 * h:C * jv + 512 * (h + 1)],
                            start=(jv == 0), stop=False))(j, h, jv), deps=deps))
                byT.rc(mms[-1])
                for h in range(2):
                    mms.append(p.op("tensor", (lambda h: lambda e: e.matmul(
                        PA[:, 512 * h:512 * (h + 1)], onesb[0:1, :],
                        outb_rb[0:1, 512 * h:512 * (h + 1)], start=False, stop=True))(h),
                        deps=[boutb.w, ld["ONESB"]]))
                bPA.wc(mms[-1])
                t_a1 = p.op("vector", (lambda asl: lambda e: e.tensor_add(
                    a1b[:, C * asl:C * (asl + 1)], a0b[:, C * asl:C * (asl + 1)],
                    PA[:, :]))(asl),
                    deps=[mms[-1], t_a0] + ba1[asl].wdeps())
                bPA.rc(t_a1)
                ba0[asl].rc(t_a1)
                ba1[asl].wc(t_a1)
                if dbg:
                    t_dmp = p.dma("sync", "dbg", (lambda m, j, asl: lambda e: e.dma_start(
                        out=A1D[MAC * m + MIC * j:MAC * m + MIC * (j + 1), :],
                        in_=a1b[:, C * asl:C * (asl + 1)]))(m, j, asl), deps=[t_a1])
                    ba1[asl].rc(t_dmp)
                stoks2, sc2, nb2 = ln_stats(a1b[:, C * asl:C * (asl + 1)], [t_a1], j % 4)
                t_n2 = p.op("scalar", (lambda asl, sc2, nb2: lambda e: e.activation(
                    out=a1nb[:, C * asl:C * (asl + 1)], in_=a1b[:, C * asl:C * (asl + 1)],
                    func=AF.Identity, scale=sc2, bias=nb2))(asl, sc2, nb2),
                    deps=stoks2 + [t_a1] + ba1nb[asl].wdeps())
                ba1nb[asl].wc(t_n2)
                bstt[j % 4].rc(t_n2)
                ba1[asl].rc(t_n2)
                transpose8(
                    (lambda asl: lambda k: a1nb[:, C * asl + 128 * k:C * asl + 128 * (k + 1)])(asl),
                    None, [t_n2], ba1nT,
                    dst_view=a1nT.rearrange("p (k t) -> p k t", t=512)[:, :, 128 * j:128 * (j + 1)])
                ba1nb[asl].rc(ba1nT.w)
              h_last = None
              for dk in range(NCT):
                bank, bankb = (PW, bPW) if dk % 2 == 0 else (PX, bPX)
                mms = []
                for ck in range(NCT):
                    deps = [ba1nT.w, bw1.w]
                    if ck == 0:
                        deps += bankb.wdeps()
                    mms.append(p.op("tensor", (lambda bank, dk, ck, half: lambda e: e.matmul(
                        bank[:, 0:256], w1_s[:, C * ck + 128 * dk:C * ck + 128 * (dk + 1)],
                        a1nT[:, 512 * ck + 256 * half:512 * ck + 256 * (half + 1)],
                        start=(ck == 0), stop=(ck == NCT - 1)))(bank, dk, ck, half),
                        deps=deps))
                bankb.wc(mms[-1])
                ba1nT.rc(mms[-1])
                deps_e = [mms[-1], bb1c.w]
                if dk == 0:
                    deps_e += bhT.wdeps()
                h_last = p.op("scalar", (lambda bank, dk, half: lambda e: e.activation(
                    out=hT[:, 512 * dk + 256 * half:512 * dk + 256 * (half + 1)],
                    in_=bank[:, 0:256],
                    func=AF.Relu, bias=b1_c[:, dk:dk + 1]))(bank, dk, half), deps=deps_e)
                bankb.rc(h_last)
              bhT.wc(h_last)
              if dbg:
                t_dmp5 = p.dma("sync", "dbg", (lambda m, half: lambda e: e.dma_start(
                    out=HD.rearrange("dk q t -> q dk t")[:, :, MAC * m + 256 * half:MAC * m + 256 * (half + 1)],
                    in_=hT.rearrange("p (dk t) -> p dk t", dk=NCT)[:, :, 256 * half:256 * (half + 1)]))(m, half),
                    deps=bhT.r())
                bhT.rc(t_dmp5)
              for j in (2 * half, 2 * half + 1):
                asl = j % 2
                mms = []
                for h in range(2):
                    for dk in range(NCT):
                        deps = [bhT.w, bw2.w]
                        if h == 0 and dk == 0:
                            deps += bPB.wdeps()
                        mms.append(p.op("tensor", (lambda j, h, dk: lambda e: e.matmul(
                            PB[:, 512 * h:512 * (h + 1)],
                            hT[:, 512 * dk + 128 * j:512 * dk + 128 * (j + 1)],
                            w2_s[:, C * dk + 512 * h:C * dk + 512 * (h + 1)],
                            start=(dk == 0), stop=False))(j, h, dk), deps=deps))
                bhT.rc(mms[-1])
                for h in range(2):
                    mms.append(p.op("tensor", (lambda h: lambda e: e.matmul(
                        PB[:, 512 * h:512 * (h + 1)], onesb[0:1, :],
                        b2_rb[0:1, 512 * h:512 * (h + 1)], start=False, stop=True))(h),
                        deps=[bb2.w, ld["ONESB"]]))
                bPB.wc(mms[-1])
                t_a2 = p.op("vector", (lambda asl: lambda e: e.tensor_add(
                    a1b[:, C * asl:C * (asl + 1)], a1b[:, C * asl:C * (asl + 1)],
                    PB[:, :]))(asl),
                    deps=[mms[-1], ba1[asl].w] + ba1[asl].rs)
                bPB.rc(t_a2)
                ba1[asl].wc(t_a2)
                if dbg:
                    t_dmp4 = p.dma("sync", "dbg", (lambda m, j, asl: lambda e: e.dma_start(
                        out=A2D[MAC * m + MIC * j:MAC * m + MIC * (j + 1), :],
                        in_=a1b[:, C * asl:C * (asl + 1)]))(m, j, asl), deps=[t_a2])
                    ba1[asl].rc(t_dmp4)
                t_x2 = p.op("vector", (lambda j, asl: lambda e: e.tensor_add(
                    x1i2[:, C * j:C * (j + 1)], x1i2[:, C * j:C * (j + 1)],
                    a1b[:, C * asl:C * (asl + 1)]))(j, asl),
                    deps=[t_a2, bx1i2.w] + bx1i2.rs + ([t_dmp4] if dbg else []))
                ba1[asl].rc(t_x2)
                bx1i2.wc(t_x2)
                if dbg:
                    t_dmp2 = p.dma("sync", "dbg", (lambda m, j: lambda e: e.dma_start(
                        out=X2D[MAC * m + MIC * j:MAC * m + MIC * (j + 1), :],
                        in_=x1i2[:, C * j:C * (j + 1)]))(m, j), deps=[bx1i2.w])
                    bx1i2.rc(t_dmp2)
                emit_lnT(x1i2[:, C * j:C * (j + 1)], [bx1i2.w], j, ln3T, bln3T,
                         ln3b, bln3b)
                bx1i2.rc(bln3T.w)

            def b_out(j, rsl, vtok, sig_tok, rsb_ap, brsb_slot, m=m):
                osl = j % 2
                t_t3 = p.op("vector", (lambda rsb_ap, rsl: lambda e: e.scalar_tensor_tensor(
                    out=t3b2[:, C * rsl:C * (rsl + 1)], in0=PB[:, :], scalar=1.0,
                    in1=rsb_ap, op0=ALU.mult, op1=ALU.mult))(rsb_ap, rsl),
                    deps=[vtok, sig_tok] + bt32[rsl].wdeps())
                bPB.rc(t_t3)
                brsb_slot.rc(t_t3)
                bt32[rsl].wc(t_t3)
                t_o = p.op("vector", (lambda j, rsl, osl: lambda e: e.tensor_add(
                    outb_f[:, C * osl:C * (osl + 1)], x1i2[:, C * j:C * (j + 1)],
                    t3b2[:, C * rsl:C * (rsl + 1)]))(j, rsl, osl),
                    deps=[t_t3, bx1i2.w] + boutf[osl].wdeps())
                bt32[rsl].rc(t_o)
                bx1i2.rc(t_o)
                boutf[osl].wc(t_o)
                t_od = p.dma("sync", "oo", (lambda m, j, osl: lambda e: e.dma_start(
                    out=OUT[MAC * m + MIC * j:MAC * m + MIC * (j + 1), :],
                    in_=outb_f[:, C * osl:C * (osl + 1)]))(m, j, osl),
                    deps=[t_o])
                boutf[osl].rc(t_od)

            emit_ffn_mix(ln3T, bln3T, xrT2, bxr2, xkT2, bxk2, dscr_B, bdscr_B)
            emit_ffn_mms(xrT2, bxr2, xkT2, bxk2, kactT2, bkact2,
                         wr_s2, wk_s2, wv_s2, bwr2, bwk2, bwv2, b_out,
                         krelu_B, bkrelu_B, rsb_B, [bhT, bhT])

        p.op("sync", lambda e: e.nop(),
             deps=([("oo", p.dma_counts["oo"])] if p.dma_counts.get("oo") else p.barrier_tokens()) +
                  ([("dbg", p.dma_counts["dbg"])] if dbg else []))

        bstk.close()
        p.build_and_emit(root)
    return nc


def _prep_shared(inputs):
    f32 = np.float32
    ac = np.ascontiguousarray
    return {
        "WRT": ac(inputs["L0_ffn_W_r"].T.astype(f32)),
        "WKT": ac(inputs["L0_ffn_W_k"].T.astype(f32)),
        "WVT": ac(inputs["L0_ffn_W_v"].T.astype(f32)),
        "WG": ac(inputs["L0_attn_gates_w"].astype(f32)),
        "LRW": ac(inputs["L0_attn_lr_w"].reshape(1, -1).astype(f32)),
        "OUTW": ac(inputs["L0_attn_out_w"].astype(f32)),
        "W1": ac(inputs["L0_attn_w1"].astype(f32)),
        "W2": ac(inputs["L0_attn_w2"].astype(f32)),
        "DNG": ac(inputs["L0_attn_dn_g"].reshape(1, -1).astype(f32)),
        "DNB": ac(inputs["L0_attn_dn_b"].reshape(1, -1).astype(f32)),
        "MNG": ac(inputs["L0_attn_mn_g"].reshape(1, -1).astype(f32)),
        "MNB": ac(inputs["L0_attn_mn_b"].reshape(1, -1).astype(f32)),
        "MUR": ac(inputs["L0_ffn_mu_r"].reshape(1, -1).astype(f32)),
        "MUK": ac(inputs["L0_ffn_mu_k"].reshape(1, -1).astype(f32)),
        "GATESB": ac(inputs["L0_attn_gates_b"].reshape(1, -1).astype(f32)),
        "OUTB": ac(inputs["L0_attn_out_b"].reshape(1, -1).astype(f32)),
        "B1": ac(inputs["L0_attn_b1"].reshape(1, -1).astype(f32)),
        "B2": ac(inputs["L0_attn_b2"].reshape(1, -1).astype(f32)),
        "LRB": ac(inputs["L0_attn_lr_b"].reshape(1, -1).astype(f32)),
        "IDENT_F": np.eye(128, dtype=f32),
        "IDENT_B": np.eye(128).astype(ml_dtypes.bfloat16),
        "ONESB": np.ones((1, 128)).astype(ml_dtypes.bfloat16),
        "LMASK": np.tril(np.ones((128, 128), dtype=f32), -1),
        "TWOI": (2.0 * np.eye(128)).astype(ml_dtypes.bfloat16),
    }


def kernel(**inputs):
    x = np.ascontiguousarray(inputs["L0_attn_x"], dtype=np.float32)
    B = x.shape[0]
    shared = _prep_shared(inputs)
    nc = build(T=x.shape[1])
    in_maps = [dict(shared, X=np.ascontiguousarray(x[b])) for b in range(B)]
    res = run_bass_kernel_spmd(nc, in_maps, core_ids=list(range(B)))
    return np.stack([np.asarray(res.results[b]["OUT"], dtype=np.float32)
                     for b in range(B)], axis=0)


# version 14
# speedup vs baseline: 1.2221x; 1.0064x over previous
import contextlib
from collections import defaultdict

import numpy as np
import ml_dtypes

import concourse.bass as bass
import concourse.mybir as mybir
from concourse.bass_utils import run_bass_kernel_spmd

F32 = mybir.dt.float32
BF16 = mybir.dt.bfloat16
AF = mybir.ActivationFunctionType
ALU = mybir.AluOpType
AX = mybir.AxisListType

C = 1024
NCT = 8
MAC = 512
MIC = 128
EPS = 1e-5

ENGINES = ("tensor", "vector", "scalar", "gpsimd", "sync")


class _Op:
    __slots__ = ("idx", "engine", "fn", "deps", "seq", "dma")

    def __init__(self, idx, engine, fn, deps, seq, dma):
        self.idx = idx
        self.engine = engine
        self.fn = fn
        self.deps = deps
        self.seq = seq
        self.dma = dma


class Prog:
    def __init__(self, nc):
        self.nc = nc
        self.ops = []
        self.counts = {e: 0 for e in ENGINES}
        self.dma_counts = defaultdict(int)
        self.dma_sem_names = []

    def op(self, engine, fn, deps=()):
        self.counts[engine] += 1
        o = _Op(len(self.ops), engine, fn, self._norm(deps), self.counts[engine], None)
        self.ops.append(o)
        return (engine, self.counts[engine])

    def dma(self, queue, dma_sem, fn, deps=()):
        if dma_sem not in self.dma_counts:
            self.dma_sem_names.append(dma_sem)
        self.dma_counts[dma_sem] += 16
        o = _Op(len(self.ops), queue, fn, self._norm(deps), self.dma_counts[dma_sem], dma_sem)
        self.ops.append(o)
        return (dma_sem, self.dma_counts[dma_sem])

    @staticmethod
    def _norm(deps):
        out = []
        seen = set()
        for d in deps:
            if d is None or d in seen:
                continue
            seen.add(d)
            out.append(d)
        return out

    def barrier_tokens(self):
        toks = [(e, self.counts[e]) for e in ENGINES if self.counts[e] > 0]
        toks += [(s, self.dma_counts[s]) for s in self.dma_sem_names]
        return toks

    def hard_barrier(self):
        toks = self.barrier_tokens()
        for e in ENGINES:
            if self.counts[e] > 0:
                self.op(e, lambda eng: eng.nop(), deps=toks)
        return self.barrier_tokens()

    def emit(self, block, sems, dma_sems):
        waited = defaultdict(set)
        for o in self.ops:
            for (s, v) in o.deps:
                waited[s].add(v)
        streams = {e: [o for o in self.ops if o.engine == e] for e in ENGINES}
        dq = {}
        for o in self.ops:
            if o.dma:
                q = dq.setdefault(o.dma, o.engine)
                assert q == o.engine, f"dma sem {o.dma} used from two queues"
        dcross = {s: sorted(waited[s]) for s in self.dma_sem_names}

        def emit_stream(eng_api, engine):
            max_w = defaultdict(int)
            crossing = sorted(waited[engine])
            ci = 0
            count = 0
            dcnt = defaultdict(int)
            dci = defaultdict(int)
            for o in streams[engine]:
                if not o.dma:
                    while ci < len(crossing) and crossing[ci] <= count:
                        v = crossing[ci]
                        if max_w[engine] < v < o.seq:
                            eng_api.wait_ge(sems[engine], v)
                            max_w[engine] = v
                        ci += 1
                else:
                    s = o.dma
                    cr = dcross[s]
                    while dci[s] < len(cr) and cr[dci[s]] <= dcnt[s]:
                        v = cr[dci[s]]
                        if max_w[s] < v:
                            eng_api.wait_ge(dma_sems[s], v)
                            max_w[s] = v
                        dci[s] += 1
                    dcnt[s] += 16
                for (s, v) in o.deps:
                    if v > max_w[s]:
                        eng_api.wait_ge(sems[s] if s in ENGINES else dma_sems[s], v)
                        max_w[s] = v
                inst = o.fn(eng_api)
                if o.dma:
                    inst.then_inc(dma_sems[o.dma], 16)
                else:
                    inst.then_inc(sems[engine], 1)
                    count += 1

        for engine in ENGINES:
            if not streams[engine]:
                continue
            section = getattr(block, engine)

            def make(engine):
                def _(eng_api):
                    emit_stream(eng_api, engine)
                return _
            section(make(engine))

    def build_and_emit(self, stack):
        nc = self.nc
        sems = {}
        for e in ENGINES:
            if any(o.engine == e and not o.dma for o in self.ops):
                sems[e] = stack.enter_context(nc.semaphore(f"sem_{e}"))
        dma_sems = {}
        for name in self.dma_sem_names:
            dma_sems[name] = stack.enter_context(nc.semaphore(f"ds_{name}"))
        block = stack.enter_context(nc.Block())
        self.emit(block, sems, dma_sems)


class Buf:
    __slots__ = ("w", "rs", "ro")

    def __init__(self, readonly=False):
        self.w = None
        self.rs = []
        self.ro = readonly

    def r(self):
        return [self.w] if self.w is not None else []

    def rc(self, tok):
        if not self.ro:
            self.rs.append(tok)

    def wdeps(self):
        d = list(self.rs)
        if self.w is not None:
            d.append(self.w)
        return d

    def wc(self, tok):
        self.w = tok
        self.rs = []


def build(T=8192, dbg=False, stop_after=None, a1_parts=127, b_parts=255):
    nc = bass.Bass("TRN2", target_bir_lowering=False)
    NMAC = T // MAC

    X = nc.dram_tensor("X", [T, C], F32, kind="ExternalInput")
    OUT = nc.dram_tensor("OUT", [T, C], F32, kind="ExternalOutput")
    WRT = nc.dram_tensor("WRT", [C, C], F32, kind="ExternalInput")
    WKT = nc.dram_tensor("WKT", [C, C], F32, kind="ExternalInput")
    WVT = nc.dram_tensor("WVT", [C, C], F32, kind="ExternalInput")
    WG = nc.dram_tensor("WG", [C, 3 * C], F32, kind="ExternalInput")
    LRW = nc.dram_tensor("LRW", [1, C], F32, kind="ExternalInput")
    OUTW = nc.dram_tensor("OUTW", [C, C], F32, kind="ExternalInput")
    W1 = nc.dram_tensor("W1", [C, C], F32, kind="ExternalInput")
    W2 = nc.dram_tensor("W2", [C, C], F32, kind="ExternalInput")
    rows_in = {}
    for nm, n in [("DNG", C), ("DNB", C), ("MNG", C), ("MNB", C), ("MUR", C),
                  ("MUK", C), ("GATESB", 3 * C), ("OUTB", C), ("B1", C), ("B2", C),
                  ("LRB", 1)]:
        rows_in[nm] = nc.dram_tensor(nm, [1, n], F32, kind="ExternalInput")
    IDENT_F = nc.dram_tensor("IDENT_F", [128, 128], F32, kind="ExternalInput")
    IDENT_B = nc.dram_tensor("IDENT_B", [128, 128], BF16, kind="ExternalInput")
    ONESB = nc.dram_tensor("ONESB", [1, 128], BF16, kind="ExternalInput")
    LMASK = nc.dram_tensor("LMASK", [128, 128], F32, kind="ExternalInput")
    TWOI = nc.dram_tensor("TWOI", [128, 128], BF16, kind="ExternalInput")
    skind = {"kind": "ExternalOutput"} if dbg else {}
    X1S = nc.dram_tensor("X1S", [T, C], F32, **skind)
    QTS = nc.dram_tensor("QTS", [NCT, 128, T], BF16, **skind)
    if dbg:
        SD = nc.dram_tensor("SD", [128, NCT * C], F32, kind="ExternalOutput")
        GD = nc.dram_tensor("GD", [128, NCT * C], F32, kind="ExternalOutput")
        A1D = nc.dram_tensor("A1D", [T, C], F32, kind="ExternalOutput")
        X2D = nc.dram_tensor("X2D", [T, C], F32, kind="ExternalOutput")
        YD = nc.dram_tensor("YD", [NCT, 128, T], BF16, kind="ExternalOutput")
        A2D = nc.dram_tensor("A2D", [T, C], F32, kind="ExternalOutput")
        HD = nc.dram_tensor("HD", [NCT, 128, T], BF16, kind="ExternalOutput")

    p = Prog(nc)
    root = contextlib.ExitStack()

    def SB(stk, n, sh, dt):
        return stk.enter_context(nc.sbuf_tensor(n, sh, dt))

    with root:
        ident_f = SB(root, "ident_f", [128, 128], F32)
        ident_b = SB(root, "ident_b", [128, 128], BF16)
        onesb = SB(root, "onesb", [1, 128], BF16)
        lmask = SB(root, "lmask", [128, 128], F32)
        twoi = SB(root, "twoi", [128, 128], BF16)
        twoi4 = SB(root, "twoi4", [128, 512], BF16)
        ccols = SB(root, "ccols", [128, 8], F32)
        onef = SB(root, "onef", [1, 1], F32)
        dng_c = SB(root, "dng_c", [128, NCT], F32)
        mng_c = SB(root, "mng_c", [128, NCT], F32)
        dnb_cb = SB(root, "dnb_cb", [128, NCT], BF16)
        mnb_cb = SB(root, "mnb_cb", [128, NCT], BF16)
        mur_c = SB(root, "mur_c", [128, NCT], F32)
        muk_c = SB(root, "muk_c", [128, NCT], F32)
        qb_c = SB(root, "qb_c", [128, NCT], F32)
        kb_c = SB(root, "kb_c", [128, NCT], F32)
        b1_c = SB(root, "b1_c", [128, NCT], F32)
        lrb_bc = SB(root, "lrb_bc", [128, 1], F32)
        lrw_b = SB(root, "lrw_b", [128, NCT], BF16)
        browv_b = SB(root, "browv_b", [1, C], BF16)
        outb_rb = SB(root, "outb_rb", [1, C], BF16)
        b2_rb = SB(root, "b2_rb", [1, C], BF16)
        stt = SB(root, "stt", [128, 32], F32)
        sqd = SB(root, "sqd", [128, 2 * C], BF16)

        PA = root.enter_context(nc.psum_tensor("PA", [128, 1024], F32))
        PB = root.enter_context(nc.psum_tensor("PB", [128, 1024], F32))
        PW = root.enter_context(nc.psum_tensor("PW", [128, 512], F32))
        PX = root.enter_context(nc.psum_tensor("PX", [128, 512], F32))
        PT = root.enter_context(nc.psum_tensor("PT", [128, 1024], BF16))
        PS = root.enter_context(nc.psum_tensor("PS", [128, 512], F32))
        bPA, bPB, bPW, bPX, bPT, bPS = (Buf() for _ in range(6))

        bstt = [Buf() for _ in range(4)]
        bsqd = [Buf(), Buf()]

        bccols = Buf()
        btwoi4 = Buf()
        bonef = Buf()
        bdng, bmng, bdnbb, bmnbb, bmur, bmuk = (Buf() for _ in range(6))
        bqb, bkb, bb1c, blrb, blrwb, bbrowv, boutb, bb2 = (Buf() for _ in range(8))

        gstk = contextlib.ExitStack()
        wg_s = SB(gstk, "wg_s", [128, NCT * 3 * C], BF16)
        bwg = Buf(readonly=True)
        fwstk = contextlib.ExitStack()
        wr_s = SB(fwstk, "wr_s", [128, NCT * C], BF16)
        wk_s = SB(fwstk, "wk_s", [128, NCT * C], BF16)
        wv_s = SB(fwstk, "wv_s", [128, NCT * C], BF16)
        bwr, bwk, bwv = (Buf(readonly=True) for _ in range(3))
        initstk = contextlib.ExitStack()
        rowbuf = SB(initstk, "rowbuf", [1, 3 * C], F32)
        brow = Buf()
        brow2t = SB(initstk, "brow2", [1, 3 * C], F32)
        bbrow2 = Buf()
        colf = SB(initstk, "colf", [128, NCT], F32)
        bcolf = Buf()
        wstage = SB(initstk, "wstage", [128, 3 * C], F32)
        wstage2 = SB(initstk, "wstage2", [128, 3 * C], F32)
        wraw = SB(initstk, "wraw", [128, 3 * C], BF16)
        bwstage, bwstage2, bwraw = Buf(), Buf(), Buf()
        wstages = [(wstage, bwstage), (wstage2, bwstage2)]

        ld = {}
        for nm, dst, src in [("IDENT_F", ident_f, IDENT_F), ("IDENT_B", ident_b, IDENT_B),
                             ("ONESB", onesb, ONESB), ("LMASK", lmask, LMASK),
                             ("TWOI", twoi, TWOI)]:
            ld[nm] = p.dma("sync", "cl", (lambda dst, src: lambda e: e.dma_start(
                out=dst[:, :], in_=src[:, :]))(dst, src))
        cc_toks = []
        for i, v in [(0, EPS), (1, -0.5), (2, 0.5), (3, -1.0), (4, 1.0 + EPS),
                     (5, EPS * EPS)]:
            cc_toks.append(p.op("vector", (lambda i, v: lambda e: e.memset(
                ccols[:, i:i + 1], v))(i, v)))
        bccols.wc(cc_toks[-1])
        cc = cc_toks[-1]
        t_onef = p.op("vector", lambda e: e.memset(onef[0:1, 0:1], 1.0))
        bonef.wc(t_onef)
        tw_toks = []
        for i in range(4):
            tw_toks.append(p.op("vector", (lambda i: lambda e: e.tensor_copy(
                twoi4[:, 128 * i:128 * (i + 1)], twoi[:, :]))(i),
                deps=[ld["TWOI"]] + (btwoi4.wdeps() if i == 0 else [])))
        btwoi4.wc(tw_toks[-1])

        def load_row(nm, n):
            tl = p.dma("sync", "cl", (lambda nm, n: lambda e: e.dma_start(
                out=rowbuf[0:1, 0:n], in_=rows_in[nm][0:1, 0:n]))(nm, n),
                deps=brow.wdeps())
            brow.wc(tl)
            return tl

        def row_to_cols(row_ap, row_tok, n, dst, bdst):
            mms = []
            for k in range(n // 128):
                deps = [row_tok, t_onef]
                if k == 0:
                    deps += bPS.wdeps()
                mms.append(p.op("tensor", (lambda k, row_ap: lambda e: e.matmul(
                    PS[:, k:k + 1], row_ap[0:1, 128 * k:128 * (k + 1)],
                    onef[0:1, 0:1], start=True, stop=True))(k, row_ap), deps=deps))
            bPS.wc(mms[-1])
            t_e = p.op("vector", (lambda n: lambda e: e.tensor_copy(
                dst[:, 0:n // 128], PS[:, 0:n // 128]))(n),
                deps=[mms[-1]] + bdst.wdeps())
            bPS.rc(t_e)
            bdst.wc(t_e)
            return t_e

        for nm, dst, bdst in [("DNG", dng_c, bdng), ("MNG", mng_c, bmng),
                              ("MUR", mur_c, bmur), ("MUK", muk_c, bmuk)]:
            tl = load_row(nm, C)
            te = row_to_cols(rowbuf, tl, C, dst, bdst)
            brow.rc(te)
        for nm, dstb, bdstb in [("DNB", dnb_cb, bdnbb), ("MNB", mnb_cb, bmnbb)]:
            tl = load_row(nm, C)
            te = row_to_cols(rowbuf, tl, C, colf, bcolf)
            brow.rc(te)
            tc = p.op("scalar", (lambda dstb: lambda e: e.activation(
                out=dstb[:, :], in_=colf[:, :], func=AF.Identity))(dstb),
                deps=[te] + bdstb.wdeps())
            bcolf.rc(tc)
            bdstb.wc(tc)

        fold_regions = [PA[0:1, 0:512], PA[0:1, 512:1024], PB[0:1, 0:512],
                        PB[0:1, 512:1024], PW[0:1, 0:512], PX[0:1, 0:512]]
        fold_banks = [bPA, bPA, bPB, bPB, bPW, bPX]
        gate_fold = [None] * 6
        wg_last = None
        for ck in range(NCT):
            wst, bwst = wstages[ck % 2]
            dl = p.dma("scalar", "wl", (lambda ck, wst: lambda e: e.dma_start(
                out=wst[:, :], in_=WG[128 * ck:128 * (ck + 1), :]))(ck, wst),
                deps=bwst.wdeps())
            bwst.wc(dl)
            tr = p.op("scalar", (lambda wst: lambda e: e.activation(
                out=wraw[:, :], in_=wst[:, :], func=AF.Identity))(wst),
                      deps=[dl] + bwraw.wdeps())
            bwraw.wc(tr)
            bwst.rc(tr)
            for h in range(6):
                deps = [tr, bdnbb.w]
                if ck == 0:
                    deps += fold_banks[h].wdeps()
                mm = p.op("tensor", (lambda h, ck: lambda e: e.matmul(
                    fold_regions[h], dnb_cb[:, ck:ck + 1], wraw[:, 512 * h:512 * (h + 1)],
                    start=(ck == 0), stop=(ck == NCT - 1)))(h, ck), deps=deps)
                bwraw.rc(mm)
                if ck == NCT - 1:
                    fold_banks[h].wc(mm)
                    gate_fold[h] = mm
            ts = p.op("scalar", (lambda ck, wst: lambda e: e.activation(
                out=wg_s[:, 3 * C * ck:3 * C * (ck + 1)], in_=wst[:, :],
                func=AF.Identity, scale=dng_c[:, ck:ck + 1]))(ck, wst),
                deps=[dl, bdng.w])
            bwst.rc(ts)
            bdng.rc(ts)
            wg_last = ts
        bwg.w = wg_last
        t_gb = load_row("GATESB", 3 * C)
        ev_last = None
        for h in range(6):
            ev = p.op("vector", (lambda h: lambda e: e.tensor_add(
                brow2t[0:1, 512 * h:512 * (h + 1)], rowbuf[0:1, 512 * h:512 * (h + 1)],
                fold_regions[h]))(h),
                deps=[t_gb, gate_fold[h]] + (bbrow2.wdeps() if h == 0 else []))
            fold_banks[h].rc(ev)
            brow.rc(ev)
            ev_last = ev
        bbrow2.wc(ev_last)
        t_bv = p.op("scalar", lambda e: e.activation(out=browv_b[0:1, :],
                                                     in_=brow2t[0:1, 2 * C:3 * C],
                                                     func=AF.Identity), deps=[ev_last])
        bbrow2.rc(t_bv)
        bbrowv.wc(t_bv)
        t_qbc = row_to_cols(brow2t, ev_last, C, qb_c, bqb)
        bbrow2.rc(t_qbc)
        mms = []
        for k in range(NCT):
            deps = [ev_last, t_onef]
            if k == 0:
                deps += bPS.wdeps()
            mms.append(p.op("tensor", (lambda k: lambda e: e.matmul(
                PS[:, k:k + 1], brow2t[0:1, C + 128 * k:C + 128 * (k + 1)],
                onef[0:1, 0:1], start=True, stop=True))(k), deps=deps))
        bPS.wc(mms[-1])
        t_kbc = p.op("vector", lambda e: e.tensor_copy(kb_c[:, 0:NCT], PS[:, 0:NCT]),
                     deps=[mms[-1]] + bkb.wdeps())
        bPS.rc(t_kbc)
        bkb.wc(t_kbc)
        bbrow2.rc(t_kbc)

        tl = p.dma("sync", "cl", lambda e: e.dma_start(out=rowbuf[0:1, 0:C],
                                                       in_=LRW[0:1, 0:C]),
                   deps=brow.wdeps())
        brow.wc(tl)
        t_lrwf = row_to_cols(rowbuf, tl, C, colf, bcolf)
        brow.rc(t_lrwf)
        t_lrwb = p.op("vector", lambda e: e.tensor_mul(lrw_b[:, :], colf[:, :],
                                                       dng_c[:, :]),
                      deps=[t_lrwf, bdng.w] + blrwb.wdeps())
        bcolf.rc(t_lrwb)
        bdng.rc(t_lrwb)
        blrwb.wc(t_lrwb)
        lrwr_b = SB(initstk, "lrwr_b", [128, NCT], BF16)
        blrwr = Buf()
        t_lrwr = p.op("scalar", lambda e: e.activation(out=lrwr_b[:, :], in_=colf[:, :],
                                                       func=AF.Identity), deps=[t_lrwf])
        bcolf.rc(t_lrwr)
        blrwr.wc(t_lrwr)
        lrf_mm = None
        for ck in range(NCT):
            deps = [bdnbb.w, t_lrwr]
            if ck == 0:
                deps += bPS.wdeps()
            lrf_mm = p.op("tensor", (lambda ck: lambda e: e.matmul(
                PS[0:1, 0:1], dnb_cb[:, ck:ck + 1], lrwr_b[:, ck:ck + 1],
                start=(ck == 0), stop=(ck == NCT - 1)))(ck), deps=deps)
        bPS.wc(lrf_mm)
        t_lrb = load_row("LRB", 1)
        lrb_row = SB(initstk, "lrb_row", [1, 2], F32)
        lrb_rowb = SB(initstk, "lrb_rowb", [1, 1], BF16)
        blrr = Buf()
        t1 = p.op("vector", lambda e: e.tensor_add(lrb_row[0:1, 0:1], rowbuf[0:1, 0:1],
                                                   PS[0:1, 0:1]),
                  deps=[t_lrb, lrf_mm] + blrr.wdeps())
        bPS.rc(t1)
        brow.rc(t1)
        blrr.wc(t1)
        t2 = p.op("scalar", lambda e: e.activation(out=lrb_rowb[0:1, 0:1],
                                                   in_=lrb_row[0:1, 0:1], func=AF.Identity),
                  deps=[t1])
        blrr.rc(t2)
        mmb = p.op("tensor", lambda e: e.matmul(PS[:, 0:1], onesb[0:1, :],
                                                lrb_rowb[0:1, 0:1], start=True, stop=True),
                   deps=[t2, ld["ONESB"]] + bPS.wdeps())
        bPS.wc(mmb)
        t_lrbb = p.op("scalar", lambda e: e.activation(out=lrb_bc[:, :], in_=PS[:, 0:1],
                                                       func=AF.Identity),
                      deps=[mmb] + blrb.wdeps())
        bPS.rc(t_lrbb)
        blrb.wc(t_lrbb)

        t_ob = load_row("OUTB", C)
        t_obb = p.op("scalar", lambda e: e.activation(out=outb_rb[0:1, :],
                                                      in_=rowbuf[0:1, 0:C], func=AF.Identity),
                     deps=[t_ob])
        brow.rc(t_obb)
        boutb.wc(t_obb)
        t_b2 = load_row("B2", C)
        t_b2b = p.op("scalar", lambda e: e.activation(out=b2_rb[0:1, :],
                                                      in_=rowbuf[0:1, 0:C], func=AF.Identity),
                     deps=[t_b2])
        brow.rc(t_b2b)
        bb2.wc(t_b2b)

        b1f = [None, None]
        for ck in range(NCT):
            wst, bwst = wstages[ck % 2]
            dl = p.dma("scalar", "wl", (lambda ck, wst: lambda e: e.dma_start(
                out=wst[:, 0:C], in_=W1[128 * ck:128 * (ck + 1), :]))(ck, wst),
                deps=bwst.wdeps())
            bwst.wc(dl)
            tr = p.op("scalar", (lambda wst: lambda e: e.activation(
                out=wraw[:, 0:C], in_=wst[:, 0:C], func=AF.Identity))(wst),
                      deps=[dl] + bwraw.wdeps())
            bwraw.wc(tr)
            bwst.rc(tr)
            for h in range(2):
                deps = [tr, bmnbb.w]
                if ck == 0:
                    deps += (bPA.wdeps() if h == 0 else bPB.wdeps())
                fm = p.op("tensor", (lambda h, ck: lambda e: e.matmul(
                    (PA if h == 0 else PB)[0:1, 0:512], mnb_cb[:, ck:ck + 1],
                    wraw[:, 512 * h:512 * (h + 1)], start=(ck == 0),
                    stop=(ck == NCT - 1)))(h, ck), deps=deps)
                bwraw.rc(fm)
                if ck == NCT - 1:
                    (bPA if h == 0 else bPB).wc(fm)
                    b1f[h] = fm
        t_b1 = load_row("B1", C)
        t_bf0 = p.op("vector", lambda e: e.tensor_add(
            brow2t[0:1, 0:512], rowbuf[0:1, 0:512], PA[0:1, 0:512]),
            deps=[t_b1, b1f[0]] + bbrow2.wdeps())
        bPA.rc(t_bf0)
        brow.rc(t_bf0)
        t_bf1 = p.op("vector", lambda e: e.tensor_add(
            brow2t[0:1, 512:1024], rowbuf[0:1, 512:1024], PB[0:1, 0:512]),
            deps=[t_b1, b1f[1], t_bf0])
        bPB.rc(t_bf1)
        brow.rc(t_bf1)
        bbrow2.wc(t_bf1)
        t_b1c = row_to_cols(brow2t, t_bf1, C, b1_c, bb1c)
        bbrow2.rc(t_b1c)

        def load_ffn(dst, bdst, srcd, half=False):
            last = None
            for ck in range(NCT):
                wst, bwst = wstages[ck % 2]
                dl = p.dma("scalar", "wl", (lambda ck, srcd, wst: lambda e: e.dma_start(
                    out=wst[:, 0:C],
                    in_=srcd[128 * ck:128 * (ck + 1), :]))(ck, srcd, wst),
                    deps=bwst.wdeps())
                bwst.wc(dl)
                if half:
                    tc = p.op("scalar", (lambda ck, dst, wst: lambda e: e.activation(
                        out=dst[:, C * ck:C * (ck + 1)], in_=wst[:, 0:C],
                        func=AF.Identity, scale=ccols[:, 2:3]))(ck, dst, wst),
                        deps=[dl, cc])
                else:
                    tc = p.op("scalar", (lambda ck, dst, wst: lambda e: e.activation(
                        out=dst[:, C * ck:C * (ck + 1)], in_=wst[:, 0:C],
                        func=AF.Identity))(ck, dst, wst),
                        deps=[dl])
                bwst.rc(tc)
                last = tc
            bdst.w = last
            return last

        t_wr = load_ffn(wr_s, bwr, WRT)
        t_wk = load_ffn(wk_s, bwk, WKT)
        t_wv = load_ffn(wv_s, bwv, WVT, half=True)

        bar_p0 = p.hard_barrier()
        initstk.close()
        a0stk = contextlib.ExitStack()
        dscr = SB(a0stk, "dscr", [128, 1024], BF16)
        bdscr = Buf()
        krelu = SB(a0stk, "krelu", [128, 1024], F32)
        bkrelu = Buf()
        r_sb = SB(a0stk, "r_sb", [128, 2 * C], F32)
        br_sb = [Buf(), Buf()]

        x0b = SB(a0stk, "x0b", [128, 2 * C], F32)
        bx0 = [Buf(), Buf()]
        x0add = SB(a0stk, "x0add", [128, 2 * C], F32)
        bx0add = [Buf(), Buf()]
        lnb = SB(a0stk, "lnb", [128, 4 * C], BF16)
        blnb = [Buf(), Buf(), Buf(), Buf()]
        lnT = SB(a0stk, "lnT", [128, NCT * 513], BF16)
        blnT = Buf()
        xrT = SB(a0stk, "xrT", [128, 2 * NCT * 512], BF16)
        xkT = SB(a0stk, "xkT", [128, 2 * NCT * 512], BF16)
        bxr = [Buf(), Buf()]
        bxk = [Buf(), Buf()]
        kactT = SB(a0stk, "kactT", [128, NCT * 512], BF16)
        bkact = Buf()
        x1b = SB(a0stk, "x1b", [128, 2 * C], F32)
        bx1 = [Buf(), Buf()]

        def transpose8(src_fn, dst_fn, deps_src, bdst, dst_view=None):
            trs = []
            wd = bPT.wdeps()
            for k in range(8):
                deps = list(deps_src) + [ld["IDENT_B"]]
                if k == 0:
                    deps += wd
                trs.append(p.op("tensor", (lambda k: lambda e: e.transpose(
                    PT[:, 128 * k:128 * (k + 1)], src_fn(k), ident_b[:, :]))(k),
                    deps=deps))
            bPT.wc(trs[-1])
            wd2 = bdst.wdeps()
            if dst_view is not None:
                evs_last = p.op("vector", (lambda dst_view: lambda e: e.tensor_copy(
                    dst_view, PT.rearrange("p (k q) -> p k q", q=128)))(dst_view),
                    deps=[trs[-1]] + wd2)
            else:
                evs_last = None
                for k in range(8):
                    deps = [trs[-1]]
                    if k == 0:
                        deps += wd2
                    evs_last = p.op("vector", (lambda k: lambda e: e.tensor_copy(
                        dst_fn(k), PT[:, 128 * k:128 * (k + 1)]))(k), deps=deps)
            bPT.rc(evs_last)
            bdst.wc(evs_last)
            return evs_last

        def ln_stats(src_ap, deps_x, slot, second_ln=False):
            s0 = 8 * slot
            bs = bstt[slot]
            t_sum = p.op("vector", (lambda src_ap, s0: lambda e: e.reduce_sum(
                stt[:, s0:s0 + 1], src_ap, axis=AX.X))(src_ap, s0),
                deps=deps_x + bs.wdeps())
            t_sq = p.op("scalar", (lambda src_ap, s0: lambda e: e.activation(
                out=sqd[:, 0:C], in_=src_ap, func=AF.Square,
                accum_out=stt[:, s0 + 1:s0 + 2]))(src_ap, s0),
                deps=deps_x + bsqd[0].wdeps() + [t_sum])
            bsqd[0].wc(t_sq)
            t_mean = p.op("vector", (lambda s0: lambda e: e.tensor_scalar(
                out=stt[:, s0 + 2:s0 + 3], in0=stt[:, s0:s0 + 1], scalar1=1.0 / C,
                scalar2=None, op0=ALU.mult))(s0), deps=[t_sum])
            t_mm2 = p.op("vector", (lambda s0: lambda e: e.tensor_mul(
                stt[:, s0 + 3:s0 + 4], stt[:, s0 + 2:s0 + 3], stt[:, s0 + 2:s0 + 3]))(s0),
                deps=[t_mean])
            t_var = p.op("vector", (lambda s0: lambda e: e.tensor_scalar(
                out=stt[:, s0 + 4:s0 + 5], in0=stt[:, s0 + 1:s0 + 2], scalar1=1.0 / C,
                scalar2=stt[:, s0 + 3:s0 + 4], op0=ALU.mult, op1=ALU.subtract))(s0),
                deps=[t_sq, t_mm2])
            if not second_ln:
                t_ln = p.op("scalar", (lambda s0: lambda e: e.activation(
                    out=stt[:, s0 + 5:s0 + 6], in_=stt[:, s0 + 4:s0 + 5], func=AF.Ln,
                    bias=ccols[:, 0:1]))(s0), deps=[t_var, cc])
            else:
                t_ln = p.op("scalar", (lambda s0: lambda e: e.activation(
                    out=stt[:, s0 + 5:s0 + 6], in_=stt[:, s0 + 4:s0 + 5], func=AF.Ln,
                    scale=ccols[:, 4:5], bias=ccols[:, 5:6]))(s0), deps=[t_var, cc])
            t_sc = p.op("scalar", (lambda s0: lambda e: e.activation(
                out=stt[:, s0 + 6:s0 + 7], in_=stt[:, s0 + 5:s0 + 6], func=AF.Exp,
                scale=ccols[:, 1:2]))(s0), deps=[t_ln])
            t_nb = p.op("vector", (lambda s0: lambda e: e.tensor_scalar(
                out=stt[:, s0 + 7:s0 + 8], in0=stt[:, s0 + 2:s0 + 3],
                scalar1=stt[:, s0 + 6:s0 + 7], scalar2=-1.0, op0=ALU.mult,
                op1=ALU.mult))(s0), deps=[t_sc, t_mean])
            bs.wc(t_nb)
            return [t_sc, t_nb], stt[:, s0 + 6:s0 + 7], stt[:, s0 + 7:s0 + 8]

        def stats_norm(src_ap, deps_x, j, lnb_t, blnb_t, sl, second_ln=False):
            stoks, sc_ap, nb_ap = ln_stats(src_ap, deps_x, j % 4, second_ln)
            t_n = p.op("scalar", (lambda src_ap, sc_ap, nb_ap, sl: lambda e:
                       e.activation(out=lnb_t[:, C * sl:C * (sl + 1)], in_=src_ap,
                                    func=AF.Identity, scale=sc_ap, bias=nb_ap))(
                           src_ap, sc_ap, nb_ap, sl),
                       deps=stoks + deps_x + blnb_t[sl].wdeps())
            blnb_t[sl].wc(t_n)
            bstt[j % 4].rc(t_n)
            return t_n

        def lnb_to_lnT(j, lnT_t, blnT_t, lnb_t, blnb_t, sl, t_n):
            transpose8(
                (lambda sl: lambda k: lnb_t[:, C * sl + 128 * k:C * sl + 128 * (k + 1)])(sl),
                None, [t_n], blnT_t,
                dst_view=lnT_t.rearrange("p (k t) -> p k t", t=513)[:, :, 1 + 128 * j:1 + 128 * (j + 1)])
            blnb_t[sl].rc(blnT_t.w)

        def emit_lnT(src_ap, deps_x, j, lnT_t, blnT_t, lnb_t, blnb_t, second_ln=False):
            sl = j % 2
            t_n = stats_norm(src_ap, deps_x, j, lnb_t, blnb_t, sl, second_ln)
            lnb_to_lnT(j, lnT_t, blnT_t, lnb_t, blnb_t, sl, t_n)
            return t_n

        def emit_ffn_mix(lnT_t, blnT_t, xrT_t, bxr_t, xkT_t, bxk_t,
                         dscr_t, bdscr_t):
            mix_last = None
            xr_last = None
            for k in range(NCT):
                cur = lnT_t[:, 513 * k + 1:513 * k + 513]
                shf = lnT_t[:, 513 * k:513 * k + 512]
                t_d = p.op("vector", (lambda cur, shf, k: lambda e: e.tensor_sub(
                    dscr_t[:, 512 * (k % 2):512 * (k % 2 + 1)], cur, shf))(cur, shf, k),
                    deps=blnT_t.r() + bdscr_t.wdeps())
                bdscr_t.wc(t_d)
                blnT_t.rc(t_d)
                t_xr = p.op("vector", (lambda k, shf: lambda e: e.scalar_tensor_tensor(
                    out=xrT_t[:, 512 * k:512 * (k + 1)],
                    in0=dscr_t[:, 512 * (k % 2):512 * (k % 2 + 1)],
                    scalar=mur_c[:, k:k + 1], in1=shf, op0=ALU.mult, op1=ALU.add))(k, shf),
                    deps=[t_d, bmur.w] + (bxr_t.wdeps() if k == 0 else []))
                t_xk = p.op("vector", (lambda k, shf: lambda e: e.scalar_tensor_tensor(
                    out=xkT_t[:, 512 * k:512 * (k + 1)],
                    in0=dscr_t[:, 512 * (k % 2):512 * (k % 2 + 1)],
                    scalar=muk_c[:, k:k + 1], in1=shf, op0=ALU.mult, op1=ALU.add))(k, shf),
                    deps=[t_d, bmuk.w] + (bxk_t.wdeps() if k == 0 else []))
                bdscr_t.rc(t_xr)
                bdscr_t.rc(t_xk)
                blnT_t.rc(t_xr)
                blnT_t.rc(t_xk)
                xr_last = t_xr
                mix_last = t_xk
            bxr_t.wc(xr_last)
            bxk_t.wc(mix_last)
            bc_last = None
            for k in range(NCT):
                bc_last = p.op("vector", (lambda k: lambda e: e.tensor_copy(
                    lnT_t[:, 513 * k:513 * k + 1],
                    lnT_t[:, 513 * k + 512:513 * k + 513]))(k),
                    deps=[mix_last, xr_last] + blnT_t.r())
            blnT_t.wc(bc_last)

        def emit_ffn_mms(xrT_t, bxr_t, xkT_t, bxk_t, kactT_t, bkact_t,
                         wr_t, wk_t, wv_t, bwr_t, bwk_t, bwv_t,
                         out_action, krelu_t, bkrelu_t, rsb_t, brsb_t,
                         pre_dk=None, post_dk=None, pre_jloop=None):
            ka_last = None
            for dk in range(NCT):
                if pre_dk is not None and dk % 2 == 0:
                    pre_dk(dk)
                bank, bankb = (PW, bPW) if dk % 2 == 0 else (PX, bPX)
                mms = []
                for ck in range(NCT):
                    deps = [bxk_t.w, bwk_t.w]
                    if ck == 0:
                        deps += bankb.wdeps()
                    mms.append(p.op("tensor", (lambda bank, dk, ck: lambda e: e.matmul(
                        bank[:, :], wk_t[:, C * ck + 128 * dk:C * ck + 128 * (dk + 1)],
                        xkT_t[:, 512 * ck:512 * (ck + 1)],
                        start=(ck == 0), stop=(ck == NCT - 1)))(bank, dk, ck), deps=deps))
                bankb.wc(mms[-1])
                bxk_t.rc(mms[-1])
                t_r = p.op("scalar", (lambda bank, dk: lambda e: e.activation(
                    out=krelu_t[:, 512 * (dk % 2):512 * (dk % 2 + 1)], in_=bank[:, :],
                    func=AF.Relu))(bank, dk),
                    deps=[mms[-1]] + bkrelu_t.wdeps())
                bankb.rc(t_r)
                bkrelu_t.wc(t_r)
                t_s = p.op("scalar", (lambda dk: lambda e: e.activation(
                    out=kactT_t[:, 512 * dk:512 * (dk + 1)],
                    in_=krelu_t[:, 512 * (dk % 2):512 * (dk % 2 + 1)], func=AF.Square))(dk),
                    deps=[t_r] + (bkact_t.wdeps() if dk == 0 else []))
                bkrelu_t.rc(t_s)
                ka_last = t_s
                if post_dk is not None and dk % 2 == 1:
                    post_dk(dk)
            bkact_t.wc(ka_last)
            if pre_jloop is not None:
                pre_jloop()
            for j in range(4):
                rmms = []
                for h in range(2):
                    for ck in range(NCT):
                        deps = [bxr_t.w, bwr_t.w]
                        if h == 0 and ck == 0:
                            deps += bPA.wdeps()
                        rmms.append(p.op("tensor", (lambda j, h, ck: lambda e: e.matmul(
                            PA[:, 512 * h:512 * (h + 1)],
                            xrT_t[:, 512 * ck + 128 * j:512 * ck + 128 * (j + 1)],
                            wr_t[:, C * ck + 512 * h:C * ck + 512 * (h + 1)],
                            start=(ck == 0), stop=(ck == NCT - 1)))(j, h, ck),
                            deps=deps))
                bPA.wc(rmms[-1])
                bxr_t.rc(rmms[-1])
                rsl = j % 2
                sig = p.op("scalar", (lambda rsl: lambda e: e.activation(
                    out=rsb_t[:, C * rsl:C * (rsl + 1)], in_=PA[:, :],
                    func=AF.Sigmoid))(rsl),
                    deps=[rmms[-1]] + brsb_t[rsl].wdeps())
                bPA.rc(sig)
                brsb_t[rsl].wc(sig)
                vmms = []
                for h in range(2):
                    for dk in range(NCT):
                        deps = [bkact_t.w, bwv_t.w]
                        if h == 0 and dk == 0:
                            deps += bPB.wdeps()
                        vmms.append(p.op("tensor", (lambda j, h, dk: lambda e: e.matmul(
                            PB[:, 512 * h:512 * (h + 1)],
                            kactT_t[:, 512 * dk + 128 * j:512 * dk + 128 * (j + 1)],
                            wv_t[:, C * dk + 512 * h:C * dk + 512 * (h + 1)],
                            start=(dk == 0), stop=(dk == NCT - 1)))(j, h, dk),
                            deps=deps))
                bPB.wc(vmms[-1])
                bkact_t.rc(vmms[-1])
                out_action(j, rsl, vmms[-1], sig,
                           rsb_t[:, C * rsl:C * (rsl + 1)], brsb_t[rsl])

        tz = None
        for k in range(NCT):
            tz = p.op("vector", (lambda k: lambda e: e.memset(
                lnT[:, 513 * k:513 * k + 1], 0.0))(k),
                deps=(blnT.wdeps() + bar_p0) if k == 0 else [])
        blnT.wc(tz)

        SLOT = NCT * 512

        def a0_mm_stage(m, pre_dk=None, post_dk=None, pre_jloop=None):
            ms = m % 2
            xr_v = xrT[:, SLOT * ms:SLOT * (ms + 1)]
            xk_v = xkT[:, SLOT * ms:SLOT * (ms + 1)]

            def a0_out(j, rsl, vtok, sig_tok, rsb_ap, brsb_slot, m=m):
                xsl = j % 2
                t_xl = p.dma("sync", "x0a", (lambda m, j, xsl: lambda e: e.dma_start(
                    out=x0add[:, C * xsl:C * (xsl + 1)],
                    in_=X[MAC * m + MIC * j:MAC * m + MIC * (j + 1), :]))(m, j, xsl),
                    deps=bx0add[xsl].wdeps())
                bx0add[xsl].wc(t_xl)
                t_t3 = p.op("vector", (lambda rsb_ap, xsl: lambda e: e.scalar_tensor_tensor(
                    out=x1b[:, C * xsl:C * (xsl + 1)], in0=PB[:, :], scalar=1.0,
                    in1=rsb_ap, op0=ALU.mult, op1=ALU.mult))(rsb_ap, xsl),
                    deps=[vtok, sig_tok] + bx1[xsl].wdeps())
                bPB.rc(t_t3)
                brsb_slot.rc(t_t3)
                bx1[xsl].wc(t_t3)
                t_x1 = p.op("vector", (lambda xsl: lambda e: e.tensor_add(
                    x1b[:, C * xsl:C * (xsl + 1)], x1b[:, C * xsl:C * (xsl + 1)],
                    x0add[:, C * xsl:C * (xsl + 1)]))(xsl),
                    deps=[t_t3, t_xl] + bx1[xsl].wdeps())
                bx0add[xsl].rc(t_x1)
                bx1[xsl].wc(t_x1)
                t_sp = p.dma("sync", "x1o", (lambda m, j, xsl: lambda e: e.dma_start(
                    out=X1S[MAC * m + MIC * j:MAC * m + MIC * (j + 1), :],
                    in_=x1b[:, C * xsl:C * (xsl + 1)]))(m, j, xsl),
                    deps=[t_x1])
                bx1[xsl].rc(t_sp)

            emit_ffn_mms(xr_v, bxr[ms], xk_v, bxk[ms], kactT, bkact,
                         wr_s, wk_s, wv_s, bwr, bwk, bwv, a0_out,
                         krelu, bkrelu, r_sb, br_sb,
                         pre_dk=pre_dk, post_dk=post_dk, pre_jloop=pre_jloop)

        def a0_stats_micro(m, j):
            xs = j % 2
            t_x0 = p.dma("sync", "x0i", (lambda m, j, xs: lambda e: e.dma_start(
                out=x0b[:, C * xs:C * (xs + 1)],
                in_=X[MAC * m + MIC * j:MAC * m + MIC * (j + 1), :]))(m, j, xs),
                deps=bx0[xs].wdeps() + (bar_p0 if m == 0 and j < 2 else []))
            bx0[xs].wc(t_x0)
            t_n = stats_norm(x0b[:, C * xs:C * (xs + 1)], [bx0[xs].w],
                             j, lnb, blnb, j)
            bx0[xs].rc(t_n)
            return t_n

        for m in range(NMAC):
            ms = m % 2
            if m == 0:
                for j in range(4):
                    t_n = a0_stats_micro(0, j)
                    lnb_to_lnT(j, lnT, blnT, lnb, blnb, j, t_n)
                emit_ffn_mix(lnT, blnT, xrT[:, 0:SLOT], bxr[0],
                             xkT[:, 0:SLOT], bxk[0], dscr, bdscr)
                continue
            ntoks = {}

            def pre_dk(dk, m=m, ntoks=ntoks):
                ntoks[dk // 2] = a0_stats_micro(m, dk // 2)

            def post_dk(dk, m=m, ntoks=ntoks):
                jj = (dk - 1) // 2
                lnb_to_lnT(jj, lnT, blnT, lnb, blnb, jj, ntoks[jj])

            def pre_j(m=m, ms=ms):
                emit_ffn_mix(lnT, blnT, xrT[:, SLOT * ms:SLOT * (ms + 1)], bxr[ms],
                             xkT[:, SLOT * ms:SLOT * (ms + 1)], bxk[ms], dscr, bdscr)

            a0_mm_stage(m - 1, pre_dk=pre_dk, post_dk=post_dk, pre_jloop=pre_j)
        a0_mm_stage(NMAC - 1)

        bar_a0 = p.hard_barrier()
        a0stk.close()
        fwstk.close()
        if stop_after == "a0":
            t_cp = p.dma("sync", "oo", lambda e: e.dma_start(out=OUT[:, :], in_=X1S[:, :]),
                         deps=bar_a0)
            p.op("sync", lambda e: e.nop(), deps=[t_cp])
            gstk.close()
            p.build_and_emit(root)
            return nc

        gfstk = contextlib.ExitStack()
        G_f = gfstk.enter_context(nc.sbuf_tensor("G_f", [128, NCT * C], F32, side="right"))
        bG = Buf()
        a1stk = contextlib.ExitStack()
        S_f = SB(a1stk, "S_f", [128, NCT * C], F32)
        S_b = SB(a1stk, "S_b", [128, NCT * C], BF16)
        bS = Buf()
        bSb_t = [Buf() for _ in range(NCT)]
        x1i = SB(a1stk, "x1i", [128, 2 * C], F32)
        bx1i = [Buf(), Buf()]
        a0nb = SB(a1stk, "a0nb", [128, 2 * C], BF16)
        ba0nb = [Buf(), Buf()]
        a0nT = SB(a1stk, "a0nT", [128, NCT * 512], BF16)
        ba0nT = Buf()
        kT = SB(a1stk, "kT", [128, NCT * 512], BF16)
        bkT = Buf()
        khat = SB(a1stk, "khat", [128, 4 * C], BF16)
        bkhat = Buf()
        qsp = khat
        bqsp = bkhat
        ut = SB(a1stk, "ut", [128, 4 * C], BF16)
        but = [Buf(), Buf(), Buf(), Buf()]
        wt = SB(a1stk, "wt", [128, 4 * C], BF16)
        bwt = Buf()
        b_sb = SB(a1stk, "b_sb", [128, C], BF16)
        bbsb = Buf()
        lr_c = SB(a1stk, "lr_c", [128, 4], F32)
        blr = Buf()
        nb_A = SB(a1stk, "nb_A", [128, 4 * 128], BF16)
        nb_AT = SB(a1stk, "nb_AT", [128, 4 * 128], BF16)
        nb_M = SB(a1stk, "nb_M", [128, 4 * 128], BF16)
        nb_MT = SB(a1stk, "nb_MT", [128, 4 * 128], BF16)
        nb_Q = nb_A
        bnA, bnAT, bnM, bnMT = (Buf() for _ in range(4))
        bnQ = bnA
        g_sc = SB(a1stk, "g_sc", [128, 4 * 128], F32)
        bg_sc = Buf()
        B_sb = SB(a1stk, "B_sb", [128, 4 * 128], F32)
        bB_sb = Buf()
        rr_b = SB(a1stk, "rr_b", [1, 128], BF16)
        brr = Buf()
        af_sc = SB(a1stk, "af_sc", [128, 128], F32)
        baf = Buf()

        tz1 = p.op("vector", lambda e: e.memset(S_f[:, :], 0.0), deps=bar_a0)
        bS.wc(tz1)
        tz2 = p.op("scalar", lambda e: e.activation(out=S_b[:, :], in_=S_f[:, :],
                                                    func=AF.Identity), deps=[tz1] + bar_a0)
        for bb in bSb_t:
            bb.wc(tz2)
        tz3 = p.op("vector", lambda e: e.memset(G_f[:, :], 0.0), deps=bar_a0)
        bG.wc(tz3)

        for m in range(NMAC):
            for j in range(4):
                sl = j % 2
                t_x1 = p.dma("sync", "x1i", (lambda m, j, sl: lambda e: e.dma_start(
                    out=x1i[:, C * sl:C * (sl + 1)],
                    in_=X1S[MAC * m + MIC * j:MAC * m + MIC * (j + 1), :]))(m, j, sl),
                    deps=bx1i[sl].wdeps() + (bar_a0 if m == 0 and j < 2 else []))
                bx1i[sl].wc(t_x1)
                stoks, sc_ap, nb_ap = ln_stats(x1i[:, C * sl:C * (sl + 1)],
                                               [bx1i[sl].w], j % 4, second_ln=True)
                t_n = p.op("scalar", (lambda sl, sc_ap, nb_ap: lambda e: e.activation(
                    out=a0nb[:, C * sl:C * (sl + 1)], in_=x1i[:, C * sl:C * (sl + 1)],
                    func=AF.Identity, scale=sc_ap, bias=nb_ap))(sl, sc_ap, nb_ap),
                    deps=stoks + [bx1i[sl].w] + ba0nb[sl].wdeps())
                ba0nb[sl].wc(t_n)
                bstt[j % 4].rc(t_n)
                bx1i[sl].rc(t_n)
                transpose8(
                    (lambda sl: lambda k: a0nb[:, C * sl + 128 * k:C * sl + 128 * (k + 1)])(sl),
                    (lambda j: lambda k: a0nT[:, 512 * k + 128 * j:512 * k + 128 * (j + 1)])(j),
                    [t_n], ba0nT)
                ba0nb[sl].rc(ba0nT.w)
            for which in ("q", "k"):
                off = 0 if which == "q" else C
                dstb, bdst = (qsp, bqsp) if which == "q" else (kT, bkT)
                bias_c = qb_c if which == "q" else kb_c
                bias_tok = bqb.w if which == "q" else bkb.w
                e_last = None
                for dk in range(NCT):
                    bank, bankb = (PW, bPW) if dk % 2 == 0 else (PX, bPX)
                    mms = []
                    for ck in range(NCT):
                        deps = [ba0nT.w, bwg.w]
                        if ck == 0:
                            deps += bankb.wdeps()
                        mms.append(p.op("tensor", (lambda bank, dk, ck, off: lambda e: e.matmul(
                            bank[:, :],
                            wg_s[:, 3 * C * ck + off + 128 * dk:3 * C * ck + off + 128 * (dk + 1)],
                            a0nT[:, 512 * ck:512 * (ck + 1)],
                            start=(ck == 0), stop=(ck == NCT - 1)))(bank, dk, ck, off),
                            deps=deps))
                    bankb.wc(mms[-1])
                    ba0nT.rc(mms[-1])
                    if which == "q":
                        dst_ap = qsp.bitcast(BF16)[:, 512 * dk:512 * (dk + 1)]
                    else:
                        dst_ap = kT[:, 512 * dk:512 * (dk + 1)]
                    deps_e = [mms[-1], bias_tok]
                    if dk == 0:
                        deps_e += bdst.wdeps()
                    e_last = p.op("scalar", (lambda bank, dst_ap, bias_c, dk: lambda e:
                                  e.activation(out=dst_ap, in_=bank[:, :], func=AF.Identity,
                                               bias=bias_c[:, dk:dk + 1]))(bank, dst_ap, bias_c, dk),
                                  deps=deps_e)
                    bankb.rc(e_last)
                bdst.wc(e_last)
                if which == "q":
                    t_qo = p.dma("scalar", "qo", (lambda m: lambda e: e.dma_start(
                        out=QTS.rearrange("ik p t -> p ik t")[:, :, MAC * m:MAC * (m + 1)],
                        in_=qsp.rearrange("p (ik t) -> p ik t", ik=NCT)))(m),
                        deps=bqsp.r())
                    bqsp.rc(t_qo)
            lr_last = None
            for j in range(4):
                mms = []
                for ck in range(NCT):
                    deps = [ba0nT.w, blrwb.w]
                    if ck == 0:
                        deps += bPS.wdeps()
                    mms.append(p.op("tensor", (lambda j, ck: lambda e: e.matmul(
                        PS[:, 0:1], a0nT[:, 512 * ck + 128 * j:512 * ck + 128 * (j + 1)],
                        lrw_b[:, ck:ck + 1],
                        start=(ck == 0), stop=(ck == NCT - 1)))(j, ck), deps=deps))
                bPS.wc(mms[-1])
                ba0nT.rc(mms[-1])
                lr_last = p.op("scalar", (lambda j: lambda e: e.activation(
                    out=lr_c[:, j:j + 1], in_=PS[:, 0:1], func=AF.Identity,
                    bias=lrb_bc[:, :]))(j),
                    deps=[mms[-1], blrb.w] + (blr.wdeps() if j == 0 else []))
                bPS.rc(lr_last)
            blr.wc(lr_last)
            for j in range(4 if a1_parts & 1 else 0):
                mms = []
                for ik in range(NCT):
                    deps = [bkT.w]
                    if ik == 0:
                        deps += bPS.wdeps()
                    mms.append(p.op("tensor", (lambda j, ik: lambda e: e.matmul(
                        PS[:, 0:128], kT[:, 512 * ik + 128 * j:512 * ik + 128 * (j + 1)],
                        kT[:, 512 * ik + 128 * j:512 * ik + 128 * (j + 1)],
                        start=(ik == 0), stop=(ik == NCT - 1)))(j, ik), deps=deps))
                bPS.wc(mms[-1])
                bkT.rc(mms[-1])
                s0 = 8 * (j % 4)
                t_dm = p.op("vector", lambda e: e.tensor_mul(
                    af_sc[:, :], PS[:, 0:128], ident_f[:, :]),
                    deps=[mms[-1], ld["IDENT_F"]] + baf.wdeps() + bstt[j % 4].wdeps())
                bPS.rc(t_dm)
                baf.wc(t_dm)
                t_d2 = p.op("vector", (lambda s0: lambda e: e.reduce_sum(
                    stt[:, s0:s0 + 1], af_sc[:, :], axis=AX.X))(s0), deps=[t_dm])
                baf.rc(t_d2)
                t_ln2 = p.op("scalar", (lambda s0: lambda e: e.activation(
                    out=stt[:, s0 + 1:s0 + 2], in_=stt[:, s0:s0 + 1], func=AF.Ln))(s0),
                    deps=[t_d2])
                t_rs = p.op("scalar", (lambda s0: lambda e: e.activation(
                    out=stt[:, s0 + 2:s0 + 3], in_=stt[:, s0 + 1:s0 + 2], func=AF.Exp,
                    scale=ccols[:, 1:2]))(s0), deps=[t_ln2, cc])
                bstt[j % 4].wc(t_rs)
                t_t1 = p.op("vector", (lambda j, s0: lambda e: e.tensor_scalar(
                    out=g_sc[:, 128 * j:128 * (j + 1)], in0=PS[:, 0:128],
                    scalar1=stt[:, s0 + 2:s0 + 3], scalar2=None, op0=ALU.mult))(j, s0),
                    deps=[t_rs, mms[-1]] + (bg_sc.wdeps() if j == 0 else []))
                bPS.rc(t_t1)
                bstt[j % 4].rc(t_t1)
                t_rr = p.op("tensor", (lambda s0: lambda e: e.transpose(
                    PS[0:1, 128:256], stt[:, s0 + 2:s0 + 3], ident_f[:, :]))(s0),
                    deps=[t_rs, t_t1, ld["IDENT_F"]] + bPS.wdeps())
                bPS.wc(t_rr)
                bstt[j % 4].rc(t_rr)
                t_rrb = p.op("scalar", lambda e: e.activation(
                    out=rr_b[0:1, :], in_=PS[0:1, 128:256], func=AF.Identity),
                    deps=[t_rr] + brr.wdeps())
                bPS.rc(t_rrb)
                brr.wc(t_rrb)
                t_bc = p.op("tensor", lambda e: e.matmul(
                    PS[:, 256:384], onesb[0:1, :], rr_b[0:1, :], start=True, stop=True),
                    deps=[t_rrb, ld["ONESB"]] + bPS.wdeps())
                bPS.wc(t_bc)
                brr.rc(t_bc)
                t_Bs = p.op("vector", (lambda j: lambda e: e.tensor_copy(
                    B_sb[:, 128 * j:128 * (j + 1)], PS[:, 256:384]))(j),
                    deps=[t_bc] + (bB_sb.wdeps() if j == 0 else []))
                bPS.rc(t_Bs)
                t_l2 = p.op("vector", (lambda j: lambda e: e.tensor_mul(
                    g_sc[:, 128 * j:128 * (j + 1)], g_sc[:, 128 * j:128 * (j + 1)],
                    B_sb[:, 128 * j:128 * (j + 1)]))(j), deps=[t_t1, t_Bs])
                t_l3 = p.op("vector", (lambda j: lambda e: e.tensor_mul(
                    g_sc[:, 128 * j:128 * (j + 1)], g_sc[:, 128 * j:128 * (j + 1)],
                    lmask[:, :]))(j), deps=[t_l2, ld["LMASK"]])
                t_A = p.op("vector", (lambda j: lambda e: e.tensor_tensor(
                    nb_A[:, 128 * j:128 * (j + 1)], g_sc[:, 128 * j:128 * (j + 1)],
                    ident_f[:, :], op=ALU.add))(j),
                    deps=[t_l3, ld["IDENT_F"]] + (bnA.wdeps() if j == 0 else []))
                t_M0 = p.op("vector", (lambda j: lambda e: e.tensor_tensor(
                    nb_M[:, 128 * j:128 * (j + 1)], g_sc[:, 128 * j:128 * (j + 1)],
                    ident_f[:, :], op=ALU.subtract))(j),
                    deps=[t_l3, ld["IDENT_F"]] + (bnM.wdeps() if j == 0 else []))
                if j == 3:
                    bg_sc.wc(t_l3)
                    bB_sb.wc(t_Bs)
                tt1 = p.op("tensor", (lambda j: lambda e: e.transpose(
                    PT[:, 0:128], nb_A[:, 128 * j:128 * (j + 1)], ident_b[:, :]))(j),
                    deps=[t_A, ld["IDENT_B"]] + bPT.wdeps())
                tt2 = p.op("tensor", (lambda j: lambda e: e.transpose(
                    PT[:, 128:256], nb_M[:, 128 * j:128 * (j + 1)], ident_b[:, :]))(j),
                    deps=[t_M0, ld["IDENT_B"]])
                bPT.wc(tt2)
                te1 = p.op("vector", (lambda j: lambda e: e.tensor_copy(
                    nb_AT[:, 128 * j:128 * (j + 1)], PT[:, 0:128]))(j),
                    deps=[tt2] + (bnAT.wdeps() if j == 0 else []))
                te2 = p.op("vector", (lambda j: lambda e: e.tensor_copy(
                    nb_MT[:, 128 * j:128 * (j + 1)], PT[:, 128:256]))(j),
                    deps=[tt2] + (bnMT.wdeps() if j == 0 else []))
                bPT.rc(te1)
                bPT.rc(te2)
                if j == 3:
                    bnA.wc(t_A)
                    bnM.wc(t_M0)
                    bnAT.wc(te1)
                    bnMT.wc(te2)
            for it in range(6 if a1_parts & 1 else 0):
                pms = []
                for j in range(4):
                    deps = [bnAT.w, bnM.w]
                    if j == 0:
                        deps += bPW.wdeps()
                    pms.append(p.op("tensor", (lambda j: lambda e: e.matmul(
                        PW[:, 128 * j:128 * (j + 1)], nb_AT[:, 128 * j:128 * (j + 1)],
                        nb_M[:, 128 * j:128 * (j + 1)], start=True, stop=True))(j),
                        deps=deps))
                bPW.wc(pms[-1])
                bnAT.rc(pms[-1])
                bnM.rc(pms[-1])
                t_Q = p.op("vector", lambda e: e.tensor_tensor(
                    nb_Q[:, 0:512], PW[:, 0:512], twoi4[:, :], op=ALU.add),
                    deps=[pms[-1], btwoi4.w] + bnQ.wdeps())
                bPW.rc(t_Q)
                bnQ.wc(t_Q)
                mms2 = []
                for j in range(4):
                    deps = [bnMT.w, bnQ.w]
                    if j == 0:
                        deps += bPX.wdeps()
                    mms2.append(p.op("tensor", (lambda j: lambda e: e.matmul(
                        PX[:, 128 * j:128 * (j + 1)], nb_MT[:, 128 * j:128 * (j + 1)],
                        nb_Q[:, 128 * j:128 * (j + 1)], start=True, stop=True))(j),
                        deps=deps))
                bPX.wc(mms2[-1])
                bnMT.rc(mms2[-1])
                bnQ.rc(mms2[-1])
                mms3 = []
                for j in range(4):
                    deps = [bnQ.w, bnMT.w]
                    if j == 0:
                        deps += bPW.wdeps()
                    mms3.append(p.op("tensor", (lambda j: lambda e: e.matmul(
                        PW[:, 128 * j:128 * (j + 1)], nb_Q[:, 128 * j:128 * (j + 1)],
                        nb_MT[:, 128 * j:128 * (j + 1)], start=True, stop=True))(j),
                        deps=deps))
                bPW.wc(mms3[-1])
                bnQ.rc(mms3[-1])
                bnMT.rc(mms3[-1])
                t_M = p.op("vector", lambda e: e.tensor_copy(nb_M[:, 0:512], PX[:, 0:512]),
                           deps=[mms2[-1]] + bnM.wdeps())
                bPX.rc(t_M)
                bnM.wc(t_M)
                t_MT = p.op("vector", lambda e: e.tensor_copy(nb_MT[:, 0:512], PW[:, 0:512]),
                            deps=[mms3[-1]] + bnMT.wdeps())
                bPW.rc(t_MT)
                bnMT.wc(t_MT)
            for j in range(4 if a1_parts & 1 else 0):
                last_sc = None
                for ik in range(NCT):
                    deps = [bkT.w, bB_sb.w] + (bkT.rs[-1:] if bkT.rs else [])
                    last_sc = p.op("vector", (lambda j, ik: lambda e: e.tensor_mul(
                        kT[:, 512 * ik + 128 * j:512 * ik + 128 * (j + 1)],
                        kT[:, 512 * ik + 128 * j:512 * ik + 128 * (j + 1)],
                        B_sb[:, 128 * j:128 * (j + 1)]))(j, ik), deps=deps)
                bkT.wc(last_sc)
                bB_sb.rc(last_sc)
                transpose8(
                    (lambda j: lambda ik: kT[:, 512 * ik + 128 * j:512 * ik + 128 * (j + 1)])(j),
                    None, [last_sc], bkhat,
                    dst_view=khat.rearrange("p (j2 ik q) -> p j2 ik q", j2=4, q=128)[:, j, :, :])
                bkT.rc(bkhat.w)
            inc_ks = bool(a1_parts & 8)
            inc_x = bool(a1_parts & 16)
            inc_u = bool(a1_parts & 32)
            inc_sf = bool(a1_parts & 64)

            def rec_vks(j, close_group):
                bank, bankb = (PA, bPA) if j % 2 == 0 else (PB, bPB)
                mms = []
                for h in range(2):
                    for ck in range(NCT):
                        deps = [ba0nT.w, bwg.w]
                        if h == 0 and ck == 0:
                            deps += bankb.wdeps()
                        mms.append(p.op("tensor", (lambda bank, j, h, ck: lambda e: e.matmul(
                            bank[:, 512 * h:512 * (h + 1)],
                            a0nT[:, 512 * ck + 128 * j:512 * ck + 128 * (j + 1)],
                            wg_s[:, 3 * C * ck + 2 * C + 512 * h:3 * C * ck + 2 * C + 512 * (h + 1)],
                            start=(ck == 0), stop=False))(bank, j, h, ck), deps=deps))
                ba0nT.rc(mms[-1])
                for h in range(2):
                    mms.append(p.op("tensor", (lambda bank, h, st2: lambda e: e.matmul(
                        bank[:, 512 * h:512 * (h + 1)], onesb[0:1, :],
                        browv_b[0:1, 512 * h:512 * (h + 1)], start=False,
                        stop=st2))(bank, h, (not inc_ks) and close_group),
                        deps=[bbrowv.w, ld["ONESB"]]))
                if inc_ks:
                    for h in range(2):
                        for ik in range(NCT):
                            is_stop = close_group and h == 1 and ik == NCT - 1 and False
                            is_stop = close_group and ik == NCT - 1
                            deps = [bkT.w, bSb_t[ik].w]
                            mms.append(p.op("tensor", (lambda bank, j, h, ik, is_stop: lambda e: e.matmul(
                                bank[:, 512 * h:512 * (h + 1)],
                                kT[:, 512 * ik + 128 * j:512 * ik + 128 * (j + 1)],
                                S_b[:, C * ik + 512 * h:C * ik + 512 * (h + 1)],
                                start=False, stop=is_stop))(bank, j, h, ik, is_stop),
                                deps=deps))
                            bSb_t[ik].rc(mms[-1])
                    bkT.rc(mms[-1])
                bankb.wc(mms[-1])

            def rec_corr(j):
                bank, bankb = (PA, bPA) if j % 2 == 0 else (PB, bPB)
                t_xu = None
                for jj in range(j):
                    xms = []
                    for ik in range(NCT):
                        deps = [bkT.w]
                        if ik == 0:
                            deps += bPS.wdeps()
                        xms.append(p.op("tensor", (lambda j, jj, ik: lambda e: e.matmul(
                            PS[:, 0:128],
                            kT[:, 512 * ik + 128 * jj:512 * ik + 128 * (jj + 1)],
                            kT[:, 512 * ik + 128 * j:512 * ik + 128 * (j + 1)],
                            start=(ik == 0), stop=(ik == NCT - 1)))(j, jj, ik), deps=deps))
                    bPS.wc(xms[-1])
                    bkT.rc(xms[-1])
                    t_xt = p.op("scalar", lambda e: e.activation(
                        out=af_sc.bitcast(BF16)[:, 0:128], in_=PS[:, 0:128],
                        func=AF.Identity), deps=[xms[-1]] + baf.wdeps())
                    bPS.rc(t_xt)
                    baf.wc(t_xt)
                    for h in range(2):
                        t_xu = p.op("tensor", (lambda h, jj, j: lambda e: e.matmul(
                            bank[:, 512 * h:512 * (h + 1)], af_sc.bitcast(BF16)[:, 0:128],
                            ut[:, C * jj + 512 * h:C * jj + 512 * (h + 1)],
                            start=False, stop=(jj == j - 1)))(h, jj, j),
                            deps=[t_xt, but[jj].w])
                    baf.rc(t_xu)
                    but[jj].rc(t_xu)
                if t_xu is not None:
                    bankb.wc(t_xu)

            def rec_bu(j):
                bank, bankb = (PA, bPA) if j % 2 == 0 else (PB, bPB)
                t_b2e = p.op("scalar", (lambda bank: lambda e: e.activation(
                    out=b_sb[:, 0:1024], in_=bank[:, 0:1024], func=AF.Identity))(bank),
                    deps=[bankb.w] + bbsb.wdeps())
                bankb.rc(t_b2e)
                bbsb.wc(t_b2e)
                if not inc_u:
                    return
                ums = []
                for h in range(2):
                    ums.append(p.op("tensor", (lambda bank, j, h: lambda e: e.matmul(
                        bank[:, 512 * h:512 * (h + 1)], nb_MT[:, 128 * j:128 * (j + 1)],
                        b_sb[:, 512 * h:512 * (h + 1)], start=True, stop=True))(bank, j, h),
                        deps=[bnMT.w, t_b2e] + (bankb.wdeps() if h == 0 else [])))
                bankb.wc(ums[-1])
                bnMT.rc(ums[-1])
                bbsb.rc(ums[-1])
                t_u2 = p.op("scalar", (lambda bank, j: lambda e: e.activation(
                    out=ut[:, C * j:C * (j + 1)], in_=bank[:, 0:1024],
                    func=AF.Identity))(bank, j),
                    deps=[ums[-1]] + but[j].wdeps())
                but[j].wc(t_u2)
                t_w2 = p.op("vector", (lambda j: lambda e: e.tensor_scalar(
                    out=wt[:, C * j:C * (j + 1)], in0=ut[:, C * j:C * (j + 1)],
                    scalar1=lr_c[:, j:j + 1], scalar2=None, op0=ALU.mult))(j),
                    deps=[t_u2, blr.w] + (bwt.wdeps() if j == 0 else []))
                but[j].rc(t_w2)
                blr.rc(t_w2)
                if j == 3:
                    bwt.wc(t_w2)
                bankb.rc(t_w2)

            if a1_parts & 2:
                for j in range(4):
                    close = (j == 0) or not inc_x
                    rec_vks(j, close_group=close)
                    if j > 0 and inc_x:
                        rec_corr(j)
                    rec_bu(j)
                if inc_sf:
                    for ik in range(NCT):
                        bank2, bank2b = (PA, bPA) if ik % 2 == 0 else (PB, bPB)
                        fms = []
                        for jj in range(4):
                            for h in range(2):
                                deps = [bkhat.w, but[jj].w]
                                if jj == 0 and h == 0:
                                    deps += bank2b.wdeps()
                                fms.append(p.op("tensor",
                                    (lambda bank2, ik, jj, h: lambda e: e.matmul(
                                        bank2[:, 512 * h:512 * (h + 1)],
                                        khat[:, C * jj + 128 * ik:C * jj + 128 * (ik + 1)],
                                        ut[:, C * jj + 512 * h:C * jj + 512 * (h + 1)],
                                        start=(jj == 0), stop=(jj == 3)))(bank2, ik, jj, h),
                                    deps=deps))
                        bank2b.wc(fms[-1])
                        bkhat.rc(fms[-1])
                        for jj in range(4):
                            but[jj].rc(fms[-1])
                        t_sa = p.op("vector", (lambda bank2, ik: lambda e: e.tensor_add(
                            S_f[:, C * ik:C * (ik + 1)], S_f[:, C * ik:C * (ik + 1)],
                            bank2[:, :]))(bank2, ik),
                            deps=[fms[-1]] + bS.wdeps())
                        bank2b.rc(t_sa)
                        bS.wc(t_sa)
                        t_sc2 = p.op("scalar", (lambda ik: lambda e: e.activation(
                            out=S_b[:, C * ik:C * (ik + 1)], in_=S_f[:, C * ik:C * (ik + 1)],
                            func=AF.Identity))(ik),
                            deps=[t_sa] + bSb_t[ik].wdeps())
                        bS.rc(t_sc2)
                        bSb_t[ik].wc(t_sc2)
            for ik in range(NCT if a1_parts & 4 else 0):
                bank2, bank2b = (PB, bPB) if ik % 2 == 0 else (PA, bPA)
                gms = []
                for jj in range(4):
                    for h in range(2):
                        deps = [bkhat.w, bwt.w]
                        if jj == 0 and h == 0:
                            deps += bank2b.wdeps()
                        gms.append(p.op("tensor", (lambda bank2, ik, jj, h: lambda e: e.matmul(
                            bank2[:, 512 * h:512 * (h + 1)],
                            khat[:, C * jj + 128 * ik:C * jj + 128 * (ik + 1)],
                            wt[:, C * jj + 512 * h:C * jj + 512 * (h + 1)],
                            start=(jj == 0), stop=(jj == 3)))(bank2, ik, jj, h),
                            deps=deps))
                bank2b.wc(gms[-1])
                bkhat.rc(gms[-1])
                bwt.rc(gms[-1])
                t_ga = p.op("vector", (lambda bank2, ik: lambda e: e.tensor_add(
                    G_f[:, C * ik:C * (ik + 1)], G_f[:, C * ik:C * (ik + 1)],
                    bank2[:, :]))(bank2, ik),
                    deps=[gms[-1]] + bG.wdeps())
                bank2b.rc(t_ga)
                bG.wc(t_ga)
        if dbg:
            t_sd = p.dma("sync", "dbg", lambda e: e.dma_start(out=SD[:, :], in_=S_f[:, :]),
                         deps=bS.r())
            bS.rc(t_sd)
            t_gd = p.dma("sync", "dbg", lambda e: e.dma_start(out=GD[:, :], in_=G_f[:, :]),
                         deps=bG.r())
            bG.rc(t_gd)
        if stop_after == "a1":
            toks = []
            for ik in range(NCT):
                toks.append(p.dma("sync", "oo", (lambda ik: lambda e: e.dma_start(
                    out=OUT[128 * ik:128 * (ik + 1), :],
                    in_=S_f[:, C * ik:C * (ik + 1)]))(ik), deps=bS.r() + bG.r()))
            for t in toks:
                bS.rc(t)
            p.op("sync", lambda e: e.nop(), deps=[toks[-1]] +
                 ([("dbg", p.dma_counts["dbg"])] if dbg else []))
            a1stk.close()
            gstk.close()
            gfstk.close()
            p.build_and_emit(root)
            return nc
        bar_a1 = p.hard_barrier()
        a1stk.close()
        gstk.close()

        bstk = contextlib.ExitStack()
        G_b = SB(bstk, "G_b", [128, NCT * C], BF16)
        bGb = Buf(readonly=True)
        t_gc = None
        for ik in range(2 * NCT):
            t_gc = p.op("scalar", (lambda ik: lambda e: e.activation(
                out=G_b[:, 512 * ik:512 * (ik + 1)], in_=G_f[:, 512 * ik:512 * (ik + 1)],
                func=AF.Identity))(ik), deps=[bG.w] + (bar_a1 if ik == 0 else []))
            bG.rc(t_gc)
        bGb.w = t_gc
        gbar = p.hard_barrier()
        gfstk.close()

        wr_s2 = SB(bstk, "wr_s2", [128, NCT * C], BF16)
        wk_s2 = SB(bstk, "wk_s2", [128, NCT * C], BF16)
        wv_s2 = SB(bstk, "wv_s2", [128, NCT * C], BF16)
        outw_s = SB(bstk, "outw_s", [128, NCT * C], BF16)
        w1_s = SB(bstk, "w1_s", [128, NCT * C], BF16)
        w2_s = SB(bstk, "w2_s", [128, NCT * C], BF16)
        bwstk = contextlib.ExitStack()
        wfstage2 = SB(bwstk, "wfstage2", [128, 2 * C], F32)
        bwf2 = [Buf(), Buf()]
        bwr2, bwk2, bwv2, boutw, bw1, bw2 = (Buf(readonly=True) for _ in range(6))

        def load_w2(dst, bdst, srcd, scale_col=None, scale_cols=None, bar=None,
                    scale_tok=None):
            last = None
            for ck in range(NCT):
                sl = ck % 2
                deps = bwf2[sl].wdeps()
                if bar is not None and ck < 2:
                    deps = deps + bar
                dl = p.dma("scalar", "wl", (lambda ck, sl, srcd: lambda e: e.dma_start(
                    out=wfstage2[:, C * sl:C * (sl + 1)],
                    in_=srcd[128 * ck:128 * (ck + 1), :]))(ck, sl, srcd),
                    deps=deps)
                bwf2[sl].wc(dl)
                if scale_cols is not None:
                    sc_ap = scale_cols[:, ck:ck + 1]
                elif scale_col is not None:
                    sc_ap = scale_col
                else:
                    sc_ap = None
                if sc_ap is None:
                    tc = p.op("scalar", (lambda ck, sl, dst: lambda e: e.activation(
                        out=dst[:, C * ck:C * (ck + 1)],
                        in_=wfstage2[:, C * sl:C * (sl + 1)], func=AF.Identity))(ck, sl, dst),
                        deps=[dl])
                else:
                    tc = p.op("scalar", (lambda ck, sl, dst, sc_ap: lambda e: e.activation(
                        out=dst[:, C * ck:C * (ck + 1)],
                        in_=wfstage2[:, C * sl:C * (sl + 1)], func=AF.Identity,
                        scale=sc_ap))(ck, sl, dst, sc_ap),
                        deps=[dl] + ([scale_tok] if scale_tok else [cc]))
                bwf2[sl].rc(tc)
                last = tc
            bdst.w = last
            return last

        load_w2(wr_s2, bwr2, WRT, bar=gbar)
        load_w2(wk_s2, bwk2, WKT)
        load_w2(wv_s2, bwv2, WVT, scale_col=ccols[:, 2:3])
        load_w2(outw_s, boutw, OUTW, scale_col=ccols[:, 3:4])
        load_w2(w1_s, bw1, W1, scale_cols=mng_c, scale_tok=bmng.w)
        load_w2(w2_s, bw2, W2)
        wbar = p.hard_barrier()
        bwstk.close()

        x1i2 = SB(bstk, "x1i2", [128, 4 * C], F32)
        bx1i2 = Buf()
        qTin = SB(bstk, "qTin", [128, NCT * 512], BF16)
        bqT = Buf()
        kactT2 = qTin
        bkact2 = bqT
        a0b = SB(bstk, "a0b", [128, 2 * C], F32)
        ba0 = [Buf(), Buf()]
        outb_f = a0b
        boutf = ba0
        yT = SB(bstk, "yT", [128, NCT * 512], BF16)
        byT = Buf()
        xrT2 = yT
        bxr2 = byT
        a1b = SB(bstk, "a1b", [128, 3 * C], F32)
        ba1 = [Buf(), Buf(), Buf(), bqT]
        t3b2 = a1b
        bt32 = ba1

        def a1s(j):
            return a1b[:, C * j:C * (j + 1)] if j < 3 else qTin.bitcast(F32)[:, 0:C]
        a1nb = SB(bstk, "a1nb", [128, 2 * C], BF16)
        _ban = Buf()
        ba1nb = [_ban, _ban]
        ln3b = a1nb
        bln3b = ba1nb
        a1nT = SB(bstk, "a1nT", [128, NCT * 512], BF16)
        ba1nT = Buf()
        xkT2 = a1nT
        bxk2 = ba1nT
        hT = SB(bstk, "hT", [128, NCT * 512], BF16)
        bhT = Buf()
        ln3T = bstk.enter_context(nc.sbuf_tensor("ln3T", [128, NCT * 513], BF16, side="right"))
        bln3T = Buf()
        dscr_B = sqd.bitcast(BF16)[:, C:2 * C]
        bdscr_B = Buf()
        krelu_B = a1nb.bitcast(F32)
        bkrelu_B = _ban
        rsb_B = hT.bitcast(F32)

        tz4 = None
        for k in range(NCT):
            tz4 = p.op("vector", (lambda k: lambda e: e.memset(
                ln3T[:, 513 * k:513 * k + 1], 0.0))(k),
                deps=(bln3T.wdeps() + wbar) if k == 0 else [])
        bln3T.wc(tz4)

        for m in range(NMAC if b_parts & 1 else 0):
            t_x1 = p.dma("sync", "x1i", (lambda m: lambda e: e.dma_start(
                out=x1i2.rearrange("p (j c) -> p j c", j=4),
                in_=X1S.rearrange("(mm j p) c -> p mm j c", p=128, j=4)[:, m, :, :]))(m),
                deps=bx1i2.wdeps() + (wbar if m == 0 else []))
            bx1i2.wc(t_x1)
            t_qi = p.dma("scalar", "qi", (lambda m: lambda e: e.dma_start(
                out=qTin.rearrange("p (ik t) -> p ik t", ik=NCT),
                in_=QTS.rearrange("ik p t -> p ik t")[:, :, MAC * m:MAC * (m + 1)]))(m),
                deps=bqT.wdeps() + (wbar if m == 0 else []))
            bqT.wc(t_qi)
            y_last = None
            for jq in range(NCT):
                bank, bankb = (PW, bPW) if jq % 2 == 0 else (PX, bPX)
                mms = []
                for ik in range(NCT):
                    deps = [bGb.w, bqT.w]
                    if ik == 0:
                        deps += bankb.wdeps()
                    mms.append(p.op("tensor", (lambda bank, jq, ik: lambda e: e.matmul(
                        bank[:, :], G_b[:, C * ik + 128 * jq:C * ik + 128 * (jq + 1)],
                        qTin[:, 512 * ik:512 * (ik + 1)],
                        start=(ik == 0), stop=(ik == NCT - 1)))(bank, jq, ik), deps=deps))
                bankb.wc(mms[-1])
                bqT.rc(mms[-1])
                deps_e = [mms[-1]]
                if jq == 0:
                    deps_e += byT.wdeps()
                y_last = p.op("scalar", (lambda bank, jq: lambda e: e.activation(
                    out=yT[:, 512 * jq:512 * (jq + 1)], in_=bank[:, :],
                    func=AF.Identity))(bank, jq), deps=deps_e)
                bankb.rc(y_last)
            byT.wc(y_last)
            if dbg:
                t_dmp3 = p.dma("sync", "dbg", (lambda m: lambda e: e.dma_start(
                    out=YD.rearrange("jq q t -> q jq t")[:, :, MAC * m:MAC * (m + 1)],
                    in_=yT.rearrange("p (jq t) -> p jq t", jq=NCT)))(m), deps=byT.r())
                byT.rc(t_dmp3)
            def emit_Fa(j, m=m):
                asl = j % 2
                stoks, sc_ap, nb_ap = ln_stats(x1i2[:, C * j:C * (j + 1)], [bx1i2.w],
                                               j % 4)
                t_a0 = p.op("scalar", (lambda j, asl, sc_ap, nb_ap: lambda e: e.activation(
                    out=a0b[:, C * asl:C * (asl + 1)], in_=x1i2[:, C * j:C * (j + 1)],
                    func=AF.Identity, scale=sc_ap, bias=nb_ap))(j, asl, sc_ap, nb_ap),
                    deps=stoks + [bx1i2.w] + ba0[asl].wdeps())
                ba0[asl].wc(t_a0)
                bstt[j % 4].rc(t_a0)
                bx1i2.rc(t_a0)
                mms = []
                for h in range(2):
                    for jv in range(NCT):
                        deps = [byT.w, boutw.w]
                        if h == 0 and jv == 0:
                            deps += bPA.wdeps()
                        mms.append(p.op("tensor", (lambda j, h, jv: lambda e: e.matmul(
                            PA[:, 512 * h:512 * (h + 1)],
                            yT[:, 512 * jv + 128 * j:512 * jv + 128 * (j + 1)],
                            outw_s[:, C * jv + 512 * h:C * jv + 512 * (h + 1)],
                            start=(jv == 0), stop=False))(j, h, jv), deps=deps))
                byT.rc(mms[-1])
                for h in range(2):
                    mms.append(p.op("tensor", (lambda h: lambda e: e.matmul(
                        PA[:, 512 * h:512 * (h + 1)], onesb[0:1, :],
                        outb_rb[0:1, 512 * h:512 * (h + 1)], start=False, stop=True))(h),
                        deps=[boutb.w, ld["ONESB"]]))
                bPA.wc(mms[-1])
                t_a1 = p.op("vector", (lambda j, asl: lambda e: e.tensor_add(
                    a1s(j), a0b[:, C * asl:C * (asl + 1)],
                    PA[:, :]))(j, asl),
                    deps=[mms[-1], t_a0] + ba1[j].wdeps())
                bPA.rc(t_a1)
                ba0[asl].rc(t_a1)
                ba1[j].wc(t_a1)
                if dbg:
                    t_dmp = p.dma("sync", "dbg", (lambda m, j: lambda e: e.dma_start(
                        out=A1D[MAC * m + MIC * j:MAC * m + MIC * (j + 1), :],
                        in_=a1s(j)))(m, j), deps=[t_a1])
                    ba1[j].rc(t_dmp)
                return t_a1

            def emit_Fb(j, t_a1):
                asl = j % 2
                stoks2, sc2, nb2 = ln_stats(a1s(j), [t_a1], j % 4)
                t_n2 = p.op("scalar", (lambda j, asl, sc2, nb2: lambda e: e.activation(
                    out=a1nb[:, C * asl:C * (asl + 1)], in_=a1s(j),
                    func=AF.Identity, scale=sc2, bias=nb2))(j, asl, sc2, nb2),
                    deps=stoks2 + [t_a1] + ba1nb[asl].wdeps())
                ba1nb[asl].wc(t_n2)
                bstt[j % 4].rc(t_n2)
                ba1[j].rc(t_n2)
                transpose8(
                    (lambda asl: lambda k: a1nb[:, C * asl + 128 * k:C * asl + 128 * (k + 1)])(asl),
                    None, [t_n2], ba1nT,
                    dst_view=a1nT.rearrange("p (k t) -> p k t", t=512)[:, :, 128 * j:128 * (j + 1)])
                ba1nb[asl].rc(ba1nT.w)

            def emit_H(half, m=m):
                h_last = None
                for dk in range(NCT):
                    bank, bankb = (PW, bPW) if dk % 2 == 0 else (PX, bPX)
                    mms = []
                    for ck in range(NCT):
                        deps = [ba1nT.w, bw1.w]
                        if ck == 0:
                            deps += bankb.wdeps()
                        mms.append(p.op("tensor", (lambda bank, dk, ck, half: lambda e: e.matmul(
                            bank[:, 0:256], w1_s[:, C * ck + 128 * dk:C * ck + 128 * (dk + 1)],
                            a1nT[:, 512 * ck + 256 * half:512 * ck + 256 * (half + 1)],
                            start=(ck == 0), stop=(ck == NCT - 1)))(bank, dk, ck, half),
                            deps=deps))
                    bankb.wc(mms[-1])
                    ba1nT.rc(mms[-1])
                    deps_e = [mms[-1], bb1c.w]
                    if dk == 0:
                        deps_e += bhT.wdeps()
                    h_last = p.op("scalar", (lambda bank, dk, half: lambda e: e.activation(
                        out=hT[:, 512 * dk + 256 * half:512 * dk + 256 * (half + 1)],
                        in_=bank[:, 0:256],
                        func=AF.Relu, bias=b1_c[:, dk:dk + 1]))(bank, dk, half), deps=deps_e)
                    bankb.rc(h_last)
                bhT.wc(h_last)
                if dbg:
                    t_dmp5 = p.dma("sync", "dbg", (lambda m, half: lambda e: e.dma_start(
                        out=HD.rearrange("dk q t -> q dk t")[:, :, MAC * m + 256 * half:MAC * m + 256 * (half + 1)],
                        in_=hT.rearrange("p (dk t) -> p dk t", dk=NCT)[:, :, 256 * half:256 * (half + 1)]))(m, half),
                        deps=bhT.r())
                    bhT.rc(t_dmp5)

            def emit_M(j, m=m):
                mms = []
                for h in range(2):
                    for dk in range(NCT):
                        deps = [bhT.w, bw2.w]
                        if h == 0 and dk == 0:
                            deps += bPB.wdeps()
                        mms.append(p.op("tensor", (lambda j, h, dk: lambda e: e.matmul(
                            PB[:, 512 * h:512 * (h + 1)],
                            hT[:, 512 * dk + 128 * j:512 * dk + 128 * (j + 1)],
                            w2_s[:, C * dk + 512 * h:C * dk + 512 * (h + 1)],
                            start=(dk == 0), stop=False))(j, h, dk), deps=deps))
                bhT.rc(mms[-1])
                for h in range(2):
                    mms.append(p.op("tensor", (lambda h: lambda e: e.matmul(
                        PB[:, 512 * h:512 * (h + 1)], onesb[0:1, :],
                        b2_rb[0:1, 512 * h:512 * (h + 1)], start=False, stop=True))(h),
                        deps=[bb2.w, ld["ONESB"]]))
                bPB.wc(mms[-1])
                t_a2 = p.op("vector", (lambda j: lambda e: e.tensor_add(
                    a1s(j), a1s(j),
                    PB[:, :]))(j),
                    deps=[mms[-1], ba1[j].w] + ba1[j].rs)
                bPB.rc(t_a2)
                ba1[j].wc(t_a2)
                if dbg:
                    t_dmp4 = p.dma("sync", "dbg", (lambda m, j: lambda e: e.dma_start(
                        out=A2D[MAC * m + MIC * j:MAC * m + MIC * (j + 1), :],
                        in_=a1s(j)))(m, j), deps=[t_a2])
                    ba1[j].rc(t_dmp4)
                t_x2 = p.op("vector", (lambda j: lambda e: e.tensor_add(
                    x1i2[:, C * j:C * (j + 1)], x1i2[:, C * j:C * (j + 1)],
                    a1s(j)))(j),
                    deps=[t_a2, bx1i2.w] + bx1i2.rs + ([t_dmp4] if dbg else []))
                ba1[j].rc(t_x2)
                bx1i2.wc(t_x2)
                if dbg:
                    t_dmp2 = p.dma("sync", "dbg", (lambda m, j: lambda e: e.dma_start(
                        out=X2D[MAC * m + MIC * j:MAC * m + MIC * (j + 1), :],
                        in_=x1i2[:, C * j:C * (j + 1)]))(m, j), deps=[bx1i2.w])
                    bx1i2.rc(t_dmp2)
                emit_lnT(x1i2[:, C * j:C * (j + 1)], [bx1i2.w], j, ln3T, bln3T,
                         ln3b, bln3b)
                bx1i2.rc(bln3T.w)

            ta1 = {}
            for j in (0, 1):
                ta1[j] = emit_Fa(j)
            for j in (0, 1):
                emit_Fb(j, ta1[j])
            for j in (2, 3):
                ta1[j] = emit_Fa(j)
            for j in (2, 3):
                emit_Fb(j, ta1[j])
            emit_H(0)
            emit_M(0)
            emit_M(1)
            emit_H(1)
            emit_M(2)
            emit_M(3)

            def b_out(j, rsl, vtok, sig_tok, rsb_ap, brsb_slot, m=m):
                osl = j % 2
                t_t3 = p.op("vector", (lambda rsb_ap, rsl: lambda e: e.scalar_tensor_tensor(
                    out=t3b2[:, C * rsl:C * (rsl + 1)], in0=PB[:, :], scalar=1.0,
                    in1=rsb_ap, op0=ALU.mult, op1=ALU.mult))(rsb_ap, rsl),
                    deps=[vtok, sig_tok] + bt32[rsl].wdeps())
                bPB.rc(t_t3)
                brsb_slot.rc(t_t3)
                bt32[rsl].wc(t_t3)
                t_o = p.op("vector", (lambda j, rsl, osl: lambda e: e.tensor_add(
                    outb_f[:, C * osl:C * (osl + 1)], x1i2[:, C * j:C * (j + 1)],
                    t3b2[:, C * rsl:C * (rsl + 1)]))(j, rsl, osl),
                    deps=[t_t3, bx1i2.w] + boutf[osl].wdeps())
                bt32[rsl].rc(t_o)
                bx1i2.rc(t_o)
                boutf[osl].wc(t_o)
                t_od = p.dma("sync", "oo", (lambda m, j, osl: lambda e: e.dma_start(
                    out=OUT[MAC * m + MIC * j:MAC * m + MIC * (j + 1), :],
                    in_=outb_f[:, C * osl:C * (osl + 1)]))(m, j, osl),
                    deps=[t_o])
                boutf[osl].rc(t_od)

            emit_ffn_mix(ln3T, bln3T, xrT2, bxr2, xkT2, bxk2, dscr_B, bdscr_B)
            emit_ffn_mms(xrT2, bxr2, xkT2, bxk2, kactT2, bkact2,
                         wr_s2, wk_s2, wv_s2, bwr2, bwk2, bwv2, b_out,
                         krelu_B, bkrelu_B, rsb_B, [bhT, bhT])

        p.op("sync", lambda e: e.nop(),
             deps=([("oo", p.dma_counts["oo"])] if p.dma_counts.get("oo") else p.barrier_tokens()) +
                  ([("dbg", p.dma_counts["dbg"])] if dbg else []))

        bstk.close()
        p.build_and_emit(root)
    return nc


def _prep_shared(inputs):
    f32 = np.float32
    ac = np.ascontiguousarray
    return {
        "WRT": ac(inputs["L0_ffn_W_r"].T.astype(f32)),
        "WKT": ac(inputs["L0_ffn_W_k"].T.astype(f32)),
        "WVT": ac(inputs["L0_ffn_W_v"].T.astype(f32)),
        "WG": ac(inputs["L0_attn_gates_w"].astype(f32)),
        "LRW": ac(inputs["L0_attn_lr_w"].reshape(1, -1).astype(f32)),
        "OUTW": ac(inputs["L0_attn_out_w"].astype(f32)),
        "W1": ac(inputs["L0_attn_w1"].astype(f32)),
        "W2": ac(inputs["L0_attn_w2"].astype(f32)),
        "DNG": ac(inputs["L0_attn_dn_g"].reshape(1, -1).astype(f32)),
        "DNB": ac(inputs["L0_attn_dn_b"].reshape(1, -1).astype(f32)),
        "MNG": ac(inputs["L0_attn_mn_g"].reshape(1, -1).astype(f32)),
        "MNB": ac(inputs["L0_attn_mn_b"].reshape(1, -1).astype(f32)),
        "MUR": ac(inputs["L0_ffn_mu_r"].reshape(1, -1).astype(f32)),
        "MUK": ac(inputs["L0_ffn_mu_k"].reshape(1, -1).astype(f32)),
        "GATESB": ac(inputs["L0_attn_gates_b"].reshape(1, -1).astype(f32)),
        "OUTB": ac(inputs["L0_attn_out_b"].reshape(1, -1).astype(f32)),
        "B1": ac(inputs["L0_attn_b1"].reshape(1, -1).astype(f32)),
        "B2": ac(inputs["L0_attn_b2"].reshape(1, -1).astype(f32)),
        "LRB": ac(inputs["L0_attn_lr_b"].reshape(1, -1).astype(f32)),
        "IDENT_F": np.eye(128, dtype=f32),
        "IDENT_B": np.eye(128).astype(ml_dtypes.bfloat16),
        "ONESB": np.ones((1, 128)).astype(ml_dtypes.bfloat16),
        "LMASK": np.tril(np.ones((128, 128), dtype=f32), -1),
        "TWOI": (2.0 * np.eye(128)).astype(ml_dtypes.bfloat16),
    }


def kernel(**inputs):
    x = np.ascontiguousarray(inputs["L0_attn_x"], dtype=np.float32)
    B = x.shape[0]
    shared = _prep_shared(inputs)
    nc = build(T=x.shape[1])
    in_maps = [dict(shared, X=np.ascontiguousarray(x[b])) for b in range(B)]
    res = run_bass_kernel_spmd(nc, in_maps, core_ids=list(range(B)))
    return np.stack([np.asarray(res.results[b]["OUT"], dtype=np.float32)
                     for b in range(B)], axis=0)


# version 15
# speedup vs baseline: 1.2291x; 1.0057x over previous
import contextlib
from collections import defaultdict

import numpy as np
import ml_dtypes

import concourse.bass as bass
import concourse.mybir as mybir
from concourse.bass_utils import run_bass_kernel_spmd

F32 = mybir.dt.float32
BF16 = mybir.dt.bfloat16
AF = mybir.ActivationFunctionType
ALU = mybir.AluOpType
AX = mybir.AxisListType

C = 1024
NCT = 8
MAC = 512
MIC = 128
EPS = 1e-5

ENGINES = ("tensor", "vector", "scalar", "gpsimd", "sync")


class _Op:
    __slots__ = ("idx", "engine", "fn", "deps", "seq", "dma")

    def __init__(self, idx, engine, fn, deps, seq, dma):
        self.idx = idx
        self.engine = engine
        self.fn = fn
        self.deps = deps
        self.seq = seq
        self.dma = dma


class Prog:
    def __init__(self, nc):
        self.nc = nc
        self.ops = []
        self.counts = {e: 0 for e in ENGINES}
        self.dma_counts = defaultdict(int)
        self.dma_sem_names = []

    def op(self, engine, fn, deps=()):
        self.counts[engine] += 1
        o = _Op(len(self.ops), engine, fn, self._norm(deps), self.counts[engine], None)
        self.ops.append(o)
        return (engine, self.counts[engine])

    def dma(self, queue, dma_sem, fn, deps=()):
        if dma_sem not in self.dma_counts:
            self.dma_sem_names.append(dma_sem)
        self.dma_counts[dma_sem] += 16
        o = _Op(len(self.ops), queue, fn, self._norm(deps), self.dma_counts[dma_sem], dma_sem)
        self.ops.append(o)
        return (dma_sem, self.dma_counts[dma_sem])

    @staticmethod
    def _norm(deps):
        out = []
        seen = set()
        for d in deps:
            if d is None or d in seen:
                continue
            seen.add(d)
            out.append(d)
        return out

    def barrier_tokens(self):
        toks = [(e, self.counts[e]) for e in ENGINES if self.counts[e] > 0]
        toks += [(s, self.dma_counts[s]) for s in self.dma_sem_names]
        return toks

    def hard_barrier(self):
        toks = self.barrier_tokens()
        for e in ENGINES:
            if self.counts[e] > 0:
                self.op(e, lambda eng: eng.nop(), deps=toks)
        return self.barrier_tokens()

    def emit(self, block, sems, dma_sems):
        waited = defaultdict(set)
        for o in self.ops:
            for (s, v) in o.deps:
                waited[s].add(v)
        streams = {e: [o for o in self.ops if o.engine == e] for e in ENGINES}
        dq = {}
        for o in self.ops:
            if o.dma:
                q = dq.setdefault(o.dma, o.engine)
                assert q == o.engine, f"dma sem {o.dma} used from two queues"
        dcross = {s: sorted(waited[s]) for s in self.dma_sem_names}

        def emit_stream(eng_api, engine):
            max_w = defaultdict(int)
            crossing = sorted(waited[engine])
            ci = 0
            count = 0
            dcnt = defaultdict(int)
            dci = defaultdict(int)
            for o in streams[engine]:
                if not o.dma:
                    while ci < len(crossing) and crossing[ci] <= count:
                        v = crossing[ci]
                        if max_w[engine] < v < o.seq:
                            eng_api.wait_ge(sems[engine], v)
                            max_w[engine] = v
                        ci += 1
                else:
                    s = o.dma
                    cr = dcross[s]
                    while dci[s] < len(cr) and cr[dci[s]] <= dcnt[s]:
                        v = cr[dci[s]]
                        if max_w[s] < v:
                            eng_api.wait_ge(dma_sems[s], v)
                            max_w[s] = v
                        dci[s] += 1
                    dcnt[s] += 16
                for (s, v) in o.deps:
                    if v > max_w[s]:
                        eng_api.wait_ge(sems[s] if s in ENGINES else dma_sems[s], v)
                        max_w[s] = v
                inst = o.fn(eng_api)
                if o.dma:
                    inst.then_inc(dma_sems[o.dma], 16)
                else:
                    inst.then_inc(sems[engine], 1)
                    count += 1

        for engine in ENGINES:
            if not streams[engine]:
                continue
            section = getattr(block, engine)

            def make(engine):
                def _(eng_api):
                    emit_stream(eng_api, engine)
                return _
            section(make(engine))

    def build_and_emit(self, stack):
        nc = self.nc
        sems = {}
        for e in ENGINES:
            if any(o.engine == e and not o.dma for o in self.ops):
                sems[e] = stack.enter_context(nc.semaphore(f"sem_{e}"))
        dma_sems = {}
        for name in self.dma_sem_names:
            dma_sems[name] = stack.enter_context(nc.semaphore(f"ds_{name}"))
        block = stack.enter_context(nc.Block())
        self.emit(block, sems, dma_sems)


class Buf:
    __slots__ = ("w", "rs", "ro")

    def __init__(self, readonly=False):
        self.w = None
        self.rs = []
        self.ro = readonly

    def r(self):
        return [self.w] if self.w is not None else []

    def rc(self, tok):
        if not self.ro:
            self.rs.append(tok)

    def wdeps(self):
        d = list(self.rs)
        if self.w is not None:
            d.append(self.w)
        return d

    def wc(self, tok):
        self.w = tok
        self.rs = []


def build(T=8192, dbg=False, stop_after=None, a1_parts=127, b_parts=255):
    nc = bass.Bass("TRN2", target_bir_lowering=False)
    NMAC = T // MAC

    X = nc.dram_tensor("X", [T, C], F32, kind="ExternalInput")
    OUT = nc.dram_tensor("OUT", [T, C], F32, kind="ExternalOutput")
    WRT = nc.dram_tensor("WRT", [C, C], F32, kind="ExternalInput")
    WKT = nc.dram_tensor("WKT", [C, C], F32, kind="ExternalInput")
    WVT = nc.dram_tensor("WVT", [C, C], F32, kind="ExternalInput")
    WG = nc.dram_tensor("WG", [C, 3 * C], F32, kind="ExternalInput")
    LRW = nc.dram_tensor("LRW", [1, C], F32, kind="ExternalInput")
    OUTW = nc.dram_tensor("OUTW", [C, C], F32, kind="ExternalInput")
    W1 = nc.dram_tensor("W1", [C, C], F32, kind="ExternalInput")
    W2 = nc.dram_tensor("W2", [C, C], F32, kind="ExternalInput")
    rows_in = {}
    for nm, n in [("DNG", C), ("DNB", C), ("MNG", C), ("MNB", C), ("MUR", C),
                  ("MUK", C), ("GATESB", 3 * C), ("OUTB", C), ("B1", C), ("B2", C),
                  ("LRB", 1)]:
        rows_in[nm] = nc.dram_tensor(nm, [1, n], F32, kind="ExternalInput")
    IDENT_F = nc.dram_tensor("IDENT_F", [128, 128], F32, kind="ExternalInput")
    IDENT_B = nc.dram_tensor("IDENT_B", [128, 128], BF16, kind="ExternalInput")
    ONESB = nc.dram_tensor("ONESB", [1, 128], BF16, kind="ExternalInput")
    LMASK = nc.dram_tensor("LMASK", [128, 128], F32, kind="ExternalInput")
    TWOI = nc.dram_tensor("TWOI", [128, 128], BF16, kind="ExternalInput")
    skind = {"kind": "ExternalOutput"} if dbg else {}
    X1S = nc.dram_tensor("X1S", [T, C], F32, **skind)
    QTS = nc.dram_tensor("QTS", [NCT, 128, T], BF16, **skind)
    if dbg:
        SD = nc.dram_tensor("SD", [128, NCT * C], F32, kind="ExternalOutput")
        GD = nc.dram_tensor("GD", [128, NCT * C], F32, kind="ExternalOutput")
        A1D = nc.dram_tensor("A1D", [T, C], F32, kind="ExternalOutput")
        X2D = nc.dram_tensor("X2D", [T, C], F32, kind="ExternalOutput")
        YD = nc.dram_tensor("YD", [NCT, 128, T], BF16, kind="ExternalOutput")
        A2D = nc.dram_tensor("A2D", [T, C], F32, kind="ExternalOutput")
        HD = nc.dram_tensor("HD", [NCT, 128, T], BF16, kind="ExternalOutput")

    p = Prog(nc)
    root = contextlib.ExitStack()

    def SB(stk, n, sh, dt):
        return stk.enter_context(nc.sbuf_tensor(n, sh, dt))

    with root:
        ident_f = SB(root, "ident_f", [128, 128], F32)
        ident_b = SB(root, "ident_b", [128, 128], BF16)
        onesb = SB(root, "onesb", [1, 128], BF16)
        lmask = SB(root, "lmask", [128, 128], F32)
        twoi = SB(root, "twoi", [128, 128], BF16)
        twoi4 = SB(root, "twoi4", [128, 512], BF16)
        ccols = SB(root, "ccols", [128, 8], F32)
        onef = SB(root, "onef", [1, 1], F32)
        dng_c = SB(root, "dng_c", [128, NCT], F32)
        mng_c = SB(root, "mng_c", [128, NCT], F32)
        dnb_cb = SB(root, "dnb_cb", [128, NCT], BF16)
        mnb_cb = SB(root, "mnb_cb", [128, NCT], BF16)
        mur_c = SB(root, "mur_c", [128, NCT], F32)
        muk_c = SB(root, "muk_c", [128, NCT], F32)
        qb_c = SB(root, "qb_c", [128, NCT], F32)
        kb_c = SB(root, "kb_c", [128, NCT], F32)
        b1_c = SB(root, "b1_c", [128, NCT], F32)
        lrb_bc = SB(root, "lrb_bc", [128, 1], F32)
        lrw_b = SB(root, "lrw_b", [128, NCT], BF16)
        browv_b = SB(root, "browv_b", [1, C], BF16)
        outb_rb = SB(root, "outb_rb", [1, C], BF16)
        b2_rb = SB(root, "b2_rb", [1, C], BF16)
        stt = SB(root, "stt", [128, 32], F32)
        sqd = SB(root, "sqd", [128, 2 * C], BF16)

        PA = root.enter_context(nc.psum_tensor("PA", [128, 1024], F32))
        PB = root.enter_context(nc.psum_tensor("PB", [128, 1024], F32))
        PW = root.enter_context(nc.psum_tensor("PW", [128, 512], F32))
        PX = root.enter_context(nc.psum_tensor("PX", [128, 512], F32))
        PT = root.enter_context(nc.psum_tensor("PT", [128, 1024], BF16))
        PS = root.enter_context(nc.psum_tensor("PS", [128, 512], F32))
        bPA, bPB, bPW, bPX, bPT, bPS = (Buf() for _ in range(6))

        bstt = [Buf() for _ in range(4)]
        bsqd = [Buf(), Buf()]

        bccols = Buf()
        btwoi4 = Buf()
        bonef = Buf()
        bdng, bmng, bdnbb, bmnbb, bmur, bmuk = (Buf() for _ in range(6))
        bqb, bkb, bb1c, blrb, blrwb, bbrowv, boutb, bb2 = (Buf() for _ in range(8))

        gstk = contextlib.ExitStack()
        wg_s = SB(gstk, "wg_s", [128, NCT * 3 * C], BF16)
        bwg = Buf(readonly=True)
        fwstk = contextlib.ExitStack()
        wr_s = SB(fwstk, "wr_s", [128, NCT * C], BF16)
        wk_s = SB(fwstk, "wk_s", [128, NCT * C], BF16)
        wv_s = SB(fwstk, "wv_s", [128, NCT * C], BF16)
        bwr, bwk, bwv = (Buf(readonly=True) for _ in range(3))
        initstk = contextlib.ExitStack()
        rowbuf = SB(initstk, "rowbuf", [1, 3 * C], F32)
        brow = Buf()
        brow2t = SB(initstk, "brow2", [1, 3 * C], F32)
        bbrow2 = Buf()
        colf = SB(initstk, "colf", [128, NCT], F32)
        bcolf = Buf()
        wstage = SB(initstk, "wstage", [128, 3 * C], F32)
        wstage2 = SB(initstk, "wstage2", [128, 3 * C], F32)
        wraw = SB(initstk, "wraw", [128, 3 * C], BF16)
        bwstage, bwstage2, bwraw = Buf(), Buf(), Buf()
        wstages = [(wstage, bwstage), (wstage2, bwstage2)]

        ld = {}
        for nm, dst, src in [("IDENT_F", ident_f, IDENT_F), ("IDENT_B", ident_b, IDENT_B),
                             ("ONESB", onesb, ONESB), ("LMASK", lmask, LMASK),
                             ("TWOI", twoi, TWOI)]:
            ld[nm] = p.dma("sync", "cl", (lambda dst, src: lambda e: e.dma_start(
                out=dst[:, :], in_=src[:, :]))(dst, src))
        cc_toks = []
        for i, v in [(0, EPS), (1, -0.5), (2, 0.5), (3, -1.0), (4, 1.0 + EPS),
                     (5, EPS * EPS)]:
            cc_toks.append(p.op("vector", (lambda i, v: lambda e: e.memset(
                ccols[:, i:i + 1], v))(i, v)))
        bccols.wc(cc_toks[-1])
        cc = cc_toks[-1]
        t_onef = p.op("vector", lambda e: e.memset(onef[0:1, 0:1], 1.0))
        bonef.wc(t_onef)
        tw_toks = []
        for i in range(4):
            tw_toks.append(p.op("vector", (lambda i: lambda e: e.tensor_copy(
                twoi4[:, 128 * i:128 * (i + 1)], twoi[:, :]))(i),
                deps=[ld["TWOI"]] + (btwoi4.wdeps() if i == 0 else [])))
        btwoi4.wc(tw_toks[-1])

        def load_row(nm, n):
            tl = p.dma("sync", "cl", (lambda nm, n: lambda e: e.dma_start(
                out=rowbuf[0:1, 0:n], in_=rows_in[nm][0:1, 0:n]))(nm, n),
                deps=brow.wdeps())
            brow.wc(tl)
            return tl

        def row_to_cols(row_ap, row_tok, n, dst, bdst):
            mms = []
            for k in range(n // 128):
                deps = [row_tok, t_onef]
                if k == 0:
                    deps += bPS.wdeps()
                mms.append(p.op("tensor", (lambda k, row_ap: lambda e: e.matmul(
                    PS[:, k:k + 1], row_ap[0:1, 128 * k:128 * (k + 1)],
                    onef[0:1, 0:1], start=True, stop=True))(k, row_ap), deps=deps))
            bPS.wc(mms[-1])
            t_e = p.op("vector", (lambda n: lambda e: e.tensor_copy(
                dst[:, 0:n // 128], PS[:, 0:n // 128]))(n),
                deps=[mms[-1]] + bdst.wdeps())
            bPS.rc(t_e)
            bdst.wc(t_e)
            return t_e

        for nm, dst, bdst in [("DNG", dng_c, bdng), ("MNG", mng_c, bmng),
                              ("MUR", mur_c, bmur), ("MUK", muk_c, bmuk)]:
            tl = load_row(nm, C)
            te = row_to_cols(rowbuf, tl, C, dst, bdst)
            brow.rc(te)
        for nm, dstb, bdstb in [("DNB", dnb_cb, bdnbb), ("MNB", mnb_cb, bmnbb)]:
            tl = load_row(nm, C)
            te = row_to_cols(rowbuf, tl, C, colf, bcolf)
            brow.rc(te)
            tc = p.op("scalar", (lambda dstb: lambda e: e.activation(
                out=dstb[:, :], in_=colf[:, :], func=AF.Identity))(dstb),
                deps=[te] + bdstb.wdeps())
            bcolf.rc(tc)
            bdstb.wc(tc)

        fold_regions = [PA[0:1, 0:512], PA[0:1, 512:1024], PB[0:1, 0:512],
                        PB[0:1, 512:1024], PW[0:1, 0:512], PX[0:1, 0:512]]
        fold_banks = [bPA, bPA, bPB, bPB, bPW, bPX]
        gate_fold = [None] * 6
        wg_last = None
        for ck in range(NCT):
            wst, bwst = wstages[ck % 2]
            dl = p.dma("scalar", "wl", (lambda ck, wst: lambda e: e.dma_start(
                out=wst[:, :], in_=WG[128 * ck:128 * (ck + 1), :]))(ck, wst),
                deps=bwst.wdeps())
            bwst.wc(dl)
            tr = p.op("scalar", (lambda wst: lambda e: e.activation(
                out=wraw[:, :], in_=wst[:, :], func=AF.Identity))(wst),
                      deps=[dl] + bwraw.wdeps())
            bwraw.wc(tr)
            bwst.rc(tr)
            for h in range(6):
                deps = [tr, bdnbb.w]
                if ck == 0:
                    deps += fold_banks[h].wdeps()
                mm = p.op("tensor", (lambda h, ck: lambda e: e.matmul(
                    fold_regions[h], dnb_cb[:, ck:ck + 1], wraw[:, 512 * h:512 * (h + 1)],
                    start=(ck == 0), stop=(ck == NCT - 1)))(h, ck), deps=deps)
                bwraw.rc(mm)
                if ck == NCT - 1:
                    fold_banks[h].wc(mm)
                    gate_fold[h] = mm
            ts = p.op("scalar", (lambda ck, wst: lambda e: e.activation(
                out=wg_s[:, 3 * C * ck:3 * C * (ck + 1)], in_=wst[:, :],
                func=AF.Identity, scale=dng_c[:, ck:ck + 1]))(ck, wst),
                deps=[dl, bdng.w])
            bwst.rc(ts)
            bdng.rc(ts)
            wg_last = ts
        bwg.w = wg_last
        t_gb = load_row("GATESB", 3 * C)
        ev_last = None
        for h in range(6):
            ev = p.op("vector", (lambda h: lambda e: e.tensor_add(
                brow2t[0:1, 512 * h:512 * (h + 1)], rowbuf[0:1, 512 * h:512 * (h + 1)],
                fold_regions[h]))(h),
                deps=[t_gb, gate_fold[h]] + (bbrow2.wdeps() if h == 0 else []))
            fold_banks[h].rc(ev)
            brow.rc(ev)
            ev_last = ev
        bbrow2.wc(ev_last)
        t_bv = p.op("scalar", lambda e: e.activation(out=browv_b[0:1, :],
                                                     in_=brow2t[0:1, 2 * C:3 * C],
                                                     func=AF.Identity), deps=[ev_last])
        bbrow2.rc(t_bv)
        bbrowv.wc(t_bv)
        t_qbc = row_to_cols(brow2t, ev_last, C, qb_c, bqb)
        bbrow2.rc(t_qbc)
        mms = []
        for k in range(NCT):
            deps = [ev_last, t_onef]
            if k == 0:
                deps += bPS.wdeps()
            mms.append(p.op("tensor", (lambda k: lambda e: e.matmul(
                PS[:, k:k + 1], brow2t[0:1, C + 128 * k:C + 128 * (k + 1)],
                onef[0:1, 0:1], start=True, stop=True))(k), deps=deps))
        bPS.wc(mms[-1])
        t_kbc = p.op("vector", lambda e: e.tensor_copy(kb_c[:, 0:NCT], PS[:, 0:NCT]),
                     deps=[mms[-1]] + bkb.wdeps())
        bPS.rc(t_kbc)
        bkb.wc(t_kbc)
        bbrow2.rc(t_kbc)

        tl = p.dma("sync", "cl", lambda e: e.dma_start(out=rowbuf[0:1, 0:C],
                                                       in_=LRW[0:1, 0:C]),
                   deps=brow.wdeps())
        brow.wc(tl)
        t_lrwf = row_to_cols(rowbuf, tl, C, colf, bcolf)
        brow.rc(t_lrwf)
        t_lrwb = p.op("vector", lambda e: e.tensor_mul(lrw_b[:, :], colf[:, :],
                                                       dng_c[:, :]),
                      deps=[t_lrwf, bdng.w] + blrwb.wdeps())
        bcolf.rc(t_lrwb)
        bdng.rc(t_lrwb)
        blrwb.wc(t_lrwb)
        lrwr_b = SB(initstk, "lrwr_b", [128, NCT], BF16)
        blrwr = Buf()
        t_lrwr = p.op("scalar", lambda e: e.activation(out=lrwr_b[:, :], in_=colf[:, :],
                                                       func=AF.Identity), deps=[t_lrwf])
        bcolf.rc(t_lrwr)
        blrwr.wc(t_lrwr)
        lrf_mm = None
        for ck in range(NCT):
            deps = [bdnbb.w, t_lrwr]
            if ck == 0:
                deps += bPS.wdeps()
            lrf_mm = p.op("tensor", (lambda ck: lambda e: e.matmul(
                PS[0:1, 0:1], dnb_cb[:, ck:ck + 1], lrwr_b[:, ck:ck + 1],
                start=(ck == 0), stop=(ck == NCT - 1)))(ck), deps=deps)
        bPS.wc(lrf_mm)
        t_lrb = load_row("LRB", 1)
        lrb_row = SB(initstk, "lrb_row", [1, 2], F32)
        lrb_rowb = SB(initstk, "lrb_rowb", [1, 1], BF16)
        blrr = Buf()
        t1 = p.op("vector", lambda e: e.tensor_add(lrb_row[0:1, 0:1], rowbuf[0:1, 0:1],
                                                   PS[0:1, 0:1]),
                  deps=[t_lrb, lrf_mm] + blrr.wdeps())
        bPS.rc(t1)
        brow.rc(t1)
        blrr.wc(t1)
        t2 = p.op("scalar", lambda e: e.activation(out=lrb_rowb[0:1, 0:1],
                                                   in_=lrb_row[0:1, 0:1], func=AF.Identity),
                  deps=[t1])
        blrr.rc(t2)
        mmb = p.op("tensor", lambda e: e.matmul(PS[:, 0:1], onesb[0:1, :],
                                                lrb_rowb[0:1, 0:1], start=True, stop=True),
                   deps=[t2, ld["ONESB"]] + bPS.wdeps())
        bPS.wc(mmb)
        t_lrbb = p.op("scalar", lambda e: e.activation(out=lrb_bc[:, :], in_=PS[:, 0:1],
                                                       func=AF.Identity),
                      deps=[mmb] + blrb.wdeps())
        bPS.rc(t_lrbb)
        blrb.wc(t_lrbb)

        t_ob = load_row("OUTB", C)
        t_obb = p.op("scalar", lambda e: e.activation(out=outb_rb[0:1, :],
                                                      in_=rowbuf[0:1, 0:C], func=AF.Identity),
                     deps=[t_ob])
        brow.rc(t_obb)
        boutb.wc(t_obb)
        t_b2 = load_row("B2", C)
        t_b2b = p.op("scalar", lambda e: e.activation(out=b2_rb[0:1, :],
                                                      in_=rowbuf[0:1, 0:C], func=AF.Identity),
                     deps=[t_b2])
        brow.rc(t_b2b)
        bb2.wc(t_b2b)

        b1f = [None, None]
        for ck in range(NCT):
            wst, bwst = wstages[ck % 2]
            dl = p.dma("scalar", "wl", (lambda ck, wst: lambda e: e.dma_start(
                out=wst[:, 0:C], in_=W1[128 * ck:128 * (ck + 1), :]))(ck, wst),
                deps=bwst.wdeps())
            bwst.wc(dl)
            tr = p.op("scalar", (lambda wst: lambda e: e.activation(
                out=wraw[:, 0:C], in_=wst[:, 0:C], func=AF.Identity))(wst),
                      deps=[dl] + bwraw.wdeps())
            bwraw.wc(tr)
            bwst.rc(tr)
            for h in range(2):
                deps = [tr, bmnbb.w]
                if ck == 0:
                    deps += (bPA.wdeps() if h == 0 else bPB.wdeps())
                fm = p.op("tensor", (lambda h, ck: lambda e: e.matmul(
                    (PA if h == 0 else PB)[0:1, 0:512], mnb_cb[:, ck:ck + 1],
                    wraw[:, 512 * h:512 * (h + 1)], start=(ck == 0),
                    stop=(ck == NCT - 1)))(h, ck), deps=deps)
                bwraw.rc(fm)
                if ck == NCT - 1:
                    (bPA if h == 0 else bPB).wc(fm)
                    b1f[h] = fm
        t_b1 = load_row("B1", C)
        t_bf0 = p.op("vector", lambda e: e.tensor_add(
            brow2t[0:1, 0:512], rowbuf[0:1, 0:512], PA[0:1, 0:512]),
            deps=[t_b1, b1f[0]] + bbrow2.wdeps())
        bPA.rc(t_bf0)
        brow.rc(t_bf0)
        t_bf1 = p.op("vector", lambda e: e.tensor_add(
            brow2t[0:1, 512:1024], rowbuf[0:1, 512:1024], PB[0:1, 0:512]),
            deps=[t_b1, b1f[1], t_bf0])
        bPB.rc(t_bf1)
        brow.rc(t_bf1)
        bbrow2.wc(t_bf1)
        t_b1c = row_to_cols(brow2t, t_bf1, C, b1_c, bb1c)
        bbrow2.rc(t_b1c)

        def load_ffn(dst, bdst, srcd, half=False):
            last = None
            for ck in range(NCT):
                wst, bwst = wstages[ck % 2]
                dl = p.dma("scalar", "wl", (lambda ck, srcd, wst: lambda e: e.dma_start(
                    out=wst[:, 0:C],
                    in_=srcd[128 * ck:128 * (ck + 1), :]))(ck, srcd, wst),
                    deps=bwst.wdeps())
                bwst.wc(dl)
                if half:
                    tc = p.op("scalar", (lambda ck, dst, wst: lambda e: e.activation(
                        out=dst[:, C * ck:C * (ck + 1)], in_=wst[:, 0:C],
                        func=AF.Identity, scale=ccols[:, 2:3]))(ck, dst, wst),
                        deps=[dl, cc])
                else:
                    tc = p.op("scalar", (lambda ck, dst, wst: lambda e: e.activation(
                        out=dst[:, C * ck:C * (ck + 1)], in_=wst[:, 0:C],
                        func=AF.Identity))(ck, dst, wst),
                        deps=[dl])
                bwst.rc(tc)
                last = tc
            bdst.w = last
            return last

        t_wr = load_ffn(wr_s, bwr, WRT)
        t_wk = load_ffn(wk_s, bwk, WKT)
        t_wv = load_ffn(wv_s, bwv, WVT, half=True)

        bar_p0 = p.hard_barrier()
        initstk.close()
        a0stk = contextlib.ExitStack()
        dscr = SB(a0stk, "dscr", [128, 1024], BF16)
        bdscr = Buf()
        krelu = SB(a0stk, "krelu", [128, 1024], F32)
        bkrelu = Buf()
        r_sb = SB(a0stk, "r_sb", [128, 2 * C], F32)
        br_sb = [Buf(), Buf()]

        x0b = SB(a0stk, "x0b", [128, 2 * C], F32)
        bx0 = [Buf(), Buf()]
        x0add = SB(a0stk, "x0add", [128, 2 * C], F32)
        bx0add = [Buf(), Buf()]
        lnb = SB(a0stk, "lnb", [128, 4 * C], BF16)
        blnb = [Buf(), Buf(), Buf(), Buf()]
        lnT = SB(a0stk, "lnT", [128, NCT * 513], BF16)
        blnT = Buf()
        xrT = SB(a0stk, "xrT", [128, 2 * NCT * 512], BF16)
        xkT = SB(a0stk, "xkT", [128, 2 * NCT * 512], BF16)
        bxr = [Buf(), Buf()]
        bxk = [Buf(), Buf()]
        kactT = SB(a0stk, "kactT", [128, NCT * 512], BF16)
        bkact = Buf()
        x1b = SB(a0stk, "x1b", [128, 2 * C], F32)
        bx1 = [Buf(), Buf()]

        def transpose8(src_fn, dst_fn, deps_src, bdst, dst_view=None):
            trs = []
            wd = bPT.wdeps()
            for k in range(8):
                deps = list(deps_src) + [ld["IDENT_B"]]
                if k == 0:
                    deps += wd
                trs.append(p.op("tensor", (lambda k: lambda e: e.transpose(
                    PT[:, 128 * k:128 * (k + 1)], src_fn(k), ident_b[:, :]))(k),
                    deps=deps))
            bPT.wc(trs[-1])
            wd2 = bdst.wdeps()
            if dst_view is not None:
                evs_last = p.op("vector", (lambda dst_view: lambda e: e.tensor_copy(
                    dst_view, PT.rearrange("p (k q) -> p k q", q=128)))(dst_view),
                    deps=[trs[-1]] + wd2)
            else:
                evs_last = None
                for k in range(8):
                    deps = [trs[-1]]
                    if k == 0:
                        deps += wd2
                    evs_last = p.op("vector", (lambda k: lambda e: e.tensor_copy(
                        dst_fn(k), PT[:, 128 * k:128 * (k + 1)]))(k), deps=deps)
            bPT.rc(evs_last)
            bdst.wc(evs_last)
            return evs_last

        def ln_stats(src_ap, deps_x, slot, second_ln=False):
            s0 = 8 * slot
            bs = bstt[slot]
            t_sum = p.op("vector", (lambda src_ap, s0: lambda e: e.reduce_sum(
                stt[:, s0:s0 + 1], src_ap, axis=AX.X))(src_ap, s0),
                deps=deps_x + bs.wdeps())
            t_sq = p.op("scalar", (lambda src_ap, s0: lambda e: e.activation(
                out=sqd[:, 0:C], in_=src_ap, func=AF.Square,
                accum_out=stt[:, s0 + 1:s0 + 2]))(src_ap, s0),
                deps=deps_x + bsqd[0].wdeps() + [t_sum])
            bsqd[0].wc(t_sq)
            t_mean = p.op("vector", (lambda s0: lambda e: e.tensor_scalar(
                out=stt[:, s0 + 2:s0 + 3], in0=stt[:, s0:s0 + 1], scalar1=1.0 / C,
                scalar2=None, op0=ALU.mult))(s0), deps=[t_sum])
            t_mm2 = p.op("vector", (lambda s0: lambda e: e.tensor_mul(
                stt[:, s0 + 3:s0 + 4], stt[:, s0 + 2:s0 + 3], stt[:, s0 + 2:s0 + 3]))(s0),
                deps=[t_mean])
            t_var = p.op("vector", (lambda s0: lambda e: e.tensor_scalar(
                out=stt[:, s0 + 4:s0 + 5], in0=stt[:, s0 + 1:s0 + 2], scalar1=1.0 / C,
                scalar2=stt[:, s0 + 3:s0 + 4], op0=ALU.mult, op1=ALU.subtract))(s0),
                deps=[t_sq, t_mm2])
            if not second_ln:
                t_ln = p.op("scalar", (lambda s0: lambda e: e.activation(
                    out=stt[:, s0 + 5:s0 + 6], in_=stt[:, s0 + 4:s0 + 5], func=AF.Ln,
                    bias=ccols[:, 0:1]))(s0), deps=[t_var, cc])
            else:
                t_ln = p.op("scalar", (lambda s0: lambda e: e.activation(
                    out=stt[:, s0 + 5:s0 + 6], in_=stt[:, s0 + 4:s0 + 5], func=AF.Ln,
                    scale=ccols[:, 4:5], bias=ccols[:, 5:6]))(s0), deps=[t_var, cc])
            t_sc = p.op("scalar", (lambda s0: lambda e: e.activation(
                out=stt[:, s0 + 6:s0 + 7], in_=stt[:, s0 + 5:s0 + 6], func=AF.Exp,
                scale=ccols[:, 1:2]))(s0), deps=[t_ln])
            t_nb = p.op("vector", (lambda s0: lambda e: e.tensor_scalar(
                out=stt[:, s0 + 7:s0 + 8], in0=stt[:, s0 + 2:s0 + 3],
                scalar1=stt[:, s0 + 6:s0 + 7], scalar2=-1.0, op0=ALU.mult,
                op1=ALU.mult))(s0), deps=[t_sc, t_mean])
            bs.wc(t_nb)
            return [t_sc, t_nb], stt[:, s0 + 6:s0 + 7], stt[:, s0 + 7:s0 + 8]

        def stats_norm(src_ap, deps_x, j, lnb_t, blnb_t, sl, second_ln=False):
            stoks, sc_ap, nb_ap = ln_stats(src_ap, deps_x, j % 4, second_ln)
            t_n = p.op("scalar", (lambda src_ap, sc_ap, nb_ap, sl: lambda e:
                       e.activation(out=lnb_t[:, C * sl:C * (sl + 1)], in_=src_ap,
                                    func=AF.Identity, scale=sc_ap, bias=nb_ap))(
                           src_ap, sc_ap, nb_ap, sl),
                       deps=stoks + deps_x + blnb_t[sl].wdeps())
            blnb_t[sl].wc(t_n)
            bstt[j % 4].rc(t_n)
            return t_n

        def lnb_to_lnT(j, lnT_t, blnT_t, lnb_t, blnb_t, sl, t_n):
            transpose8(
                (lambda sl: lambda k: lnb_t[:, C * sl + 128 * k:C * sl + 128 * (k + 1)])(sl),
                None, [t_n], blnT_t,
                dst_view=lnT_t.rearrange("p (k t) -> p k t", t=513)[:, :, 1 + 128 * j:1 + 128 * (j + 1)])
            blnb_t[sl].rc(blnT_t.w)

        def emit_lnT(src_ap, deps_x, j, lnT_t, blnT_t, lnb_t, blnb_t, second_ln=False):
            sl = j % 2
            t_n = stats_norm(src_ap, deps_x, j, lnb_t, blnb_t, sl, second_ln)
            lnb_to_lnT(j, lnT_t, blnT_t, lnb_t, blnb_t, sl, t_n)
            return t_n

        def emit_ffn_mix(lnT_t, blnT_t, xrT_t, bxr_t, xkT_t, bxk_t,
                         dscr_t, bdscr_t):
            mix_last = None
            xr_last = None
            for k in range(NCT):
                cur = lnT_t[:, 513 * k + 1:513 * k + 513]
                shf = lnT_t[:, 513 * k:513 * k + 512]
                t_d = p.op("vector", (lambda cur, shf, k: lambda e: e.tensor_sub(
                    dscr_t[:, 512 * (k % 2):512 * (k % 2 + 1)], cur, shf))(cur, shf, k),
                    deps=blnT_t.r() + bdscr_t.wdeps())
                bdscr_t.wc(t_d)
                blnT_t.rc(t_d)
                t_xr = p.op("vector", (lambda k, shf: lambda e: e.scalar_tensor_tensor(
                    out=xrT_t[:, 512 * k:512 * (k + 1)],
                    in0=dscr_t[:, 512 * (k % 2):512 * (k % 2 + 1)],
                    scalar=mur_c[:, k:k + 1], in1=shf, op0=ALU.mult, op1=ALU.add))(k, shf),
                    deps=[t_d, bmur.w] + (bxr_t.wdeps() if k == 0 else []))
                t_xk = p.op("vector", (lambda k, shf: lambda e: e.scalar_tensor_tensor(
                    out=xkT_t[:, 512 * k:512 * (k + 1)],
                    in0=dscr_t[:, 512 * (k % 2):512 * (k % 2 + 1)],
                    scalar=muk_c[:, k:k + 1], in1=shf, op0=ALU.mult, op1=ALU.add))(k, shf),
                    deps=[t_d, bmuk.w] + (bxk_t.wdeps() if k == 0 else []))
                bdscr_t.rc(t_xr)
                bdscr_t.rc(t_xk)
                blnT_t.rc(t_xr)
                blnT_t.rc(t_xk)
                xr_last = t_xr
                mix_last = t_xk
            bxr_t.wc(xr_last)
            bxk_t.wc(mix_last)
            bc_last = None
            for k in range(NCT):
                bc_last = p.op("vector", (lambda k: lambda e: e.tensor_copy(
                    lnT_t[:, 513 * k:513 * k + 1],
                    lnT_t[:, 513 * k + 512:513 * k + 513]))(k),
                    deps=[mix_last, xr_last] + blnT_t.r())
            blnT_t.wc(bc_last)

        def emit_ffn_mms(xrT_t, bxr_t, xkT_t, bxk_t, kactT_t, bkact_t,
                         wr_t, wk_t, wv_t, bwr_t, bwk_t, bwv_t,
                         out_action, krelu_t, bkrelu_t, rsb_t, brsb_t,
                         pre_dk=None, post_dk=None, pre_jloop=None):
            ka_last = None
            for dk in range(NCT):
                if pre_dk is not None and dk % 2 == 0:
                    pre_dk(dk)
                bank, bankb = (PW, bPW) if dk % 2 == 0 else (PX, bPX)
                mms = []
                for ck in range(NCT):
                    deps = [bxk_t.w, bwk_t.w]
                    if ck == 0:
                        deps += bankb.wdeps()
                    mms.append(p.op("tensor", (lambda bank, dk, ck: lambda e: e.matmul(
                        bank[:, :], wk_t[:, C * ck + 128 * dk:C * ck + 128 * (dk + 1)],
                        xkT_t[:, 512 * ck:512 * (ck + 1)],
                        start=(ck == 0), stop=(ck == NCT - 1)))(bank, dk, ck), deps=deps))
                bankb.wc(mms[-1])
                bxk_t.rc(mms[-1])
                t_r = p.op("scalar", (lambda bank, dk: lambda e: e.activation(
                    out=krelu_t[:, 512 * (dk % 2):512 * (dk % 2 + 1)], in_=bank[:, :],
                    func=AF.Relu))(bank, dk),
                    deps=[mms[-1]] + bkrelu_t.wdeps())
                bankb.rc(t_r)
                bkrelu_t.wc(t_r)
                t_s = p.op("scalar", (lambda dk: lambda e: e.activation(
                    out=kactT_t[:, 512 * dk:512 * (dk + 1)],
                    in_=krelu_t[:, 512 * (dk % 2):512 * (dk % 2 + 1)], func=AF.Square))(dk),
                    deps=[t_r] + (bkact_t.wdeps() if dk == 0 else []))
                bkrelu_t.rc(t_s)
                ka_last = t_s
                if post_dk is not None and dk % 2 == 1:
                    post_dk(dk)
            bkact_t.wc(ka_last)
            if pre_jloop is not None:
                pre_jloop()
            for j in range(4):
                rmms = []
                for h in range(2):
                    for ck in range(NCT):
                        deps = [bxr_t.w, bwr_t.w]
                        if h == 0 and ck == 0:
                            deps += bPA.wdeps()
                        rmms.append(p.op("tensor", (lambda j, h, ck: lambda e: e.matmul(
                            PA[:, 512 * h:512 * (h + 1)],
                            xrT_t[:, 512 * ck + 128 * j:512 * ck + 128 * (j + 1)],
                            wr_t[:, C * ck + 512 * h:C * ck + 512 * (h + 1)],
                            start=(ck == 0), stop=(ck == NCT - 1)))(j, h, ck),
                            deps=deps))
                bPA.wc(rmms[-1])
                bxr_t.rc(rmms[-1])
                rsl = j % 2
                sig = p.op("scalar", (lambda rsl: lambda e: e.activation(
                    out=rsb_t[:, C * rsl:C * (rsl + 1)], in_=PA[:, :],
                    func=AF.Sigmoid))(rsl),
                    deps=[rmms[-1]] + brsb_t[rsl].wdeps())
                bPA.rc(sig)
                brsb_t[rsl].wc(sig)
                vmms = []
                for h in range(2):
                    for dk in range(NCT):
                        deps = [bkact_t.w, bwv_t.w]
                        if h == 0 and dk == 0:
                            deps += bPB.wdeps()
                        vmms.append(p.op("tensor", (lambda j, h, dk: lambda e: e.matmul(
                            PB[:, 512 * h:512 * (h + 1)],
                            kactT_t[:, 512 * dk + 128 * j:512 * dk + 128 * (j + 1)],
                            wv_t[:, C * dk + 512 * h:C * dk + 512 * (h + 1)],
                            start=(dk == 0), stop=(dk == NCT - 1)))(j, h, dk),
                            deps=deps))
                bPB.wc(vmms[-1])
                bkact_t.rc(vmms[-1])
                out_action(j, rsl, vmms[-1], sig,
                           rsb_t[:, C * rsl:C * (rsl + 1)], brsb_t[rsl])

        tz = None
        for k in range(NCT):
            tz = p.op("vector", (lambda k: lambda e: e.memset(
                lnT[:, 513 * k:513 * k + 1], 0.0))(k),
                deps=(blnT.wdeps() + bar_p0) if k == 0 else [])
        blnT.wc(tz)

        SLOT = NCT * 512

        def a0_mm_stage(m, pre_dk=None, post_dk=None, pre_jloop=None):
            ms = m % 2
            xr_v = xrT[:, SLOT * ms:SLOT * (ms + 1)]
            xk_v = xkT[:, SLOT * ms:SLOT * (ms + 1)]

            def a0_out(j, rsl, vtok, sig_tok, rsb_ap, brsb_slot, m=m):
                xsl = j % 2
                t_xl = p.dma("sync", "x0a", (lambda m, j, xsl: lambda e: e.dma_start(
                    out=x0add[:, C * xsl:C * (xsl + 1)],
                    in_=X[MAC * m + MIC * j:MAC * m + MIC * (j + 1), :]))(m, j, xsl),
                    deps=bx0add[xsl].wdeps())
                bx0add[xsl].wc(t_xl)
                t_t3 = p.op("vector", (lambda rsb_ap, xsl: lambda e: e.scalar_tensor_tensor(
                    out=x1b[:, C * xsl:C * (xsl + 1)], in0=PB[:, :], scalar=1.0,
                    in1=rsb_ap, op0=ALU.mult, op1=ALU.mult))(rsb_ap, xsl),
                    deps=[vtok, sig_tok] + bx1[xsl].wdeps())
                bPB.rc(t_t3)
                brsb_slot.rc(t_t3)
                bx1[xsl].wc(t_t3)
                t_x1 = p.op("vector", (lambda xsl: lambda e: e.tensor_add(
                    x1b[:, C * xsl:C * (xsl + 1)], x1b[:, C * xsl:C * (xsl + 1)],
                    x0add[:, C * xsl:C * (xsl + 1)]))(xsl),
                    deps=[t_t3, t_xl] + bx1[xsl].wdeps())
                bx0add[xsl].rc(t_x1)
                bx1[xsl].wc(t_x1)
                t_sp = p.dma("sync", "x1o", (lambda m, j, xsl: lambda e: e.dma_start(
                    out=X1S[MAC * m + MIC * j:MAC * m + MIC * (j + 1), :],
                    in_=x1b[:, C * xsl:C * (xsl + 1)]))(m, j, xsl),
                    deps=[t_x1])
                bx1[xsl].rc(t_sp)

            emit_ffn_mms(xr_v, bxr[ms], xk_v, bxk[ms], kactT, bkact,
                         wr_s, wk_s, wv_s, bwr, bwk, bwv, a0_out,
                         krelu, bkrelu, r_sb, br_sb,
                         pre_dk=pre_dk, post_dk=post_dk, pre_jloop=pre_jloop)

        def a0_stats_micro(m, j):
            xs = j % 2
            t_x0 = p.dma("sync", "x0i", (lambda m, j, xs: lambda e: e.dma_start(
                out=x0b[:, C * xs:C * (xs + 1)],
                in_=X[MAC * m + MIC * j:MAC * m + MIC * (j + 1), :]))(m, j, xs),
                deps=bx0[xs].wdeps() + (bar_p0 if m == 0 and j < 2 else []))
            bx0[xs].wc(t_x0)
            t_n = stats_norm(x0b[:, C * xs:C * (xs + 1)], [bx0[xs].w],
                             j, lnb, blnb, j)
            bx0[xs].rc(t_n)
            return t_n

        for m in range(NMAC):
            ms = m % 2
            if m == 0:
                for j in range(4):
                    t_n = a0_stats_micro(0, j)
                    lnb_to_lnT(j, lnT, blnT, lnb, blnb, j, t_n)
                emit_ffn_mix(lnT, blnT, xrT[:, 0:SLOT], bxr[0],
                             xkT[:, 0:SLOT], bxk[0], dscr, bdscr)
                continue
            ntoks = {}

            def pre_dk(dk, m=m, ntoks=ntoks):
                ntoks[dk // 2] = a0_stats_micro(m, dk // 2)

            def post_dk(dk, m=m, ntoks=ntoks):
                jj = (dk - 1) // 2
                lnb_to_lnT(jj, lnT, blnT, lnb, blnb, jj, ntoks[jj])

            def pre_j(m=m, ms=ms):
                emit_ffn_mix(lnT, blnT, xrT[:, SLOT * ms:SLOT * (ms + 1)], bxr[ms],
                             xkT[:, SLOT * ms:SLOT * (ms + 1)], bxk[ms], dscr, bdscr)

            a0_mm_stage(m - 1, pre_dk=pre_dk, post_dk=post_dk, pre_jloop=pre_j)
        a0_mm_stage(NMAC - 1)

        bar_a0 = p.hard_barrier()
        a0stk.close()
        fwstk.close()
        if stop_after == "a0":
            t_cp = p.dma("sync", "oo", lambda e: e.dma_start(out=OUT[:, :], in_=X1S[:, :]),
                         deps=bar_a0)
            p.op("sync", lambda e: e.nop(), deps=[t_cp])
            gstk.close()
            p.build_and_emit(root)
            return nc

        gfstk = contextlib.ExitStack()
        G_f = gfstk.enter_context(nc.sbuf_tensor("G_f", [128, NCT * C], F32, side="right"))
        bG = Buf()
        a1stk = contextlib.ExitStack()
        S_f = SB(a1stk, "S_f", [128, NCT * C], F32)
        S_b = SB(a1stk, "S_b", [128, NCT * C], BF16)
        bS = Buf()
        bSb_t = [Buf() for _ in range(NCT)]
        x1i = SB(a1stk, "x1i", [128, 2 * C], F32)
        bx1i = [Buf(), Buf()]
        a0nb = SB(a1stk, "a0nb", [128, 2 * C], BF16)
        ba0nb = [Buf(), Buf()]
        a0nT = SB(a1stk, "a0nT", [128, NCT * 512], BF16)
        ba0nT = Buf()
        kT = SB(a1stk, "kT", [128, NCT * 512], BF16)
        bkT = Buf()
        khat = SB(a1stk, "khat", [128, 4 * C], BF16)
        bkhat = Buf()
        qsp = khat
        bqsp = bkhat
        ut = SB(a1stk, "ut", [128, 4 * C], BF16)
        but = [Buf(), Buf(), Buf(), Buf()]
        wt = SB(a1stk, "wt", [128, 4 * C], BF16)
        bwt = Buf()
        b_sb = SB(a1stk, "b_sb", [128, C], BF16)
        bbsb = Buf()
        lr_c = SB(a1stk, "lr_c", [128, 4], F32)
        blr = Buf()
        nb_A = SB(a1stk, "nb_A", [128, 4 * 128], BF16)
        nb_AT = SB(a1stk, "nb_AT", [128, 4 * 128], BF16)
        nb_M = SB(a1stk, "nb_M", [128, 4 * 128], BF16)
        nb_MT = SB(a1stk, "nb_MT", [128, 4 * 128], BF16)
        nb_Q = nb_A
        bnA, bnAT, bnM, bnMT = (Buf() for _ in range(4))
        bnQ = bnA
        g_sc = SB(a1stk, "g_sc", [128, 4 * 128], F32)
        bg_sc = Buf()
        B_sb = SB(a1stk, "B_sb", [128, 4 * 128], F32)
        bB_sb = Buf()
        rr_b = SB(a1stk, "rr_b", [1, 128], BF16)
        brr = Buf()
        af_sc = SB(a1stk, "af_sc", [128, 128], F32)
        baf = Buf()

        tz1 = p.op("vector", lambda e: e.memset(S_f[:, :], 0.0), deps=bar_a0)
        bS.wc(tz1)
        tz2 = p.op("scalar", lambda e: e.activation(out=S_b[:, :], in_=S_f[:, :],
                                                    func=AF.Identity), deps=[tz1] + bar_a0)
        for bb in bSb_t:
            bb.wc(tz2)
        tz3 = p.op("vector", lambda e: e.memset(G_f[:, :], 0.0), deps=bar_a0)
        bG.wc(tz3)

        for m in range(NMAC):
            for j in range(4):
                sl = j % 2
                t_x1 = p.dma("sync", "x1i", (lambda m, j, sl: lambda e: e.dma_start(
                    out=x1i[:, C * sl:C * (sl + 1)],
                    in_=X1S[MAC * m + MIC * j:MAC * m + MIC * (j + 1), :]))(m, j, sl),
                    deps=bx1i[sl].wdeps() + (bar_a0 if m == 0 and j < 2 else []))
                bx1i[sl].wc(t_x1)
                stoks, sc_ap, nb_ap = ln_stats(x1i[:, C * sl:C * (sl + 1)],
                                               [bx1i[sl].w], j % 4, second_ln=True)
                t_n = p.op("scalar", (lambda sl, sc_ap, nb_ap: lambda e: e.activation(
                    out=a0nb[:, C * sl:C * (sl + 1)], in_=x1i[:, C * sl:C * (sl + 1)],
                    func=AF.Identity, scale=sc_ap, bias=nb_ap))(sl, sc_ap, nb_ap),
                    deps=stoks + [bx1i[sl].w] + ba0nb[sl].wdeps())
                ba0nb[sl].wc(t_n)
                bstt[j % 4].rc(t_n)
                bx1i[sl].rc(t_n)
                transpose8(
                    (lambda sl: lambda k: a0nb[:, C * sl + 128 * k:C * sl + 128 * (k + 1)])(sl),
                    (lambda j: lambda k: a0nT[:, 512 * k + 128 * j:512 * k + 128 * (j + 1)])(j),
                    [t_n], ba0nT)
                ba0nb[sl].rc(ba0nT.w)
            for which in ("q", "k"):
                off = 0 if which == "q" else C
                dstb, bdst = (qsp, bqsp) if which == "q" else (kT, bkT)
                bias_c = qb_c if which == "q" else kb_c
                bias_tok = bqb.w if which == "q" else bkb.w
                e_last = None
                for dk in range(NCT):
                    bank, bankb = (PW, bPW) if dk % 2 == 0 else (PX, bPX)
                    mms = []
                    for ck in range(NCT):
                        deps = [ba0nT.w, bwg.w]
                        if ck == 0:
                            deps += bankb.wdeps()
                        mms.append(p.op("tensor", (lambda bank, dk, ck, off: lambda e: e.matmul(
                            bank[:, :],
                            wg_s[:, 3 * C * ck + off + 128 * dk:3 * C * ck + off + 128 * (dk + 1)],
                            a0nT[:, 512 * ck:512 * (ck + 1)],
                            start=(ck == 0), stop=(ck == NCT - 1)))(bank, dk, ck, off),
                            deps=deps))
                    bankb.wc(mms[-1])
                    ba0nT.rc(mms[-1])
                    if which == "q":
                        dst_ap = qsp.bitcast(BF16)[:, 512 * dk:512 * (dk + 1)]
                    else:
                        dst_ap = kT[:, 512 * dk:512 * (dk + 1)]
                    deps_e = [mms[-1], bias_tok]
                    if dk == 0:
                        deps_e += bdst.wdeps()
                    e_last = p.op("scalar", (lambda bank, dst_ap, bias_c, dk: lambda e:
                                  e.activation(out=dst_ap, in_=bank[:, :], func=AF.Identity,
                                               bias=bias_c[:, dk:dk + 1]))(bank, dst_ap, bias_c, dk),
                                  deps=deps_e)
                    bankb.rc(e_last)
                bdst.wc(e_last)
                if which == "q":
                    t_qo = p.dma("scalar", "qo", (lambda m: lambda e: e.dma_start(
                        out=QTS.rearrange("ik p t -> p ik t")[:, :, MAC * m:MAC * (m + 1)],
                        in_=qsp.rearrange("p (ik t) -> p ik t", ik=NCT)))(m),
                        deps=bqsp.r())
                    bqsp.rc(t_qo)
            lr_last = None
            for j in range(4):
                mms = []
                for ck in range(NCT):
                    deps = [ba0nT.w, blrwb.w]
                    if ck == 0:
                        deps += bPS.wdeps()
                    mms.append(p.op("tensor", (lambda j, ck: lambda e: e.matmul(
                        PS[:, 0:1], a0nT[:, 512 * ck + 128 * j:512 * ck + 128 * (j + 1)],
                        lrw_b[:, ck:ck + 1],
                        start=(ck == 0), stop=(ck == NCT - 1)))(j, ck), deps=deps))
                bPS.wc(mms[-1])
                ba0nT.rc(mms[-1])
                lr_last = p.op("scalar", (lambda j: lambda e: e.activation(
                    out=lr_c[:, j:j + 1], in_=PS[:, 0:1], func=AF.Identity,
                    bias=lrb_bc[:, :]))(j),
                    deps=[mms[-1], blrb.w] + (blr.wdeps() if j == 0 else []))
                bPS.rc(lr_last)
            blr.wc(lr_last)
            for j in range(4 if a1_parts & 1 else 0):
                mms = []
                for ik in range(NCT):
                    deps = [bkT.w]
                    if ik == 0:
                        deps += bPS.wdeps()
                    mms.append(p.op("tensor", (lambda j, ik: lambda e: e.matmul(
                        PS[:, 0:128], kT[:, 512 * ik + 128 * j:512 * ik + 128 * (j + 1)],
                        kT[:, 512 * ik + 128 * j:512 * ik + 128 * (j + 1)],
                        start=(ik == 0), stop=(ik == NCT - 1)))(j, ik), deps=deps))
                bPS.wc(mms[-1])
                bkT.rc(mms[-1])
                s0 = 8 * (j % 4)
                t_dm = p.op("vector", lambda e: e.tensor_mul(
                    af_sc[:, :], PS[:, 0:128], ident_f[:, :]),
                    deps=[mms[-1], ld["IDENT_F"]] + baf.wdeps() + bstt[j % 4].wdeps())
                bPS.rc(t_dm)
                baf.wc(t_dm)
                t_d2 = p.op("vector", (lambda s0: lambda e: e.reduce_sum(
                    stt[:, s0:s0 + 1], af_sc[:, :], axis=AX.X))(s0), deps=[t_dm])
                baf.rc(t_d2)
                t_ln2 = p.op("scalar", (lambda s0: lambda e: e.activation(
                    out=stt[:, s0 + 1:s0 + 2], in_=stt[:, s0:s0 + 1], func=AF.Ln))(s0),
                    deps=[t_d2])
                t_rs = p.op("scalar", (lambda s0: lambda e: e.activation(
                    out=stt[:, s0 + 2:s0 + 3], in_=stt[:, s0 + 1:s0 + 2], func=AF.Exp,
                    scale=ccols[:, 1:2]))(s0), deps=[t_ln2, cc])
                bstt[j % 4].wc(t_rs)
                t_t1 = p.op("vector", (lambda j, s0: lambda e: e.tensor_scalar(
                    out=g_sc[:, 128 * j:128 * (j + 1)], in0=PS[:, 0:128],
                    scalar1=stt[:, s0 + 2:s0 + 3], scalar2=None, op0=ALU.mult))(j, s0),
                    deps=[t_rs, mms[-1]] + (bg_sc.wdeps() if j == 0 else []))
                bPS.rc(t_t1)
                bstt[j % 4].rc(t_t1)
                t_rr = p.op("tensor", (lambda s0: lambda e: e.transpose(
                    PS[0:1, 128:256], stt[:, s0 + 2:s0 + 3], ident_f[:, :]))(s0),
                    deps=[t_rs, t_t1, ld["IDENT_F"]] + bPS.wdeps())
                bPS.wc(t_rr)
                bstt[j % 4].rc(t_rr)
                t_rrb = p.op("scalar", lambda e: e.activation(
                    out=rr_b[0:1, :], in_=PS[0:1, 128:256], func=AF.Identity),
                    deps=[t_rr] + brr.wdeps())
                bPS.rc(t_rrb)
                brr.wc(t_rrb)
                t_bc = p.op("tensor", lambda e: e.matmul(
                    PS[:, 256:384], onesb[0:1, :], rr_b[0:1, :], start=True, stop=True),
                    deps=[t_rrb, ld["ONESB"]] + bPS.wdeps())
                bPS.wc(t_bc)
                brr.rc(t_bc)
                t_Bs = p.op("vector", (lambda j: lambda e: e.tensor_copy(
                    B_sb[:, 128 * j:128 * (j + 1)], PS[:, 256:384]))(j),
                    deps=[t_bc] + (bB_sb.wdeps() if j == 0 else []))
                bPS.rc(t_Bs)
                t_l2 = p.op("vector", (lambda j: lambda e: e.tensor_mul(
                    g_sc[:, 128 * j:128 * (j + 1)], g_sc[:, 128 * j:128 * (j + 1)],
                    B_sb[:, 128 * j:128 * (j + 1)]))(j), deps=[t_t1, t_Bs])
                t_l3 = p.op("vector", (lambda j: lambda e: e.tensor_mul(
                    g_sc[:, 128 * j:128 * (j + 1)], g_sc[:, 128 * j:128 * (j + 1)],
                    lmask[:, :]))(j), deps=[t_l2, ld["LMASK"]])
                t_A = p.op("vector", (lambda j: lambda e: e.tensor_tensor(
                    nb_A[:, 128 * j:128 * (j + 1)], g_sc[:, 128 * j:128 * (j + 1)],
                    ident_f[:, :], op=ALU.add))(j),
                    deps=[t_l3, ld["IDENT_F"]] + (bnA.wdeps() if j == 0 else []))
                t_M0 = p.op("vector", (lambda j: lambda e: e.tensor_tensor(
                    nb_M[:, 128 * j:128 * (j + 1)], g_sc[:, 128 * j:128 * (j + 1)],
                    ident_f[:, :], op=ALU.subtract))(j),
                    deps=[t_l3, ld["IDENT_F"]] + (bnM.wdeps() if j == 0 else []))
                if j == 3:
                    bg_sc.wc(t_l3)
                    bB_sb.wc(t_Bs)
                tt1 = p.op("tensor", (lambda j: lambda e: e.transpose(
                    PT[:, 0:128], nb_A[:, 128 * j:128 * (j + 1)], ident_b[:, :]))(j),
                    deps=[t_A, ld["IDENT_B"]] + bPT.wdeps())
                tt2 = p.op("tensor", (lambda j: lambda e: e.transpose(
                    PT[:, 128:256], nb_M[:, 128 * j:128 * (j + 1)], ident_b[:, :]))(j),
                    deps=[t_M0, ld["IDENT_B"]])
                bPT.wc(tt2)
                te1 = p.op("vector", (lambda j: lambda e: e.tensor_copy(
                    nb_AT[:, 128 * j:128 * (j + 1)], PT[:, 0:128]))(j),
                    deps=[tt2] + (bnAT.wdeps() if j == 0 else []))
                te2 = p.op("vector", (lambda j: lambda e: e.tensor_copy(
                    nb_MT[:, 128 * j:128 * (j + 1)], PT[:, 128:256]))(j),
                    deps=[tt2] + (bnMT.wdeps() if j == 0 else []))
                bPT.rc(te1)
                bPT.rc(te2)
                if j == 3:
                    bnA.wc(t_A)
                    bnM.wc(t_M0)
                    bnAT.wc(te1)
                    bnMT.wc(te2)
            for it in range(6 if a1_parts & 1 else 0):
                pms = []
                for j in range(4):
                    deps = [bnAT.w, bnM.w]
                    if j == 0:
                        deps += bPW.wdeps()
                    pms.append(p.op("tensor", (lambda j: lambda e: e.matmul(
                        PW[:, 128 * j:128 * (j + 1)], nb_AT[:, 128 * j:128 * (j + 1)],
                        nb_M[:, 128 * j:128 * (j + 1)], start=True, stop=True))(j),
                        deps=deps))
                bPW.wc(pms[-1])
                bnAT.rc(pms[-1])
                bnM.rc(pms[-1])
                t_Q = p.op("vector", lambda e: e.tensor_tensor(
                    nb_Q[:, 0:512], PW[:, 0:512], twoi4[:, :], op=ALU.add),
                    deps=[pms[-1], btwoi4.w] + bnQ.wdeps())
                bPW.rc(t_Q)
                bnQ.wc(t_Q)
                mms2 = []
                for j in range(4):
                    deps = [bnMT.w, bnQ.w]
                    if j == 0:
                        deps += bPX.wdeps()
                    mms2.append(p.op("tensor", (lambda j: lambda e: e.matmul(
                        PX[:, 128 * j:128 * (j + 1)], nb_MT[:, 128 * j:128 * (j + 1)],
                        nb_Q[:, 128 * j:128 * (j + 1)], start=True, stop=True))(j),
                        deps=deps))
                bPX.wc(mms2[-1])
                bnMT.rc(mms2[-1])
                bnQ.rc(mms2[-1])
                mms3 = []
                for j in range(4):
                    deps = [bnQ.w, bnMT.w]
                    if j == 0:
                        deps += bPW.wdeps()
                    mms3.append(p.op("tensor", (lambda j: lambda e: e.matmul(
                        PW[:, 128 * j:128 * (j + 1)], nb_Q[:, 128 * j:128 * (j + 1)],
                        nb_MT[:, 128 * j:128 * (j + 1)], start=True, stop=True))(j),
                        deps=deps))
                bPW.wc(mms3[-1])
                bnQ.rc(mms3[-1])
                bnMT.rc(mms3[-1])
                t_M = p.op("vector", lambda e: e.tensor_copy(nb_M[:, 0:512], PX[:, 0:512]),
                           deps=[mms2[-1]] + bnM.wdeps())
                bPX.rc(t_M)
                bnM.wc(t_M)
                t_MT = p.op("vector", lambda e: e.tensor_copy(nb_MT[:, 0:512], PW[:, 0:512]),
                            deps=[mms3[-1]] + bnMT.wdeps())
                bPW.rc(t_MT)
                bnMT.wc(t_MT)
            for j in range(4 if a1_parts & 1 else 0):
                last_sc = None
                for ik in range(NCT):
                    deps = [bkT.w, bB_sb.w] + (bkT.rs[-1:] if bkT.rs else [])
                    last_sc = p.op("vector", (lambda j, ik: lambda e: e.tensor_mul(
                        kT[:, 512 * ik + 128 * j:512 * ik + 128 * (j + 1)],
                        kT[:, 512 * ik + 128 * j:512 * ik + 128 * (j + 1)],
                        B_sb[:, 128 * j:128 * (j + 1)]))(j, ik), deps=deps)
                bkT.wc(last_sc)
                bB_sb.rc(last_sc)
                transpose8(
                    (lambda j: lambda ik: kT[:, 512 * ik + 128 * j:512 * ik + 128 * (j + 1)])(j),
                    None, [last_sc], bkhat,
                    dst_view=khat.rearrange("p (j2 ik q) -> p j2 ik q", j2=4, q=128)[:, j, :, :])
                bkT.rc(bkhat.w)
            inc_ks = bool(a1_parts & 8)
            inc_x = bool(a1_parts & 16)
            inc_u = bool(a1_parts & 32)
            inc_sf = bool(a1_parts & 64)

            def rec_vks(j, close_group):
                bank, bankb = (PA, bPA) if j % 2 == 0 else (PB, bPB)
                mms = []
                for h in range(2):
                    for ck in range(NCT):
                        deps = [ba0nT.w, bwg.w]
                        if h == 0 and ck == 0:
                            deps += bankb.wdeps()
                        mms.append(p.op("tensor", (lambda bank, j, h, ck: lambda e: e.matmul(
                            bank[:, 512 * h:512 * (h + 1)],
                            a0nT[:, 512 * ck + 128 * j:512 * ck + 128 * (j + 1)],
                            wg_s[:, 3 * C * ck + 2 * C + 512 * h:3 * C * ck + 2 * C + 512 * (h + 1)],
                            start=(ck == 0), stop=False))(bank, j, h, ck), deps=deps))
                ba0nT.rc(mms[-1])
                for h in range(2):
                    mms.append(p.op("tensor", (lambda bank, h, st2: lambda e: e.matmul(
                        bank[:, 512 * h:512 * (h + 1)], onesb[0:1, :],
                        browv_b[0:1, 512 * h:512 * (h + 1)], start=False,
                        stop=st2))(bank, h, (not inc_ks) and close_group),
                        deps=[bbrowv.w, ld["ONESB"]]))
                if inc_ks:
                    for h in range(2):
                        for ik in range(NCT):
                            is_stop = close_group and h == 1 and ik == NCT - 1 and False
                            is_stop = close_group and ik == NCT - 1
                            deps = [bkT.w, bSb_t[ik].w]
                            mms.append(p.op("tensor", (lambda bank, j, h, ik, is_stop: lambda e: e.matmul(
                                bank[:, 512 * h:512 * (h + 1)],
                                kT[:, 512 * ik + 128 * j:512 * ik + 128 * (j + 1)],
                                S_b[:, C * ik + 512 * h:C * ik + 512 * (h + 1)],
                                start=False, stop=is_stop))(bank, j, h, ik, is_stop),
                                deps=deps))
                            bSb_t[ik].rc(mms[-1])
                    bkT.rc(mms[-1])
                bankb.wc(mms[-1])

            def rec_corr(j):
                bank, bankb = (PA, bPA) if j % 2 == 0 else (PB, bPB)
                t_xu = None
                for jj in range(j):
                    xms = []
                    for ik in range(NCT):
                        deps = [bkT.w]
                        if ik == 0:
                            deps += bPS.wdeps()
                        xms.append(p.op("tensor", (lambda j, jj, ik: lambda e: e.matmul(
                            PS[:, 0:128],
                            kT[:, 512 * ik + 128 * jj:512 * ik + 128 * (jj + 1)],
                            kT[:, 512 * ik + 128 * j:512 * ik + 128 * (j + 1)],
                            start=(ik == 0), stop=(ik == NCT - 1)))(j, jj, ik), deps=deps))
                    bPS.wc(xms[-1])
                    bkT.rc(xms[-1])
                    t_xt = p.op("scalar", lambda e: e.activation(
                        out=af_sc.bitcast(BF16)[:, 0:128], in_=PS[:, 0:128],
                        func=AF.Identity), deps=[xms[-1]] + baf.wdeps())
                    bPS.rc(t_xt)
                    baf.wc(t_xt)
                    for h in range(2):
                        t_xu = p.op("tensor", (lambda h, jj, j: lambda e: e.matmul(
                            bank[:, 512 * h:512 * (h + 1)], af_sc.bitcast(BF16)[:, 0:128],
                            ut[:, C * jj + 512 * h:C * jj + 512 * (h + 1)],
                            start=False, stop=(jj == j - 1)))(h, jj, j),
                            deps=[t_xt, but[jj].w])
                    baf.rc(t_xu)
                    but[jj].rc(t_xu)
                if t_xu is not None:
                    bankb.wc(t_xu)

            def rec_bu(j):
                bank, bankb = (PA, bPA) if j % 2 == 0 else (PB, bPB)
                t_b2e = p.op("scalar", (lambda bank: lambda e: e.activation(
                    out=b_sb[:, 0:1024], in_=bank[:, 0:1024], func=AF.Identity))(bank),
                    deps=[bankb.w] + bbsb.wdeps())
                bankb.rc(t_b2e)
                bbsb.wc(t_b2e)
                if not inc_u:
                    return
                ums = []
                for h in range(2):
                    ums.append(p.op("tensor", (lambda bank, j, h: lambda e: e.matmul(
                        bank[:, 512 * h:512 * (h + 1)], nb_MT[:, 128 * j:128 * (j + 1)],
                        b_sb[:, 512 * h:512 * (h + 1)], start=True, stop=True))(bank, j, h),
                        deps=[bnMT.w, t_b2e] + (bankb.wdeps() if h == 0 else [])))
                bankb.wc(ums[-1])
                bnMT.rc(ums[-1])
                bbsb.rc(ums[-1])
                t_u2 = p.op("scalar", (lambda bank, j: lambda e: e.activation(
                    out=ut[:, C * j:C * (j + 1)], in_=bank[:, 0:1024],
                    func=AF.Identity))(bank, j),
                    deps=[ums[-1]] + but[j].wdeps())
                but[j].wc(t_u2)
                t_w2 = p.op("vector", (lambda j: lambda e: e.tensor_scalar(
                    out=wt[:, C * j:C * (j + 1)], in0=ut[:, C * j:C * (j + 1)],
                    scalar1=lr_c[:, j:j + 1], scalar2=None, op0=ALU.mult))(j),
                    deps=[t_u2, blr.w] + (bwt.wdeps() if j == 0 else []))
                but[j].rc(t_w2)
                blr.rc(t_w2)
                if j == 3:
                    bwt.wc(t_w2)
                bankb.rc(t_w2)

            if a1_parts & 2:
                for j in range(4):
                    close = (j == 0) or not inc_x
                    rec_vks(j, close_group=close)
                    if j > 0 and inc_x:
                        rec_corr(j)
                    rec_bu(j)
                if inc_sf:
                    for ik in range(NCT):
                        bank2, bank2b = (PA, bPA) if ik % 2 == 0 else (PB, bPB)
                        fms = []
                        for jj in range(4):
                            for h in range(2):
                                deps = [bkhat.w, but[jj].w]
                                if jj == 0 and h == 0:
                                    deps += bank2b.wdeps()
                                fms.append(p.op("tensor",
                                    (lambda bank2, ik, jj, h: lambda e: e.matmul(
                                        bank2[:, 512 * h:512 * (h + 1)],
                                        khat[:, C * jj + 128 * ik:C * jj + 128 * (ik + 1)],
                                        ut[:, C * jj + 512 * h:C * jj + 512 * (h + 1)],
                                        start=(jj == 0), stop=(jj == 3)))(bank2, ik, jj, h),
                                    deps=deps))
                        bank2b.wc(fms[-1])
                        bkhat.rc(fms[-1])
                        for jj in range(4):
                            but[jj].rc(fms[-1])
                        t_sa = p.op("vector", (lambda bank2, ik: lambda e: e.tensor_add(
                            S_f[:, C * ik:C * (ik + 1)], S_f[:, C * ik:C * (ik + 1)],
                            bank2[:, :]))(bank2, ik),
                            deps=[fms[-1]] + bS.wdeps())
                        bank2b.rc(t_sa)
                        bS.wc(t_sa)
                        t_sc2 = p.op("scalar", (lambda ik: lambda e: e.activation(
                            out=S_b[:, C * ik:C * (ik + 1)], in_=S_f[:, C * ik:C * (ik + 1)],
                            func=AF.Identity))(ik),
                            deps=[t_sa] + bSb_t[ik].wdeps())
                        bS.rc(t_sc2)
                        bSb_t[ik].wc(t_sc2)
            for ik in range(NCT if a1_parts & 4 else 0):
                bank2, bank2b = (PB, bPB) if ik % 2 == 0 else (PA, bPA)
                gms = []
                for jj in range(4):
                    for h in range(2):
                        deps = [bkhat.w, bwt.w]
                        if jj == 0 and h == 0:
                            deps += bank2b.wdeps()
                        gms.append(p.op("tensor", (lambda bank2, ik, jj, h: lambda e: e.matmul(
                            bank2[:, 512 * h:512 * (h + 1)],
                            khat[:, C * jj + 128 * ik:C * jj + 128 * (ik + 1)],
                            wt[:, C * jj + 512 * h:C * jj + 512 * (h + 1)],
                            start=(jj == 0), stop=(jj == 3)))(bank2, ik, jj, h),
                            deps=deps))
                bank2b.wc(gms[-1])
                bkhat.rc(gms[-1])
                bwt.rc(gms[-1])
                t_ga = p.op("vector", (lambda bank2, ik: lambda e: e.tensor_add(
                    G_f[:, C * ik:C * (ik + 1)], G_f[:, C * ik:C * (ik + 1)],
                    bank2[:, :]))(bank2, ik),
                    deps=[gms[-1]] + bG.wdeps())
                bank2b.rc(t_ga)
                bG.wc(t_ga)
        if dbg:
            t_sd = p.dma("sync", "dbg", lambda e: e.dma_start(out=SD[:, :], in_=S_f[:, :]),
                         deps=bS.r())
            bS.rc(t_sd)
            t_gd = p.dma("sync", "dbg", lambda e: e.dma_start(out=GD[:, :], in_=G_f[:, :]),
                         deps=bG.r())
            bG.rc(t_gd)
        if stop_after == "a1":
            toks = []
            for ik in range(NCT):
                toks.append(p.dma("sync", "oo", (lambda ik: lambda e: e.dma_start(
                    out=OUT[128 * ik:128 * (ik + 1), :],
                    in_=S_f[:, C * ik:C * (ik + 1)]))(ik), deps=bS.r() + bG.r()))
            for t in toks:
                bS.rc(t)
            p.op("sync", lambda e: e.nop(), deps=[toks[-1]] +
                 ([("dbg", p.dma_counts["dbg"])] if dbg else []))
            a1stk.close()
            gstk.close()
            gfstk.close()
            p.build_and_emit(root)
            return nc
        bar_a1 = p.hard_barrier()
        a1stk.close()
        gstk.close()

        bstk = contextlib.ExitStack()
        G_b = SB(bstk, "G_b", [128, NCT * C], BF16)
        bGb = Buf(readonly=True)
        t_gc = None
        for ik in range(2 * NCT):
            t_gc = p.op("scalar", (lambda ik: lambda e: e.activation(
                out=G_b[:, 512 * ik:512 * (ik + 1)], in_=G_f[:, 512 * ik:512 * (ik + 1)],
                func=AF.Identity))(ik), deps=[bG.w] + (bar_a1 if ik == 0 else []))
            bG.rc(t_gc)
        bGb.w = t_gc
        gbar = p.hard_barrier()
        gfstk.close()

        wr_s2 = SB(bstk, "wr_s2", [128, NCT * C], BF16)
        wk_s2 = SB(bstk, "wk_s2", [128, NCT * C], BF16)
        wv_s2 = SB(bstk, "wv_s2", [128, NCT * C], BF16)
        outw_s = SB(bstk, "outw_s", [128, NCT * C], BF16)
        w1_s = SB(bstk, "w1_s", [128, NCT * C], BF16)
        w2_s = SB(bstk, "w2_s", [128, NCT * C], BF16)
        bwstk = contextlib.ExitStack()
        wfstage2 = SB(bwstk, "wfstage2", [128, 2 * C], F32)
        bwf2 = [Buf(), Buf()]
        bwr2, bwk2, bwv2, boutw, bw1, bw2 = (Buf(readonly=True) for _ in range(6))

        def load_w2(dst, bdst, srcd, scale_col=None, scale_cols=None, bar=None,
                    scale_tok=None):
            last = None
            for ck in range(NCT):
                sl = ck % 2
                deps = bwf2[sl].wdeps()
                if bar is not None and ck < 2:
                    deps = deps + bar
                dl = p.dma("scalar", "wl", (lambda ck, sl, srcd: lambda e: e.dma_start(
                    out=wfstage2[:, C * sl:C * (sl + 1)],
                    in_=srcd[128 * ck:128 * (ck + 1), :]))(ck, sl, srcd),
                    deps=deps)
                bwf2[sl].wc(dl)
                if scale_cols is not None:
                    sc_ap = scale_cols[:, ck:ck + 1]
                elif scale_col is not None:
                    sc_ap = scale_col
                else:
                    sc_ap = None
                if sc_ap is None:
                    tc = p.op("scalar", (lambda ck, sl, dst: lambda e: e.activation(
                        out=dst[:, C * ck:C * (ck + 1)],
                        in_=wfstage2[:, C * sl:C * (sl + 1)], func=AF.Identity))(ck, sl, dst),
                        deps=[dl])
                else:
                    tc = p.op("scalar", (lambda ck, sl, dst, sc_ap: lambda e: e.activation(
                        out=dst[:, C * ck:C * (ck + 1)],
                        in_=wfstage2[:, C * sl:C * (sl + 1)], func=AF.Identity,
                        scale=sc_ap))(ck, sl, dst, sc_ap),
                        deps=[dl] + ([scale_tok] if scale_tok else [cc]))
                bwf2[sl].rc(tc)
                last = tc
            bdst.w = last
            return last

        load_w2(wr_s2, bwr2, WRT, bar=gbar)
        load_w2(wk_s2, bwk2, WKT)
        load_w2(wv_s2, bwv2, WVT, scale_col=ccols[:, 2:3])
        load_w2(outw_s, boutw, OUTW, scale_col=ccols[:, 3:4])
        load_w2(w1_s, bw1, W1, scale_cols=mng_c, scale_tok=bmng.w)
        load_w2(w2_s, bw2, W2)
        wbar = p.hard_barrier()
        bwstk.close()

        x1i2 = SB(bstk, "x1i2", [128, 4 * C], F32)
        bx1i2 = Buf()
        qTin = SB(bstk, "qTin", [128, NCT * 512], BF16)
        bqT = Buf()
        kactT2 = qTin
        bkact2 = bqT
        a0b = SB(bstk, "a0b", [128, 2 * C], F32)
        ba0 = [Buf(), Buf()]
        outb_f = a0b
        boutf = ba0
        yT = SB(bstk, "yT", [128, NCT * 512], BF16)
        byT = Buf()
        xrT2 = yT
        bxr2 = byT
        a1b = SB(bstk, "a1b", [128, 3 * C], F32)
        ba1 = [Buf(), Buf(), Buf(), bqT]
        t3b2 = a1b
        bt32 = ba1

        def a1s(j):
            return a1b[:, C * j:C * (j + 1)] if j < 3 else qTin.bitcast(F32)[:, 0:C]
        a1nb = SB(bstk, "a1nb", [128, 2 * C], BF16)
        _ban = Buf()
        ba1nb = [_ban, _ban]
        ln3b = a1nb
        bln3b = ba1nb
        a1nT = SB(bstk, "a1nT", [128, NCT * 512], BF16)
        ba1nT = Buf()
        xkT2 = a1nT
        bxk2 = ba1nT
        hT = SB(bstk, "hT", [128, NCT * 512], BF16)
        bhT = Buf()
        ln3T = bstk.enter_context(nc.sbuf_tensor("ln3T", [128, NCT * 513], BF16, side="right"))
        bln3T = Buf()
        dscr_B = sqd.bitcast(BF16)[:, C:2 * C]
        bdscr_B = Buf()
        krelu_B = a1nb.bitcast(F32)
        bkrelu_B = _ban
        rsb_B = hT.bitcast(F32)

        tz4 = None
        for k in range(NCT):
            tz4 = p.op("vector", (lambda k: lambda e: e.memset(
                ln3T[:, 513 * k:513 * k + 1], 0.0))(k),
                deps=(bln3T.wdeps() + wbar) if k == 0 else [])
        bln3T.wc(tz4)

        for m in range(NMAC if b_parts & 1 else 0):
            t_x1 = p.dma("sync", "x1i", (lambda m: lambda e: e.dma_start(
                out=x1i2.rearrange("p (j c) -> p j c", j=4),
                in_=X1S.rearrange("(mm j p) c -> p mm j c", p=128, j=4)[:, m, :, :]))(m),
                deps=bx1i2.wdeps() + (wbar if m == 0 else []))
            bx1i2.wc(t_x1)
            t_qi = p.dma("scalar", "qi", (lambda m: lambda e: e.dma_start(
                out=qTin.rearrange("p (ik t) -> p ik t", ik=NCT),
                in_=QTS.rearrange("ik p t -> p ik t")[:, :, MAC * m:MAC * (m + 1)]))(m),
                deps=bqT.wdeps() + (wbar if m == 0 else []))
            bqT.wc(t_qi)
            y_last = None
            for jq in range(NCT):
                bank, bankb = (PW, bPW) if jq % 2 == 0 else (PX, bPX)
                mms = []
                for ik in range(NCT):
                    deps = [bGb.w, bqT.w]
                    if ik == 0:
                        deps += bankb.wdeps()
                    mms.append(p.op("tensor", (lambda bank, jq, ik: lambda e: e.matmul(
                        bank[:, :], G_b[:, C * ik + 128 * jq:C * ik + 128 * (jq + 1)],
                        qTin[:, 512 * ik:512 * (ik + 1)],
                        start=(ik == 0), stop=(ik == NCT - 1)))(bank, jq, ik), deps=deps))
                bankb.wc(mms[-1])
                bqT.rc(mms[-1])
                deps_e = [mms[-1]]
                if jq == 0:
                    deps_e += byT.wdeps()
                y_last = p.op("vector", (lambda bank, jq: lambda e: e.tensor_copy(
                    yT[:, 512 * jq:512 * (jq + 1)], bank[:, :]))(bank, jq), deps=deps_e)
                bankb.rc(y_last)
            byT.wc(y_last)
            if dbg:
                t_dmp3 = p.dma("sync", "dbg", (lambda m: lambda e: e.dma_start(
                    out=YD.rearrange("jq q t -> q jq t")[:, :, MAC * m:MAC * (m + 1)],
                    in_=yT.rearrange("p (jq t) -> p jq t", jq=NCT)))(m), deps=byT.r())
                byT.rc(t_dmp3)
            def emit_Fa(j, m=m):
                asl = j % 2
                stoks, sc_ap, nb_ap = ln_stats(x1i2[:, C * j:C * (j + 1)], [bx1i2.w],
                                               j % 4)
                t_a0 = p.op("scalar", (lambda j, asl, sc_ap, nb_ap: lambda e: e.activation(
                    out=a0b[:, C * asl:C * (asl + 1)], in_=x1i2[:, C * j:C * (j + 1)],
                    func=AF.Identity, scale=sc_ap, bias=nb_ap))(j, asl, sc_ap, nb_ap),
                    deps=stoks + [bx1i2.w] + ba0[asl].wdeps())
                ba0[asl].wc(t_a0)
                bstt[j % 4].rc(t_a0)
                bx1i2.rc(t_a0)
                mms = []
                for h in range(2):
                    for jv in range(NCT):
                        deps = [byT.w, boutw.w]
                        if h == 0 and jv == 0:
                            deps += bPA.wdeps()
                        mms.append(p.op("tensor", (lambda j, h, jv: lambda e: e.matmul(
                            PA[:, 512 * h:512 * (h + 1)],
                            yT[:, 512 * jv + 128 * j:512 * jv + 128 * (j + 1)],
                            outw_s[:, C * jv + 512 * h:C * jv + 512 * (h + 1)],
                            start=(jv == 0), stop=False))(j, h, jv), deps=deps))
                byT.rc(mms[-1])
                for h in range(2):
                    mms.append(p.op("tensor", (lambda h: lambda e: e.matmul(
                        PA[:, 512 * h:512 * (h + 1)], onesb[0:1, :],
                        outb_rb[0:1, 512 * h:512 * (h + 1)], start=False, stop=True))(h),
                        deps=[boutb.w, ld["ONESB"]]))
                bPA.wc(mms[-1])
                t_a1 = p.op("vector", (lambda j, asl: lambda e: e.tensor_add(
                    a1s(j), a0b[:, C * asl:C * (asl + 1)],
                    PA[:, :]))(j, asl),
                    deps=[mms[-1], t_a0] + ba1[j].wdeps())
                bPA.rc(t_a1)
                ba0[asl].rc(t_a1)
                ba1[j].wc(t_a1)
                if dbg:
                    t_dmp = p.dma("sync", "dbg", (lambda m, j: lambda e: e.dma_start(
                        out=A1D[MAC * m + MIC * j:MAC * m + MIC * (j + 1), :],
                        in_=a1s(j)))(m, j), deps=[t_a1])
                    ba1[j].rc(t_dmp)
                return t_a1

            def emit_Fb(j, t_a1):
                asl = j % 2
                stoks2, sc2, nb2 = ln_stats(a1s(j), [t_a1], j % 4)
                t_n2 = p.op("scalar", (lambda j, asl, sc2, nb2: lambda e: e.activation(
                    out=a1nb[:, C * asl:C * (asl + 1)], in_=a1s(j),
                    func=AF.Identity, scale=sc2, bias=nb2))(j, asl, sc2, nb2),
                    deps=stoks2 + [t_a1] + ba1nb[asl].wdeps())
                ba1nb[asl].wc(t_n2)
                bstt[j % 4].rc(t_n2)
                ba1[j].rc(t_n2)
                transpose8(
                    (lambda asl: lambda k: a1nb[:, C * asl + 128 * k:C * asl + 128 * (k + 1)])(asl),
                    None, [t_n2], ba1nT,
                    dst_view=a1nT.rearrange("p (k t) -> p k t", t=512)[:, :, 128 * j:128 * (j + 1)])
                ba1nb[asl].rc(ba1nT.w)

            def emit_H(half, m=m):
                h_last = None
                for dk in range(NCT):
                    bank, bankb = (PW, bPW) if dk % 2 == 0 else (PX, bPX)
                    mms = []
                    for ck in range(NCT):
                        deps = [ba1nT.w, bw1.w]
                        if ck == 0:
                            deps += bankb.wdeps()
                        mms.append(p.op("tensor", (lambda bank, dk, ck, half: lambda e: e.matmul(
                            bank[:, 0:256], w1_s[:, C * ck + 128 * dk:C * ck + 128 * (dk + 1)],
                            a1nT[:, 512 * ck + 256 * half:512 * ck + 256 * (half + 1)],
                            start=(ck == 0), stop=(ck == NCT - 1)))(bank, dk, ck, half),
                            deps=deps))
                    bankb.wc(mms[-1])
                    ba1nT.rc(mms[-1])
                    deps_e = [mms[-1], bb1c.w]
                    if dk == 0:
                        deps_e += bhT.wdeps()
                    h_last = p.op("scalar", (lambda bank, dk, half: lambda e: e.activation(
                        out=hT[:, 512 * dk + 256 * half:512 * dk + 256 * (half + 1)],
                        in_=bank[:, 0:256],
                        func=AF.Relu, bias=b1_c[:, dk:dk + 1]))(bank, dk, half), deps=deps_e)
                    bankb.rc(h_last)
                bhT.wc(h_last)
                if dbg:
                    t_dmp5 = p.dma("sync", "dbg", (lambda m, half: lambda e: e.dma_start(
                        out=HD.rearrange("dk q t -> q dk t")[:, :, MAC * m + 256 * half:MAC * m + 256 * (half + 1)],
                        in_=hT.rearrange("p (dk t) -> p dk t", dk=NCT)[:, :, 256 * half:256 * (half + 1)]))(m, half),
                        deps=bhT.r())
                    bhT.rc(t_dmp5)

            def emit_M(j, m=m):
                mms = []
                for h in range(2):
                    for dk in range(NCT):
                        deps = [bhT.w, bw2.w]
                        if h == 0 and dk == 0:
                            deps += bPB.wdeps()
                        mms.append(p.op("tensor", (lambda j, h, dk: lambda e: e.matmul(
                            PB[:, 512 * h:512 * (h + 1)],
                            hT[:, 512 * dk + 128 * j:512 * dk + 128 * (j + 1)],
                            w2_s[:, C * dk + 512 * h:C * dk + 512 * (h + 1)],
                            start=(dk == 0), stop=False))(j, h, dk), deps=deps))
                bhT.rc(mms[-1])
                for h in range(2):
                    mms.append(p.op("tensor", (lambda h: lambda e: e.matmul(
                        PB[:, 512 * h:512 * (h + 1)], onesb[0:1, :],
                        b2_rb[0:1, 512 * h:512 * (h + 1)], start=False, stop=True))(h),
                        deps=[bb2.w, ld["ONESB"]]))
                bPB.wc(mms[-1])
                t_a2 = p.op("vector", (lambda j: lambda e: e.tensor_add(
                    a1s(j), a1s(j),
                    PB[:, :]))(j),
                    deps=[mms[-1], ba1[j].w] + ba1[j].rs)
                bPB.rc(t_a2)
                ba1[j].wc(t_a2)
                if dbg:
                    t_dmp4 = p.dma("sync", "dbg", (lambda m, j: lambda e: e.dma_start(
                        out=A2D[MAC * m + MIC * j:MAC * m + MIC * (j + 1), :],
                        in_=a1s(j)))(m, j), deps=[t_a2])
                    ba1[j].rc(t_dmp4)
                t_x2 = p.op("vector", (lambda j: lambda e: e.tensor_add(
                    x1i2[:, C * j:C * (j + 1)], x1i2[:, C * j:C * (j + 1)],
                    a1s(j)))(j),
                    deps=[t_a2, bx1i2.w] + bx1i2.rs + ([t_dmp4] if dbg else []))
                ba1[j].rc(t_x2)
                bx1i2.wc(t_x2)
                if dbg:
                    t_dmp2 = p.dma("sync", "dbg", (lambda m, j: lambda e: e.dma_start(
                        out=X2D[MAC * m + MIC * j:MAC * m + MIC * (j + 1), :],
                        in_=x1i2[:, C * j:C * (j + 1)]))(m, j), deps=[bx1i2.w])
                    bx1i2.rc(t_dmp2)
            def emit_Mb(j):
                emit_lnT(x1i2[:, C * j:C * (j + 1)], [bx1i2.w], j, ln3T, bln3T,
                         ln3b, bln3b)
                bx1i2.rc(bln3T.w)

            ta1 = {}
            for j in (0, 1):
                ta1[j] = emit_Fa(j)
            for j in (0, 1):
                emit_Fb(j, ta1[j])
            for j in (2, 3):
                ta1[j] = emit_Fa(j)
            for j in (2, 3):
                emit_Fb(j, ta1[j])
            emit_H(0)
            emit_M(0)
            emit_M(1)
            emit_H(1)
            emit_Mb(0)
            emit_Mb(1)
            emit_M(2)
            emit_M(3)
            emit_Mb(2)
            emit_Mb(3)

            def b_out(j, rsl, vtok, sig_tok, rsb_ap, brsb_slot, m=m):
                osl = j % 2
                t_t3 = p.op("vector", (lambda rsb_ap, rsl: lambda e: e.scalar_tensor_tensor(
                    out=t3b2[:, C * rsl:C * (rsl + 1)], in0=PB[:, :], scalar=1.0,
                    in1=rsb_ap, op0=ALU.mult, op1=ALU.mult))(rsb_ap, rsl),
                    deps=[vtok, sig_tok] + bt32[rsl].wdeps())
                bPB.rc(t_t3)
                brsb_slot.rc(t_t3)
                bt32[rsl].wc(t_t3)
                t_o = p.op("vector", (lambda j, rsl, osl: lambda e: e.tensor_add(
                    outb_f[:, C * osl:C * (osl + 1)], x1i2[:, C * j:C * (j + 1)],
                    t3b2[:, C * rsl:C * (rsl + 1)]))(j, rsl, osl),
                    deps=[t_t3, bx1i2.w] + boutf[osl].wdeps())
                bt32[rsl].rc(t_o)
                bx1i2.rc(t_o)
                boutf[osl].wc(t_o)
                t_od = p.dma("sync", "oo", (lambda m, j, osl: lambda e: e.dma_start(
                    out=OUT[MAC * m + MIC * j:MAC * m + MIC * (j + 1), :],
                    in_=outb_f[:, C * osl:C * (osl + 1)]))(m, j, osl),
                    deps=[t_o])
                boutf[osl].rc(t_od)

            emit_ffn_mix(ln3T, bln3T, xrT2, bxr2, xkT2, bxk2, dscr_B, bdscr_B)
            emit_ffn_mms(xrT2, bxr2, xkT2, bxk2, kactT2, bkact2,
                         wr_s2, wk_s2, wv_s2, bwr2, bwk2, bwv2, b_out,
                         krelu_B, bkrelu_B, rsb_B, [bhT, bhT])

        p.op("sync", lambda e: e.nop(),
             deps=([("oo", p.dma_counts["oo"])] if p.dma_counts.get("oo") else p.barrier_tokens()) +
                  ([("dbg", p.dma_counts["dbg"])] if dbg else []))

        bstk.close()
        p.build_and_emit(root)
    return nc


def _prep_shared(inputs):
    f32 = np.float32
    ac = np.ascontiguousarray
    return {
        "WRT": ac(inputs["L0_ffn_W_r"].T.astype(f32)),
        "WKT": ac(inputs["L0_ffn_W_k"].T.astype(f32)),
        "WVT": ac(inputs["L0_ffn_W_v"].T.astype(f32)),
        "WG": ac(inputs["L0_attn_gates_w"].astype(f32)),
        "LRW": ac(inputs["L0_attn_lr_w"].reshape(1, -1).astype(f32)),
        "OUTW": ac(inputs["L0_attn_out_w"].astype(f32)),
        "W1": ac(inputs["L0_attn_w1"].astype(f32)),
        "W2": ac(inputs["L0_attn_w2"].astype(f32)),
        "DNG": ac(inputs["L0_attn_dn_g"].reshape(1, -1).astype(f32)),
        "DNB": ac(inputs["L0_attn_dn_b"].reshape(1, -1).astype(f32)),
        "MNG": ac(inputs["L0_attn_mn_g"].reshape(1, -1).astype(f32)),
        "MNB": ac(inputs["L0_attn_mn_b"].reshape(1, -1).astype(f32)),
        "MUR": ac(inputs["L0_ffn_mu_r"].reshape(1, -1).astype(f32)),
        "MUK": ac(inputs["L0_ffn_mu_k"].reshape(1, -1).astype(f32)),
        "GATESB": ac(inputs["L0_attn_gates_b"].reshape(1, -1).astype(f32)),
        "OUTB": ac(inputs["L0_attn_out_b"].reshape(1, -1).astype(f32)),
        "B1": ac(inputs["L0_attn_b1"].reshape(1, -1).astype(f32)),
        "B2": ac(inputs["L0_attn_b2"].reshape(1, -1).astype(f32)),
        "LRB": ac(inputs["L0_attn_lr_b"].reshape(1, -1).astype(f32)),
        "IDENT_F": np.eye(128, dtype=f32),
        "IDENT_B": np.eye(128).astype(ml_dtypes.bfloat16),
        "ONESB": np.ones((1, 128)).astype(ml_dtypes.bfloat16),
        "LMASK": np.tril(np.ones((128, 128), dtype=f32), -1),
        "TWOI": (2.0 * np.eye(128)).astype(ml_dtypes.bfloat16),
    }


def kernel(**inputs):
    x = np.ascontiguousarray(inputs["L0_attn_x"], dtype=np.float32)
    B = x.shape[0]
    shared = _prep_shared(inputs)
    nc = build(T=x.shape[1])
    in_maps = [dict(shared, X=np.ascontiguousarray(x[b])) for b in range(B)]
    res = run_bass_kernel_spmd(nc, in_maps, core_ids=list(range(B)))
    return np.stack([np.asarray(res.results[b]["OUT"], dtype=np.float32)
                     for b in range(B)], axis=0)


# version 16
# speedup vs baseline: 1.2649x; 1.0291x over previous
import contextlib
from collections import defaultdict

import numpy as np
import ml_dtypes

import concourse.bass as bass
import concourse.mybir as mybir
from concourse.bass_utils import run_bass_kernel_spmd

F32 = mybir.dt.float32
BF16 = mybir.dt.bfloat16
AF = mybir.ActivationFunctionType
ALU = mybir.AluOpType
AX = mybir.AxisListType

C = 1024
NCT = 8
MAC = 512
MIC = 128
EPS = 1e-5

ENGINES = ("tensor", "vector", "scalar", "gpsimd", "sync")


class _Op:
    __slots__ = ("idx", "engine", "fn", "deps", "seq", "dma")

    def __init__(self, idx, engine, fn, deps, seq, dma):
        self.idx = idx
        self.engine = engine
        self.fn = fn
        self.deps = deps
        self.seq = seq
        self.dma = dma


class Prog:
    def __init__(self, nc):
        self.nc = nc
        self.ops = []
        self.counts = {e: 0 for e in ENGINES}
        self.dma_counts = defaultdict(int)
        self.dma_sem_names = []

    def op(self, engine, fn, deps=()):
        self.counts[engine] += 1
        o = _Op(len(self.ops), engine, fn, self._norm(deps), self.counts[engine], None)
        self.ops.append(o)
        return (engine, self.counts[engine])

    def dma(self, queue, dma_sem, fn, deps=()):
        if dma_sem not in self.dma_counts:
            self.dma_sem_names.append(dma_sem)
        self.dma_counts[dma_sem] += 16
        o = _Op(len(self.ops), queue, fn, self._norm(deps), self.dma_counts[dma_sem], dma_sem)
        self.ops.append(o)
        return (dma_sem, self.dma_counts[dma_sem])

    @staticmethod
    def _norm(deps):
        out = []
        seen = set()
        for d in deps:
            if d is None or d in seen:
                continue
            seen.add(d)
            out.append(d)
        return out

    def barrier_tokens(self):
        toks = [(e, self.counts[e]) for e in ENGINES if self.counts[e] > 0]
        toks += [(s, self.dma_counts[s]) for s in self.dma_sem_names]
        return toks

    def hard_barrier(self):
        toks = self.barrier_tokens()
        for e in ENGINES:
            if self.counts[e] > 0:
                self.op(e, lambda eng: eng.nop(), deps=toks)
        return self.barrier_tokens()

    def emit(self, block, sems, dma_sems):
        waited = defaultdict(set)
        for o in self.ops:
            for (s, v) in o.deps:
                waited[s].add(v)
        streams = {e: [o for o in self.ops if o.engine == e] for e in ENGINES}
        dq = {}
        for o in self.ops:
            if o.dma:
                q = dq.setdefault(o.dma, o.engine)
                assert q == o.engine, f"dma sem {o.dma} used from two queues"
        dcross = {s: sorted(waited[s]) for s in self.dma_sem_names}

        def emit_stream(eng_api, engine):
            max_w = defaultdict(int)
            crossing = sorted(waited[engine])
            ci = 0
            count = 0
            dcnt = defaultdict(int)
            dci = defaultdict(int)
            for o in streams[engine]:
                if not o.dma:
                    while ci < len(crossing) and crossing[ci] <= count:
                        v = crossing[ci]
                        if max_w[engine] < v < o.seq:
                            eng_api.wait_ge(sems[engine], v)
                            max_w[engine] = v
                        ci += 1
                else:
                    s = o.dma
                    cr = dcross[s]
                    while dci[s] < len(cr) and cr[dci[s]] <= dcnt[s]:
                        v = cr[dci[s]]
                        if max_w[s] < v:
                            eng_api.wait_ge(dma_sems[s], v)
                            max_w[s] = v
                        dci[s] += 1
                    dcnt[s] += 16
                for (s, v) in o.deps:
                    if v > max_w[s]:
                        eng_api.wait_ge(sems[s] if s in ENGINES else dma_sems[s], v)
                        max_w[s] = v
                inst = o.fn(eng_api)
                if o.dma:
                    inst.then_inc(dma_sems[o.dma], 16)
                else:
                    inst.then_inc(sems[engine], 1)
                    count += 1

        for engine in ENGINES:
            if not streams[engine]:
                continue
            section = getattr(block, engine)

            def make(engine):
                def _(eng_api):
                    emit_stream(eng_api, engine)
                return _
            section(make(engine))

    def build_and_emit(self, stack):
        nc = self.nc
        sems = {}
        for e in ENGINES:
            if any(o.engine == e and not o.dma for o in self.ops):
                sems[e] = stack.enter_context(nc.semaphore(f"sem_{e}"))
        dma_sems = {}
        for name in self.dma_sem_names:
            dma_sems[name] = stack.enter_context(nc.semaphore(f"ds_{name}"))
        block = stack.enter_context(nc.Block())
        self.emit(block, sems, dma_sems)


class Buf:
    __slots__ = ("w", "rs", "ro")

    def __init__(self, readonly=False):
        self.w = None
        self.rs = []
        self.ro = readonly

    def r(self):
        return [self.w] if self.w is not None else []

    def rc(self, tok):
        if not self.ro:
            self.rs.append(tok)

    def wdeps(self):
        d = list(self.rs)
        if self.w is not None:
            d.append(self.w)
        return d

    def wc(self, tok):
        self.w = tok
        self.rs = []


def build(T=8192, dbg=False, stop_after=None, a1_parts=127, b_parts=255):
    nc = bass.Bass("TRN2", target_bir_lowering=False)
    NMAC = T // MAC

    X = nc.dram_tensor("X", [T, C], F32, kind="ExternalInput")
    OUT = nc.dram_tensor("OUT", [T, C], F32, kind="ExternalOutput")
    WRT = nc.dram_tensor("WRT", [C, C], F32, kind="ExternalInput")
    WKT = nc.dram_tensor("WKT", [C, C], F32, kind="ExternalInput")
    WVT = nc.dram_tensor("WVT", [C, C], F32, kind="ExternalInput")
    WG = nc.dram_tensor("WG", [C, 3 * C], F32, kind="ExternalInput")
    LRW = nc.dram_tensor("LRW", [1, C], F32, kind="ExternalInput")
    OUTW = nc.dram_tensor("OUTW", [C, C], F32, kind="ExternalInput")
    W1 = nc.dram_tensor("W1", [C, C], F32, kind="ExternalInput")
    W2 = nc.dram_tensor("W2", [C, C], F32, kind="ExternalInput")
    rows_in = {}
    for nm, n in [("DNG", C), ("DNB", C), ("MNG", C), ("MNB", C), ("MUR", C),
                  ("MUK", C), ("GATESB", 3 * C), ("OUTB", C), ("B1", C), ("B2", C),
                  ("LRB", 1)]:
        rows_in[nm] = nc.dram_tensor(nm, [1, n], F32, kind="ExternalInput")
    IDENT_F = nc.dram_tensor("IDENT_F", [128, 128], F32, kind="ExternalInput")
    IDENT_B = nc.dram_tensor("IDENT_B", [128, 128], BF16, kind="ExternalInput")
    ONESB = nc.dram_tensor("ONESB", [1, 128], BF16, kind="ExternalInput")
    LMASK = nc.dram_tensor("LMASK", [128, 128], F32, kind="ExternalInput")
    TWOI = nc.dram_tensor("TWOI", [128, 128], BF16, kind="ExternalInput")
    skind = {"kind": "ExternalOutput"} if dbg else {}
    X1S = nc.dram_tensor("X1S", [T, C], F32, **skind)
    QTS = nc.dram_tensor("QTS", [NCT, 128, T], BF16, **skind)
    if dbg:
        SD = nc.dram_tensor("SD", [128, NCT * C], F32, kind="ExternalOutput")
        GD = nc.dram_tensor("GD", [128, NCT * C], F32, kind="ExternalOutput")
        A1D = nc.dram_tensor("A1D", [T, C], F32, kind="ExternalOutput")
        X2D = nc.dram_tensor("X2D", [T, C], F32, kind="ExternalOutput")
        YD = nc.dram_tensor("YD", [NCT, 128, T], BF16, kind="ExternalOutput")
        A2D = nc.dram_tensor("A2D", [T, C], F32, kind="ExternalOutput")
        HD = nc.dram_tensor("HD", [NCT, 128, T], BF16, kind="ExternalOutput")

    p = Prog(nc)
    root = contextlib.ExitStack()

    def SB(stk, n, sh, dt):
        return stk.enter_context(nc.sbuf_tensor(n, sh, dt))

    with root:
        ident_f = SB(root, "ident_f", [128, 128], F32)
        ident_b = SB(root, "ident_b", [128, 128], BF16)
        onesb = SB(root, "onesb", [1, 128], BF16)
        lmask = SB(root, "lmask", [128, 128], F32)
        twoi = SB(root, "twoi", [128, 128], BF16)
        twoi4 = SB(root, "twoi4", [128, 512], BF16)
        ccols = SB(root, "ccols", [128, 8], F32)
        onef = SB(root, "onef", [1, 1], F32)
        dng_c = SB(root, "dng_c", [128, NCT], F32)
        mng_c = SB(root, "mng_c", [128, NCT], F32)
        dnb_cb = SB(root, "dnb_cb", [128, NCT], BF16)
        mnb_cb = SB(root, "mnb_cb", [128, NCT], BF16)
        mur_c = SB(root, "mur_c", [128, NCT], F32)
        muk_c = SB(root, "muk_c", [128, NCT], F32)
        qb_c = SB(root, "qb_c", [128, NCT], F32)
        kb_c = SB(root, "kb_c", [128, NCT], F32)
        b1_c = SB(root, "b1_c", [128, NCT], F32)
        lrb_bc = SB(root, "lrb_bc", [128, 1], F32)
        lrw_b = SB(root, "lrw_b", [128, NCT], BF16)
        browv_b = SB(root, "browv_b", [1, C], BF16)
        outb_rb = SB(root, "outb_rb", [1, C], BF16)
        b2_rb = SB(root, "b2_rb", [1, C], BF16)
        stt = SB(root, "stt", [128, 32], F32)
        sqd = SB(root, "sqd", [128, 2 * C], BF16)

        PA = root.enter_context(nc.psum_tensor("PA", [128, 1024], F32))
        PB = root.enter_context(nc.psum_tensor("PB", [128, 1024], F32))
        PW = root.enter_context(nc.psum_tensor("PW", [128, 512], F32))
        PX = root.enter_context(nc.psum_tensor("PX", [128, 512], F32))
        PT = root.enter_context(nc.psum_tensor("PT", [128, 1024], BF16))
        PS = root.enter_context(nc.psum_tensor("PS", [128, 512], F32))
        bPA, bPB, bPW, bPX, bPT, bPS = (Buf() for _ in range(6))

        bstt = [Buf() for _ in range(4)]
        bsqd = [Buf(), Buf()]

        bccols = Buf()
        btwoi4 = Buf()
        bonef = Buf()
        bdng, bmng, bdnbb, bmnbb, bmur, bmuk = (Buf() for _ in range(6))
        bqb, bkb, bb1c, blrb, blrwb, bbrowv, boutb, bb2 = (Buf() for _ in range(8))

        gstk = contextlib.ExitStack()
        wg_s = SB(gstk, "wg_s", [128, NCT * 3 * C], BF16)
        bwg = Buf(readonly=True)
        fwstk = contextlib.ExitStack()
        wr_s = SB(fwstk, "wr_s", [128, NCT * C], BF16)
        wk_s = SB(fwstk, "wk_s", [128, NCT * C], BF16)
        wv_s = SB(fwstk, "wv_s", [128, NCT * C], BF16)
        bwr, bwk, bwv = (Buf(readonly=True) for _ in range(3))
        initstk = contextlib.ExitStack()
        rowbuf = SB(initstk, "rowbuf", [1, 3 * C], F32)
        brow = Buf()
        brow2t = SB(initstk, "brow2", [1, 3 * C], F32)
        bbrow2 = Buf()
        colf = SB(initstk, "colf", [128, NCT], F32)
        bcolf = Buf()
        wstage = SB(initstk, "wstage", [128, 3 * C], F32)
        wstage2 = SB(initstk, "wstage2", [128, 3 * C], F32)
        wraw = SB(initstk, "wraw", [128, 3 * C], BF16)
        bwstage, bwstage2, bwraw = Buf(), Buf(), Buf()
        wstages = [(wstage, bwstage), (wstage2, bwstage2)]

        ld = {}
        for nm, dst, src in [("IDENT_F", ident_f, IDENT_F), ("IDENT_B", ident_b, IDENT_B),
                             ("ONESB", onesb, ONESB), ("LMASK", lmask, LMASK),
                             ("TWOI", twoi, TWOI)]:
            ld[nm] = p.dma("sync", "cl", (lambda dst, src: lambda e: e.dma_start(
                out=dst[:, :], in_=src[:, :]))(dst, src))
        cc_toks = []
        for i, v in [(0, EPS), (1, -0.5), (2, 0.5), (3, -1.0), (4, 1.0 + EPS),
                     (5, EPS * EPS)]:
            cc_toks.append(p.op("vector", (lambda i, v: lambda e: e.memset(
                ccols[:, i:i + 1], v))(i, v)))
        bccols.wc(cc_toks[-1])
        cc = cc_toks[-1]
        t_onef = p.op("vector", lambda e: e.memset(onef[0:1, 0:1], 1.0))
        bonef.wc(t_onef)
        tw_toks = []
        for i in range(4):
            tw_toks.append(p.op("vector", (lambda i: lambda e: e.tensor_copy(
                twoi4[:, 128 * i:128 * (i + 1)], twoi[:, :]))(i),
                deps=[ld["TWOI"]] + (btwoi4.wdeps() if i == 0 else [])))
        btwoi4.wc(tw_toks[-1])

        def load_row(nm, n):
            tl = p.dma("sync", "cl", (lambda nm, n: lambda e: e.dma_start(
                out=rowbuf[0:1, 0:n], in_=rows_in[nm][0:1, 0:n]))(nm, n),
                deps=brow.wdeps())
            brow.wc(tl)
            return tl

        def row_to_cols(row_ap, row_tok, n, dst, bdst):
            mms = []
            for k in range(n // 128):
                deps = [row_tok, t_onef]
                if k == 0:
                    deps += bPS.wdeps()
                mms.append(p.op("tensor", (lambda k, row_ap: lambda e: e.matmul(
                    PS[:, k:k + 1], row_ap[0:1, 128 * k:128 * (k + 1)],
                    onef[0:1, 0:1], start=True, stop=True))(k, row_ap), deps=deps))
            bPS.wc(mms[-1])
            t_e = p.op("vector", (lambda n: lambda e: e.tensor_copy(
                dst[:, 0:n // 128], PS[:, 0:n // 128]))(n),
                deps=[mms[-1]] + bdst.wdeps())
            bPS.rc(t_e)
            bdst.wc(t_e)
            return t_e

        for nm, dst, bdst in [("DNG", dng_c, bdng), ("MNG", mng_c, bmng),
                              ("MUR", mur_c, bmur), ("MUK", muk_c, bmuk)]:
            tl = load_row(nm, C)
            te = row_to_cols(rowbuf, tl, C, dst, bdst)
            brow.rc(te)
        for nm, dstb, bdstb in [("DNB", dnb_cb, bdnbb), ("MNB", mnb_cb, bmnbb)]:
            tl = load_row(nm, C)
            te = row_to_cols(rowbuf, tl, C, colf, bcolf)
            brow.rc(te)
            tc = p.op("scalar", (lambda dstb: lambda e: e.activation(
                out=dstb[:, :], in_=colf[:, :], func=AF.Identity))(dstb),
                deps=[te] + bdstb.wdeps())
            bcolf.rc(tc)
            bdstb.wc(tc)

        fold_regions = [PA[0:1, 0:512], PA[0:1, 512:1024], PB[0:1, 0:512],
                        PB[0:1, 512:1024], PW[0:1, 0:512], PX[0:1, 0:512]]
        fold_banks = [bPA, bPA, bPB, bPB, bPW, bPX]
        gate_fold = [None] * 6
        wg_last = None
        for ck in range(NCT):
            wst, bwst = wstages[ck % 2]
            dl = p.dma("scalar", "wl", (lambda ck, wst: lambda e: e.dma_start(
                out=wst[:, :], in_=WG[128 * ck:128 * (ck + 1), :]))(ck, wst),
                deps=bwst.wdeps())
            bwst.wc(dl)
            tr = p.op("scalar", (lambda wst: lambda e: e.activation(
                out=wraw[:, :], in_=wst[:, :], func=AF.Identity))(wst),
                      deps=[dl] + bwraw.wdeps())
            bwraw.wc(tr)
            bwst.rc(tr)
            for h in range(6):
                deps = [tr, bdnbb.w]
                if ck == 0:
                    deps += fold_banks[h].wdeps()
                mm = p.op("tensor", (lambda h, ck: lambda e: e.matmul(
                    fold_regions[h], dnb_cb[:, ck:ck + 1], wraw[:, 512 * h:512 * (h + 1)],
                    start=(ck == 0), stop=(ck == NCT - 1)))(h, ck), deps=deps)
                bwraw.rc(mm)
                if ck == NCT - 1:
                    fold_banks[h].wc(mm)
                    gate_fold[h] = mm
            ts = p.op("scalar", (lambda ck, wst: lambda e: e.activation(
                out=wg_s[:, 3 * C * ck:3 * C * (ck + 1)], in_=wst[:, :],
                func=AF.Identity, scale=dng_c[:, ck:ck + 1]))(ck, wst),
                deps=[dl, bdng.w])
            bwst.rc(ts)
            bdng.rc(ts)
            wg_last = ts
        bwg.w = wg_last
        t_gb = load_row("GATESB", 3 * C)
        ev_last = None
        for h in range(6):
            ev = p.op("vector", (lambda h: lambda e: e.tensor_add(
                brow2t[0:1, 512 * h:512 * (h + 1)], rowbuf[0:1, 512 * h:512 * (h + 1)],
                fold_regions[h]))(h),
                deps=[t_gb, gate_fold[h]] + (bbrow2.wdeps() if h == 0 else []))
            fold_banks[h].rc(ev)
            brow.rc(ev)
            ev_last = ev
        bbrow2.wc(ev_last)
        t_bv = p.op("scalar", lambda e: e.activation(out=browv_b[0:1, :],
                                                     in_=brow2t[0:1, 2 * C:3 * C],
                                                     func=AF.Identity), deps=[ev_last])
        bbrow2.rc(t_bv)
        bbrowv.wc(t_bv)
        t_qbc = row_to_cols(brow2t, ev_last, C, qb_c, bqb)
        bbrow2.rc(t_qbc)
        mms = []
        for k in range(NCT):
            deps = [ev_last, t_onef]
            if k == 0:
                deps += bPS.wdeps()
            mms.append(p.op("tensor", (lambda k: lambda e: e.matmul(
                PS[:, k:k + 1], brow2t[0:1, C + 128 * k:C + 128 * (k + 1)],
                onef[0:1, 0:1], start=True, stop=True))(k), deps=deps))
        bPS.wc(mms[-1])
        t_kbc = p.op("vector", lambda e: e.tensor_copy(kb_c[:, 0:NCT], PS[:, 0:NCT]),
                     deps=[mms[-1]] + bkb.wdeps())
        bPS.rc(t_kbc)
        bkb.wc(t_kbc)
        bbrow2.rc(t_kbc)

        tl = p.dma("sync", "cl", lambda e: e.dma_start(out=rowbuf[0:1, 0:C],
                                                       in_=LRW[0:1, 0:C]),
                   deps=brow.wdeps())
        brow.wc(tl)
        t_lrwf = row_to_cols(rowbuf, tl, C, colf, bcolf)
        brow.rc(t_lrwf)
        t_lrwb = p.op("vector", lambda e: e.tensor_mul(lrw_b[:, :], colf[:, :],
                                                       dng_c[:, :]),
                      deps=[t_lrwf, bdng.w] + blrwb.wdeps())
        bcolf.rc(t_lrwb)
        bdng.rc(t_lrwb)
        blrwb.wc(t_lrwb)
        lrwr_b = SB(initstk, "lrwr_b", [128, NCT], BF16)
        blrwr = Buf()
        t_lrwr = p.op("scalar", lambda e: e.activation(out=lrwr_b[:, :], in_=colf[:, :],
                                                       func=AF.Identity), deps=[t_lrwf])
        bcolf.rc(t_lrwr)
        blrwr.wc(t_lrwr)
        lrf_mm = None
        for ck in range(NCT):
            deps = [bdnbb.w, t_lrwr]
            if ck == 0:
                deps += bPS.wdeps()
            lrf_mm = p.op("tensor", (lambda ck: lambda e: e.matmul(
                PS[0:1, 0:1], dnb_cb[:, ck:ck + 1], lrwr_b[:, ck:ck + 1],
                start=(ck == 0), stop=(ck == NCT - 1)))(ck), deps=deps)
        bPS.wc(lrf_mm)
        t_lrb = load_row("LRB", 1)
        lrb_row = SB(initstk, "lrb_row", [1, 2], F32)
        lrb_rowb = SB(initstk, "lrb_rowb", [1, 1], BF16)
        blrr = Buf()
        t1 = p.op("vector", lambda e: e.tensor_add(lrb_row[0:1, 0:1], rowbuf[0:1, 0:1],
                                                   PS[0:1, 0:1]),
                  deps=[t_lrb, lrf_mm] + blrr.wdeps())
        bPS.rc(t1)
        brow.rc(t1)
        blrr.wc(t1)
        t2 = p.op("scalar", lambda e: e.activation(out=lrb_rowb[0:1, 0:1],
                                                   in_=lrb_row[0:1, 0:1], func=AF.Identity),
                  deps=[t1])
        blrr.rc(t2)
        mmb = p.op("tensor", lambda e: e.matmul(PS[:, 0:1], onesb[0:1, :],
                                                lrb_rowb[0:1, 0:1], start=True, stop=True),
                   deps=[t2, ld["ONESB"]] + bPS.wdeps())
        bPS.wc(mmb)
        t_lrbb = p.op("scalar", lambda e: e.activation(out=lrb_bc[:, :], in_=PS[:, 0:1],
                                                       func=AF.Identity),
                      deps=[mmb] + blrb.wdeps())
        bPS.rc(t_lrbb)
        blrb.wc(t_lrbb)

        t_ob = load_row("OUTB", C)
        t_obb = p.op("scalar", lambda e: e.activation(out=outb_rb[0:1, :],
                                                      in_=rowbuf[0:1, 0:C], func=AF.Identity),
                     deps=[t_ob])
        brow.rc(t_obb)
        boutb.wc(t_obb)
        t_b2 = load_row("B2", C)
        t_b2b = p.op("scalar", lambda e: e.activation(out=b2_rb[0:1, :],
                                                      in_=rowbuf[0:1, 0:C], func=AF.Identity),
                     deps=[t_b2])
        brow.rc(t_b2b)
        bb2.wc(t_b2b)

        b1f = [None, None]
        for ck in range(NCT):
            wst, bwst = wstages[ck % 2]
            dl = p.dma("scalar", "wl", (lambda ck, wst: lambda e: e.dma_start(
                out=wst[:, 0:C], in_=W1[128 * ck:128 * (ck + 1), :]))(ck, wst),
                deps=bwst.wdeps())
            bwst.wc(dl)
            tr = p.op("scalar", (lambda wst: lambda e: e.activation(
                out=wraw[:, 0:C], in_=wst[:, 0:C], func=AF.Identity))(wst),
                      deps=[dl] + bwraw.wdeps())
            bwraw.wc(tr)
            bwst.rc(tr)
            for h in range(2):
                deps = [tr, bmnbb.w]
                if ck == 0:
                    deps += (bPA.wdeps() if h == 0 else bPB.wdeps())
                fm = p.op("tensor", (lambda h, ck: lambda e: e.matmul(
                    (PA if h == 0 else PB)[0:1, 0:512], mnb_cb[:, ck:ck + 1],
                    wraw[:, 512 * h:512 * (h + 1)], start=(ck == 0),
                    stop=(ck == NCT - 1)))(h, ck), deps=deps)
                bwraw.rc(fm)
                if ck == NCT - 1:
                    (bPA if h == 0 else bPB).wc(fm)
                    b1f[h] = fm
        t_b1 = load_row("B1", C)
        t_bf0 = p.op("vector", lambda e: e.tensor_add(
            brow2t[0:1, 0:512], rowbuf[0:1, 0:512], PA[0:1, 0:512]),
            deps=[t_b1, b1f[0]] + bbrow2.wdeps())
        bPA.rc(t_bf0)
        brow.rc(t_bf0)
        t_bf1 = p.op("vector", lambda e: e.tensor_add(
            brow2t[0:1, 512:1024], rowbuf[0:1, 512:1024], PB[0:1, 0:512]),
            deps=[t_b1, b1f[1], t_bf0])
        bPB.rc(t_bf1)
        brow.rc(t_bf1)
        bbrow2.wc(t_bf1)
        t_b1c = row_to_cols(brow2t, t_bf1, C, b1_c, bb1c)
        bbrow2.rc(t_b1c)

        def load_ffn(dst, bdst, srcd, half=False):
            last = None
            for ck in range(NCT):
                wst, bwst = wstages[ck % 2]
                dl = p.dma("scalar", "wl", (lambda ck, srcd, wst: lambda e: e.dma_start(
                    out=wst[:, 0:C],
                    in_=srcd[128 * ck:128 * (ck + 1), :]))(ck, srcd, wst),
                    deps=bwst.wdeps())
                bwst.wc(dl)
                if half:
                    tc = p.op("scalar", (lambda ck, dst, wst: lambda e: e.activation(
                        out=dst[:, C * ck:C * (ck + 1)], in_=wst[:, 0:C],
                        func=AF.Identity, scale=ccols[:, 2:3]))(ck, dst, wst),
                        deps=[dl, cc])
                else:
                    tc = p.op("scalar", (lambda ck, dst, wst: lambda e: e.activation(
                        out=dst[:, C * ck:C * (ck + 1)], in_=wst[:, 0:C],
                        func=AF.Identity))(ck, dst, wst),
                        deps=[dl])
                bwst.rc(tc)
                last = tc
            bdst.w = last
            return last

        t_wr = load_ffn(wr_s, bwr, WRT)
        t_wk = load_ffn(wk_s, bwk, WKT)
        t_wv = load_ffn(wv_s, bwv, WVT, half=True)

        bar_p0 = p.barrier_tokens()
        initstk.close()
        a0stk = contextlib.ExitStack()
        xrT = SB(a0stk, "xrT", [128, 2 * NCT * 512], BF16)
        xkT = SB(a0stk, "xkT", [128, 2 * NCT * 512], BF16)
        bxr = [Buf(), Buf()]
        bxk = [Buf(), Buf()]
        kactT = SB(a0stk, "kactT", [128, NCT * 512], BF16)
        bkact = Buf()
        x1b = SB(a0stk, "x1b", [128, 2 * C], F32)
        bx1 = [Buf(), Buf()]
        r_sb = SB(a0stk, "r_sb", [128, 2 * C], F32)
        br_sb = [Buf(), Buf()]
        krelu = SB(a0stk, "krelu", [128, 1024], F32)
        bkrelu = Buf()
        dscr = SB(a0stk, "dscr", [128, 1024], BF16)
        bdscr = Buf()
        for _b in (bxr[0], bxr[1], bxk[0], bxk[1], bkact, bx1[0], bx1[1],
                   br_sb[0], br_sb[1], bkrelu, bdscr):
            _b.rs = list(bar_p0)
        x0b = SB(a0stk, "x0b", [128, 2 * C], F32)
        bx0 = [Buf(), Buf()]
        x0add = SB(a0stk, "x0add", [128, 2 * C], F32)
        bx0add = [Buf(), Buf()]
        lnb = SB(a0stk, "lnb", [128, 4 * C], BF16)
        blnb = [Buf(), Buf(), Buf(), Buf()]
        lnT = SB(a0stk, "lnT", [128, NCT * 513], BF16)
        blnT = Buf()

        def transpose8(src_fn, dst_fn, deps_src, bdst, dst_view=None):
            trs = []
            wd = bPT.wdeps()
            for k in range(8):
                deps = list(deps_src) + [ld["IDENT_B"]]
                if k == 0:
                    deps += wd
                trs.append(p.op("tensor", (lambda k: lambda e: e.transpose(
                    PT[:, 128 * k:128 * (k + 1)], src_fn(k), ident_b[:, :]))(k),
                    deps=deps))
            bPT.wc(trs[-1])
            wd2 = bdst.wdeps()
            if dst_view is not None:
                evs_last = p.op("vector", (lambda dst_view: lambda e: e.tensor_copy(
                    dst_view, PT.rearrange("p (k q) -> p k q", q=128)))(dst_view),
                    deps=[trs[-1]] + wd2)
            else:
                evs_last = None
                for k in range(8):
                    deps = [trs[-1]]
                    if k == 0:
                        deps += wd2
                    evs_last = p.op("vector", (lambda k: lambda e: e.tensor_copy(
                        dst_fn(k), PT[:, 128 * k:128 * (k + 1)]))(k), deps=deps)
            bPT.rc(evs_last)
            bdst.wc(evs_last)
            return evs_last

        def ln_stats(src_ap, deps_x, slot, second_ln=False):
            s0 = 8 * slot
            bs = bstt[slot]
            t_sum = p.op("vector", (lambda src_ap, s0: lambda e: e.reduce_sum(
                stt[:, s0:s0 + 1], src_ap, axis=AX.X))(src_ap, s0),
                deps=deps_x + bs.wdeps())
            t_sq = p.op("scalar", (lambda src_ap, s0: lambda e: e.activation(
                out=sqd[:, 0:C], in_=src_ap, func=AF.Square,
                accum_out=stt[:, s0 + 1:s0 + 2]))(src_ap, s0),
                deps=deps_x + bsqd[0].wdeps())
            bsqd[0].wc(t_sq)
            t_mean = p.op("vector", (lambda s0: lambda e: e.tensor_scalar(
                out=stt[:, s0 + 2:s0 + 3], in0=stt[:, s0:s0 + 1], scalar1=1.0 / C,
                scalar2=None, op0=ALU.mult))(s0), deps=[t_sum])
            t_mm2 = p.op("vector", (lambda s0: lambda e: e.tensor_mul(
                stt[:, s0 + 3:s0 + 4], stt[:, s0 + 2:s0 + 3], stt[:, s0 + 2:s0 + 3]))(s0),
                deps=[t_mean])
            t_var = p.op("vector", (lambda s0: lambda e: e.tensor_scalar(
                out=stt[:, s0 + 4:s0 + 5], in0=stt[:, s0 + 1:s0 + 2], scalar1=1.0 / C,
                scalar2=stt[:, s0 + 3:s0 + 4], op0=ALU.mult, op1=ALU.subtract))(s0),
                deps=[t_sq, t_mm2])
            if not second_ln:
                t_ln = p.op("scalar", (lambda s0: lambda e: e.activation(
                    out=stt[:, s0 + 5:s0 + 6], in_=stt[:, s0 + 4:s0 + 5], func=AF.Ln,
                    bias=ccols[:, 0:1]))(s0), deps=[t_var, cc])
            else:
                t_ln = p.op("scalar", (lambda s0: lambda e: e.activation(
                    out=stt[:, s0 + 5:s0 + 6], in_=stt[:, s0 + 4:s0 + 5], func=AF.Ln,
                    scale=ccols[:, 4:5], bias=ccols[:, 5:6]))(s0), deps=[t_var, cc])
            t_sc = p.op("scalar", (lambda s0: lambda e: e.activation(
                out=stt[:, s0 + 6:s0 + 7], in_=stt[:, s0 + 5:s0 + 6], func=AF.Exp,
                scale=ccols[:, 1:2]))(s0), deps=[t_ln])
            t_nb = p.op("vector", (lambda s0: lambda e: e.tensor_scalar(
                out=stt[:, s0 + 7:s0 + 8], in0=stt[:, s0 + 2:s0 + 3],
                scalar1=stt[:, s0 + 6:s0 + 7], scalar2=-1.0, op0=ALU.mult,
                op1=ALU.mult))(s0), deps=[t_sc, t_mean])
            bs.wc(t_nb)
            return [t_sc, t_nb], stt[:, s0 + 6:s0 + 7], stt[:, s0 + 7:s0 + 8]

        def stats_norm(src_ap, deps_x, j, lnb_t, blnb_t, sl, second_ln=False):
            stoks, sc_ap, nb_ap = ln_stats(src_ap, deps_x, j % 4, second_ln)
            t_n = p.op("scalar", (lambda src_ap, sc_ap, nb_ap, sl: lambda e:
                       e.activation(out=lnb_t[:, C * sl:C * (sl + 1)], in_=src_ap,
                                    func=AF.Identity, scale=sc_ap, bias=nb_ap))(
                           src_ap, sc_ap, nb_ap, sl),
                       deps=stoks + deps_x + blnb_t[sl].wdeps())
            blnb_t[sl].wc(t_n)
            bstt[j % 4].rc(t_n)
            return t_n

        def lnb_to_lnT(j, lnT_t, blnT_t, lnb_t, blnb_t, sl, t_n):
            transpose8(
                (lambda sl: lambda k: lnb_t[:, C * sl + 128 * k:C * sl + 128 * (k + 1)])(sl),
                None, [t_n], blnT_t,
                dst_view=lnT_t.rearrange("p (k t) -> p k t", t=513)[:, :, 1 + 128 * j:1 + 128 * (j + 1)])
            blnb_t[sl].rc(blnT_t.w)

        def emit_lnT(src_ap, deps_x, j, lnT_t, blnT_t, lnb_t, blnb_t, second_ln=False):
            sl = j % 2
            t_n = stats_norm(src_ap, deps_x, j, lnb_t, blnb_t, sl, second_ln)
            lnb_to_lnT(j, lnT_t, blnT_t, lnb_t, blnb_t, sl, t_n)
            return t_n

        def emit_ffn_mix(lnT_t, blnT_t, xrT_t, bxr_t, xkT_t, bxk_t,
                         dscr_t, bdscr_t):
            mix_last = None
            xr_last = None
            for k in range(NCT):
                cur = lnT_t[:, 513 * k + 1:513 * k + 513]
                shf = lnT_t[:, 513 * k:513 * k + 512]
                t_d = p.op("vector", (lambda cur, shf, k: lambda e: e.tensor_sub(
                    dscr_t[:, 512 * (k % 2):512 * (k % 2 + 1)], cur, shf))(cur, shf, k),
                    deps=blnT_t.r() + bdscr_t.wdeps())
                bdscr_t.wc(t_d)
                blnT_t.rc(t_d)
                t_xr = p.op("vector", (lambda k, shf: lambda e: e.scalar_tensor_tensor(
                    out=xrT_t[:, 512 * k:512 * (k + 1)],
                    in0=dscr_t[:, 512 * (k % 2):512 * (k % 2 + 1)],
                    scalar=mur_c[:, k:k + 1], in1=shf, op0=ALU.mult, op1=ALU.add))(k, shf),
                    deps=[t_d, bmur.w] + (bxr_t.wdeps() if k == 0 else []))
                t_xk = p.op("vector", (lambda k, shf: lambda e: e.scalar_tensor_tensor(
                    out=xkT_t[:, 512 * k:512 * (k + 1)],
                    in0=dscr_t[:, 512 * (k % 2):512 * (k % 2 + 1)],
                    scalar=muk_c[:, k:k + 1], in1=shf, op0=ALU.mult, op1=ALU.add))(k, shf),
                    deps=[t_d, bmuk.w] + (bxk_t.wdeps() if k == 0 else []))
                bdscr_t.rc(t_xr)
                bdscr_t.rc(t_xk)
                blnT_t.rc(t_xr)
                blnT_t.rc(t_xk)
                xr_last = t_xr
                mix_last = t_xk
            bxr_t.wc(xr_last)
            bxk_t.wc(mix_last)
            bc_last = None
            for k in range(NCT):
                bc_last = p.op("vector", (lambda k: lambda e: e.tensor_copy(
                    lnT_t[:, 513 * k:513 * k + 1],
                    lnT_t[:, 513 * k + 512:513 * k + 513]))(k),
                    deps=[mix_last, xr_last] + blnT_t.r())
            blnT_t.wc(bc_last)

        def emit_ffn_mms(xrT_t, bxr_t, xkT_t, bxk_t, kactT_t, bkact_t,
                         wr_t, wk_t, wv_t, bwr_t, bwk_t, bwv_t,
                         out_action, krelu_t, bkrelu_t, rsb_t, brsb_t,
                         pre_dk=None, post_dk=None, pre_jloop=None):
            ka_last = None
            for dk in range(NCT):
                if pre_dk is not None and dk % 2 == 0:
                    pre_dk(dk)
                bank, bankb = (PW, bPW) if dk % 2 == 0 else (PX, bPX)
                mms = []
                for ck in range(NCT):
                    deps = [bxk_t.w, bwk_t.w]
                    if ck == 0:
                        deps += bankb.wdeps()
                    mms.append(p.op("tensor", (lambda bank, dk, ck: lambda e: e.matmul(
                        bank[:, :], wk_t[:, C * ck + 128 * dk:C * ck + 128 * (dk + 1)],
                        xkT_t[:, 512 * ck:512 * (ck + 1)],
                        start=(ck == 0), stop=(ck == NCT - 1)))(bank, dk, ck), deps=deps))
                bankb.wc(mms[-1])
                bxk_t.rc(mms[-1])
                t_r = p.op("scalar", (lambda bank, dk: lambda e: e.activation(
                    out=krelu_t[:, 512 * (dk % 2):512 * (dk % 2 + 1)], in_=bank[:, :],
                    func=AF.Relu))(bank, dk),
                    deps=[mms[-1]] + bkrelu_t.wdeps())
                bankb.rc(t_r)
                bkrelu_t.wc(t_r)
                t_s = p.op("scalar", (lambda dk: lambda e: e.activation(
                    out=kactT_t[:, 512 * dk:512 * (dk + 1)],
                    in_=krelu_t[:, 512 * (dk % 2):512 * (dk % 2 + 1)], func=AF.Square))(dk),
                    deps=[t_r] + (bkact_t.wdeps() if dk == 0 else []))
                bkrelu_t.rc(t_s)
                ka_last = t_s
                if post_dk is not None and dk % 2 == 1:
                    post_dk(dk)
            bkact_t.wc(ka_last)
            if pre_jloop is not None:
                pre_jloop()
            for j in range(4):
                rmms = []
                for h in range(2):
                    for ck in range(NCT):
                        deps = [bxr_t.w, bwr_t.w]
                        if h == 0 and ck == 0:
                            deps += bPA.wdeps()
                        rmms.append(p.op("tensor", (lambda j, h, ck: lambda e: e.matmul(
                            PA[:, 512 * h:512 * (h + 1)],
                            xrT_t[:, 512 * ck + 128 * j:512 * ck + 128 * (j + 1)],
                            wr_t[:, C * ck + 512 * h:C * ck + 512 * (h + 1)],
                            start=(ck == 0), stop=(ck == NCT - 1)))(j, h, ck),
                            deps=deps))
                bPA.wc(rmms[-1])
                bxr_t.rc(rmms[-1])
                rsl = j % 2
                sig = p.op("scalar", (lambda rsl: lambda e: e.activation(
                    out=rsb_t[:, C * rsl:C * (rsl + 1)], in_=PA[:, :],
                    func=AF.Sigmoid))(rsl),
                    deps=[rmms[-1]] + brsb_t[rsl].wdeps())
                bPA.rc(sig)
                brsb_t[rsl].wc(sig)
                vmms = []
                for h in range(2):
                    for dk in range(NCT):
                        deps = [bkact_t.w, bwv_t.w]
                        if h == 0 and dk == 0:
                            deps += bPB.wdeps()
                        vmms.append(p.op("tensor", (lambda j, h, dk: lambda e: e.matmul(
                            PB[:, 512 * h:512 * (h + 1)],
                            kactT_t[:, 512 * dk + 128 * j:512 * dk + 128 * (j + 1)],
                            wv_t[:, C * dk + 512 * h:C * dk + 512 * (h + 1)],
                            start=(dk == 0), stop=(dk == NCT - 1)))(j, h, dk),
                            deps=deps))
                bPB.wc(vmms[-1])
                bkact_t.rc(vmms[-1])
                out_action(j, rsl, vmms[-1], sig,
                           rsb_t[:, C * rsl:C * (rsl + 1)], brsb_t[rsl])

        tz = None
        for k in range(NCT):
            tz = p.op("vector", (lambda k: lambda e: e.memset(
                lnT[:, 513 * k:513 * k + 1], 0.0))(k),
                deps=blnT.wdeps() if k == 0 else [])
        blnT.wc(tz)

        SLOT = NCT * 512

        def a0_mm_stage(m, pre_dk=None, post_dk=None, pre_jloop=None):
            ms = m % 2
            xr_v = xrT[:, SLOT * ms:SLOT * (ms + 1)]
            xk_v = xkT[:, SLOT * ms:SLOT * (ms + 1)]

            def a0_out(j, rsl, vtok, sig_tok, rsb_ap, brsb_slot, m=m):
                xsl = j % 2
                t_xl = p.dma("sync", "x0a", (lambda m, j, xsl: lambda e: e.dma_start(
                    out=x0add[:, C * xsl:C * (xsl + 1)],
                    in_=X[MAC * m + MIC * j:MAC * m + MIC * (j + 1), :]))(m, j, xsl),
                    deps=bx0add[xsl].wdeps())
                bx0add[xsl].wc(t_xl)
                t_t3 = p.op("vector", (lambda rsb_ap, xsl: lambda e: e.scalar_tensor_tensor(
                    out=x1b[:, C * xsl:C * (xsl + 1)], in0=PB[:, :], scalar=1.0,
                    in1=rsb_ap, op0=ALU.mult, op1=ALU.mult))(rsb_ap, xsl),
                    deps=[vtok, sig_tok] + bx1[xsl].wdeps())
                bPB.rc(t_t3)
                brsb_slot.rc(t_t3)
                bx1[xsl].wc(t_t3)
                t_x1 = p.op("vector", (lambda xsl: lambda e: e.tensor_add(
                    x1b[:, C * xsl:C * (xsl + 1)], x1b[:, C * xsl:C * (xsl + 1)],
                    x0add[:, C * xsl:C * (xsl + 1)]))(xsl),
                    deps=[t_t3, t_xl] + bx1[xsl].wdeps())
                bx0add[xsl].rc(t_x1)
                bx1[xsl].wc(t_x1)
                t_sp = p.dma("sync", "x1o", (lambda m, j, xsl: lambda e: e.dma_start(
                    out=X1S[MAC * m + MIC * j:MAC * m + MIC * (j + 1), :],
                    in_=x1b[:, C * xsl:C * (xsl + 1)]))(m, j, xsl),
                    deps=[t_x1])
                bx1[xsl].rc(t_sp)

            emit_ffn_mms(xr_v, bxr[ms], xk_v, bxk[ms], kactT, bkact,
                         wr_s, wk_s, wv_s, bwr, bwk, bwv, a0_out,
                         krelu, bkrelu, r_sb, br_sb,
                         pre_dk=pre_dk, post_dk=post_dk, pre_jloop=pre_jloop)

        def a0_stats_micro(m, j):
            xs = j % 2
            t_x0 = p.dma("sync", "x0i", (lambda m, j, xs: lambda e: e.dma_start(
                out=x0b[:, C * xs:C * (xs + 1)],
                in_=X[MAC * m + MIC * j:MAC * m + MIC * (j + 1), :]))(m, j, xs),
                deps=bx0[xs].wdeps())
            bx0[xs].wc(t_x0)
            t_n = stats_norm(x0b[:, C * xs:C * (xs + 1)], [bx0[xs].w],
                             j, lnb, blnb, j)
            bx0[xs].rc(t_n)
            return t_n

        for m in range(NMAC):
            ms = m % 2
            if m == 0:
                for j in range(4):
                    t_n = a0_stats_micro(0, j)
                    lnb_to_lnT(j, lnT, blnT, lnb, blnb, j, t_n)
                emit_ffn_mix(lnT, blnT, xrT[:, 0:SLOT], bxr[0],
                             xkT[:, 0:SLOT], bxk[0], dscr, bdscr)
                continue
            ntoks = {}

            def pre_dk(dk, m=m, ntoks=ntoks):
                ntoks[dk // 2] = a0_stats_micro(m, dk // 2)

            def post_dk(dk, m=m, ntoks=ntoks):
                jj = (dk - 1) // 2
                lnb_to_lnT(jj, lnT, blnT, lnb, blnb, jj, ntoks[jj])

            def pre_j(m=m, ms=ms):
                emit_ffn_mix(lnT, blnT, xrT[:, SLOT * ms:SLOT * (ms + 1)], bxr[ms],
                             xkT[:, SLOT * ms:SLOT * (ms + 1)], bxk[ms], dscr, bdscr)

            a0_mm_stage(m - 1, pre_dk=pre_dk, post_dk=post_dk, pre_jloop=pre_j)
        a0_mm_stage(NMAC - 1)

        bar_a0 = p.hard_barrier()
        a0stk.close()
        fwstk.close()
        if stop_after == "a0":
            t_cp = p.dma("sync", "oo", lambda e: e.dma_start(out=OUT[:, :], in_=X1S[:, :]),
                         deps=bar_a0)
            p.op("sync", lambda e: e.nop(), deps=[t_cp])
            gstk.close()
            p.build_and_emit(root)
            return nc

        gfstk = contextlib.ExitStack()
        G_f = gfstk.enter_context(nc.sbuf_tensor("G_f", [128, NCT * C], F32, side="right"))
        bG = Buf()
        a1stk = contextlib.ExitStack()
        S_f = SB(a1stk, "S_f", [128, NCT * C], F32)
        S_b = SB(a1stk, "S_b", [128, NCT * C], BF16)
        bS = Buf()
        bSb_t = [Buf() for _ in range(NCT)]
        x1i = SB(a1stk, "x1i", [128, 2 * C], F32)
        bx1i = [Buf(), Buf()]
        a0nb = SB(a1stk, "a0nb", [128, 2 * C], BF16)
        ba0nb = [Buf(), Buf()]
        a0nT = SB(a1stk, "a0nT", [128, NCT * 512], BF16)
        ba0nT = Buf()
        kT = SB(a1stk, "kT", [128, NCT * 512], BF16)
        bkT = Buf()
        khat = SB(a1stk, "khat", [128, 4 * C], BF16)
        bkhat = Buf()
        qsp = khat
        bqsp = bkhat
        ut = SB(a1stk, "ut", [128, 4 * C], BF16)
        but = [Buf(), Buf(), Buf(), Buf()]
        wt = SB(a1stk, "wt", [128, 4 * C], BF16)
        bwt = Buf()
        b_sb = SB(a1stk, "b_sb", [128, C], BF16)
        bbsb = Buf()
        lr_c = SB(a1stk, "lr_c", [128, 4], F32)
        blr = Buf()
        nb_A = SB(a1stk, "nb_A", [128, 4 * 128], BF16)
        nb_AT = SB(a1stk, "nb_AT", [128, 4 * 128], BF16)
        nb_M = SB(a1stk, "nb_M", [128, 4 * 128], BF16)
        nb_MT = SB(a1stk, "nb_MT", [128, 4 * 128], BF16)
        nb_Q = nb_A
        bnA, bnAT, bnM, bnMT = (Buf() for _ in range(4))
        bnQ = bnA
        g_sc = SB(a1stk, "g_sc", [128, 4 * 128], F32)
        bg_sc = Buf()
        B_sb = SB(a1stk, "B_sb", [128, 4 * 128], F32)
        bB_sb = Buf()
        rr_b = SB(a1stk, "rr_b", [1, 128], BF16)
        brr = Buf()
        af_sc = SB(a1stk, "af_sc", [128, 128], F32)
        baf = Buf()

        tz1 = p.op("vector", lambda e: e.memset(S_f[:, :], 0.0), deps=bar_a0)
        bS.wc(tz1)
        tz2 = p.op("scalar", lambda e: e.activation(out=S_b[:, :], in_=S_f[:, :],
                                                    func=AF.Identity), deps=[tz1] + bar_a0)
        for bb in bSb_t:
            bb.wc(tz2)
        tz3 = p.op("vector", lambda e: e.memset(G_f[:, :], 0.0), deps=bar_a0)
        bG.wc(tz3)

        for m in range(NMAC):
            for j in range(4):
                sl = j % 2
                t_x1 = p.dma("sync", "x1i", (lambda m, j, sl: lambda e: e.dma_start(
                    out=x1i[:, C * sl:C * (sl + 1)],
                    in_=X1S[MAC * m + MIC * j:MAC * m + MIC * (j + 1), :]))(m, j, sl),
                    deps=bx1i[sl].wdeps() + (bar_a0 if m == 0 and j < 2 else []))
                bx1i[sl].wc(t_x1)
                stoks, sc_ap, nb_ap = ln_stats(x1i[:, C * sl:C * (sl + 1)],
                                               [bx1i[sl].w], j % 4, second_ln=True)
                t_n = p.op("scalar", (lambda sl, sc_ap, nb_ap: lambda e: e.activation(
                    out=a0nb[:, C * sl:C * (sl + 1)], in_=x1i[:, C * sl:C * (sl + 1)],
                    func=AF.Identity, scale=sc_ap, bias=nb_ap))(sl, sc_ap, nb_ap),
                    deps=stoks + [bx1i[sl].w] + ba0nb[sl].wdeps())
                ba0nb[sl].wc(t_n)
                bstt[j % 4].rc(t_n)
                bx1i[sl].rc(t_n)
                transpose8(
                    (lambda sl: lambda k: a0nb[:, C * sl + 128 * k:C * sl + 128 * (k + 1)])(sl),
                    (lambda j: lambda k: a0nT[:, 512 * k + 128 * j:512 * k + 128 * (j + 1)])(j),
                    [t_n], ba0nT)
                ba0nb[sl].rc(ba0nT.w)
            for which in ("q", "k"):
                off = 0 if which == "q" else C
                dstb, bdst = (qsp, bqsp) if which == "q" else (kT, bkT)
                bias_c = qb_c if which == "q" else kb_c
                bias_tok = bqb.w if which == "q" else bkb.w
                e_last = None
                for dk in range(NCT):
                    bank, bankb = (PW, bPW) if dk % 2 == 0 else (PX, bPX)
                    mms = []
                    for ck in range(NCT):
                        deps = [ba0nT.w, bwg.w]
                        if ck == 0:
                            deps += bankb.wdeps()
                        mms.append(p.op("tensor", (lambda bank, dk, ck, off: lambda e: e.matmul(
                            bank[:, :],
                            wg_s[:, 3 * C * ck + off + 128 * dk:3 * C * ck + off + 128 * (dk + 1)],
                            a0nT[:, 512 * ck:512 * (ck + 1)],
                            start=(ck == 0), stop=(ck == NCT - 1)))(bank, dk, ck, off),
                            deps=deps))
                    bankb.wc(mms[-1])
                    ba0nT.rc(mms[-1])
                    if which == "q":
                        dst_ap = qsp.bitcast(BF16)[:, 512 * dk:512 * (dk + 1)]
                    else:
                        dst_ap = kT[:, 512 * dk:512 * (dk + 1)]
                    deps_e = [mms[-1], bias_tok]
                    if dk == 0:
                        deps_e += bdst.wdeps()
                    e_last = p.op("scalar", (lambda bank, dst_ap, bias_c, dk: lambda e:
                                  e.activation(out=dst_ap, in_=bank[:, :], func=AF.Identity,
                                               bias=bias_c[:, dk:dk + 1]))(bank, dst_ap, bias_c, dk),
                                  deps=deps_e)
                    bankb.rc(e_last)
                bdst.wc(e_last)
                if which == "q":
                    t_qo = p.dma("scalar", "qo", (lambda m: lambda e: e.dma_start(
                        out=QTS.rearrange("ik p t -> p ik t")[:, :, MAC * m:MAC * (m + 1)],
                        in_=qsp.rearrange("p (ik t) -> p ik t", ik=NCT)))(m),
                        deps=bqsp.r())
                    bqsp.rc(t_qo)
            lr_last = None
            for j in range(4):
                mms = []
                for ck in range(NCT):
                    deps = [ba0nT.w, blrwb.w]
                    if ck == 0:
                        deps += bPS.wdeps()
                    mms.append(p.op("tensor", (lambda j, ck: lambda e: e.matmul(
                        PS[:, 0:1], a0nT[:, 512 * ck + 128 * j:512 * ck + 128 * (j + 1)],
                        lrw_b[:, ck:ck + 1],
                        start=(ck == 0), stop=(ck == NCT - 1)))(j, ck), deps=deps))
                bPS.wc(mms[-1])
                ba0nT.rc(mms[-1])
                lr_last = p.op("scalar", (lambda j: lambda e: e.activation(
                    out=lr_c[:, j:j + 1], in_=PS[:, 0:1], func=AF.Identity,
                    bias=lrb_bc[:, :]))(j),
                    deps=[mms[-1], blrb.w] + (blr.wdeps() if j == 0 else []))
                bPS.rc(lr_last)
            blr.wc(lr_last)
            for j in range(4 if a1_parts & 1 else 0):
                mms = []
                for ik in range(NCT):
                    deps = [bkT.w]
                    if ik == 0:
                        deps += bPS.wdeps()
                    mms.append(p.op("tensor", (lambda j, ik: lambda e: e.matmul(
                        PS[:, 0:128], kT[:, 512 * ik + 128 * j:512 * ik + 128 * (j + 1)],
                        kT[:, 512 * ik + 128 * j:512 * ik + 128 * (j + 1)],
                        start=(ik == 0), stop=(ik == NCT - 1)))(j, ik), deps=deps))
                bPS.wc(mms[-1])
                bkT.rc(mms[-1])
                s0 = 8 * (j % 4)
                t_dm = p.op("vector", lambda e: e.tensor_mul(
                    af_sc[:, :], PS[:, 0:128], ident_f[:, :]),
                    deps=[mms[-1], ld["IDENT_F"]] + baf.wdeps() + bstt[j % 4].wdeps())
                bPS.rc(t_dm)
                baf.wc(t_dm)
                t_d2 = p.op("vector", (lambda s0: lambda e: e.reduce_sum(
                    stt[:, s0:s0 + 1], af_sc[:, :], axis=AX.X))(s0), deps=[t_dm])
                baf.rc(t_d2)
                t_ln2 = p.op("scalar", (lambda s0: lambda e: e.activation(
                    out=stt[:, s0 + 1:s0 + 2], in_=stt[:, s0:s0 + 1], func=AF.Ln))(s0),
                    deps=[t_d2])
                t_rs = p.op("scalar", (lambda s0: lambda e: e.activation(
                    out=stt[:, s0 + 2:s0 + 3], in_=stt[:, s0 + 1:s0 + 2], func=AF.Exp,
                    scale=ccols[:, 1:2]))(s0), deps=[t_ln2, cc])
                bstt[j % 4].wc(t_rs)
                t_t1 = p.op("vector", (lambda j, s0: lambda e: e.tensor_scalar(
                    out=g_sc[:, 128 * j:128 * (j + 1)], in0=PS[:, 0:128],
                    scalar1=stt[:, s0 + 2:s0 + 3], scalar2=None, op0=ALU.mult))(j, s0),
                    deps=[t_rs, mms[-1]] + (bg_sc.wdeps() if j == 0 else []))
                bPS.rc(t_t1)
                bstt[j % 4].rc(t_t1)
                t_rr = p.op("tensor", (lambda s0: lambda e: e.transpose(
                    PS[0:1, 128:256], stt[:, s0 + 2:s0 + 3], ident_f[:, :]))(s0),
                    deps=[t_rs, t_t1, ld["IDENT_F"]] + bPS.wdeps())
                bPS.wc(t_rr)
                bstt[j % 4].rc(t_rr)
                t_rrb = p.op("scalar", lambda e: e.activation(
                    out=rr_b[0:1, :], in_=PS[0:1, 128:256], func=AF.Identity),
                    deps=[t_rr] + brr.wdeps())
                bPS.rc(t_rrb)
                brr.wc(t_rrb)
                t_bc = p.op("tensor", lambda e: e.matmul(
                    PS[:, 256:384], onesb[0:1, :], rr_b[0:1, :], start=True, stop=True),
                    deps=[t_rrb, ld["ONESB"]] + bPS.wdeps())
                bPS.wc(t_bc)
                brr.rc(t_bc)
                t_Bs = p.op("vector", (lambda j: lambda e: e.tensor_copy(
                    B_sb[:, 128 * j:128 * (j + 1)], PS[:, 256:384]))(j),
                    deps=[t_bc] + (bB_sb.wdeps() if j == 0 else []))
                bPS.rc(t_Bs)
                t_l2 = p.op("vector", (lambda j: lambda e: e.tensor_mul(
                    g_sc[:, 128 * j:128 * (j + 1)], g_sc[:, 128 * j:128 * (j + 1)],
                    B_sb[:, 128 * j:128 * (j + 1)]))(j), deps=[t_t1, t_Bs])
                t_l3 = p.op("vector", (lambda j: lambda e: e.tensor_mul(
                    g_sc[:, 128 * j:128 * (j + 1)], g_sc[:, 128 * j:128 * (j + 1)],
                    lmask[:, :]))(j), deps=[t_l2, ld["LMASK"]])
                t_A = p.op("vector", (lambda j: lambda e: e.tensor_tensor(
                    nb_A[:, 128 * j:128 * (j + 1)], g_sc[:, 128 * j:128 * (j + 1)],
                    ident_f[:, :], op=ALU.add))(j),
                    deps=[t_l3, ld["IDENT_F"]] + (bnA.wdeps() if j == 0 else []))
                t_M0 = p.op("vector", (lambda j: lambda e: e.tensor_tensor(
                    nb_M[:, 128 * j:128 * (j + 1)], g_sc[:, 128 * j:128 * (j + 1)],
                    ident_f[:, :], op=ALU.subtract))(j),
                    deps=[t_l3, ld["IDENT_F"]] + (bnM.wdeps() if j == 0 else []))
                if j == 3:
                    bg_sc.wc(t_l3)
                    bB_sb.wc(t_Bs)
                tt1 = p.op("tensor", (lambda j: lambda e: e.transpose(
                    PT[:, 0:128], nb_A[:, 128 * j:128 * (j + 1)], ident_b[:, :]))(j),
                    deps=[t_A, ld["IDENT_B"]] + bPT.wdeps())
                tt2 = p.op("tensor", (lambda j: lambda e: e.transpose(
                    PT[:, 128:256], nb_M[:, 128 * j:128 * (j + 1)], ident_b[:, :]))(j),
                    deps=[t_M0, ld["IDENT_B"]])
                bPT.wc(tt2)
                te1 = p.op("vector", (lambda j: lambda e: e.tensor_copy(
                    nb_AT[:, 128 * j:128 * (j + 1)], PT[:, 0:128]))(j),
                    deps=[tt2] + (bnAT.wdeps() if j == 0 else []))
                te2 = p.op("vector", (lambda j: lambda e: e.tensor_copy(
                    nb_MT[:, 128 * j:128 * (j + 1)], PT[:, 128:256]))(j),
                    deps=[tt2] + (bnMT.wdeps() if j == 0 else []))
                bPT.rc(te1)
                bPT.rc(te2)
                if j == 3:
                    bnA.wc(t_A)
                    bnM.wc(t_M0)
                    bnAT.wc(te1)
                    bnMT.wc(te2)
            for it in range(6 if a1_parts & 1 else 0):
                pms = []
                for j in range(4):
                    deps = [bnAT.w, bnM.w]
                    if j == 0:
                        deps += bPW.wdeps()
                    pms.append(p.op("tensor", (lambda j: lambda e: e.matmul(
                        PW[:, 128 * j:128 * (j + 1)], nb_AT[:, 128 * j:128 * (j + 1)],
                        nb_M[:, 128 * j:128 * (j + 1)], start=True, stop=True))(j),
                        deps=deps))
                bPW.wc(pms[-1])
                bnAT.rc(pms[-1])
                bnM.rc(pms[-1])
                t_Q = p.op("vector", lambda e: e.tensor_tensor(
                    nb_Q[:, 0:512], PW[:, 0:512], twoi4[:, :], op=ALU.add),
                    deps=[pms[-1], btwoi4.w] + bnQ.wdeps())
                bPW.rc(t_Q)
                bnQ.wc(t_Q)
                mms2 = []
                for j in range(4):
                    deps = [bnMT.w, bnQ.w]
                    if j == 0:
                        deps += bPX.wdeps()
                    mms2.append(p.op("tensor", (lambda j: lambda e: e.matmul(
                        PX[:, 128 * j:128 * (j + 1)], nb_MT[:, 128 * j:128 * (j + 1)],
                        nb_Q[:, 128 * j:128 * (j + 1)], start=True, stop=True))(j),
                        deps=deps))
                bPX.wc(mms2[-1])
                bnMT.rc(mms2[-1])
                bnQ.rc(mms2[-1])
                mms3 = []
                for j in range(4):
                    deps = [bnQ.w, bnMT.w]
                    if j == 0:
                        deps += bPW.wdeps()
                    mms3.append(p.op("tensor", (lambda j: lambda e: e.matmul(
                        PW[:, 128 * j:128 * (j + 1)], nb_Q[:, 128 * j:128 * (j + 1)],
                        nb_MT[:, 128 * j:128 * (j + 1)], start=True, stop=True))(j),
                        deps=deps))
                bPW.wc(mms3[-1])
                bnQ.rc(mms3[-1])
                bnMT.rc(mms3[-1])
                t_M = p.op("vector", lambda e: e.tensor_copy(nb_M[:, 0:512], PX[:, 0:512]),
                           deps=[mms2[-1]] + bnM.wdeps())
                bPX.rc(t_M)
                bnM.wc(t_M)
                t_MT = p.op("vector", lambda e: e.tensor_copy(nb_MT[:, 0:512], PW[:, 0:512]),
                            deps=[mms3[-1]] + bnMT.wdeps())
                bPW.rc(t_MT)
                bnMT.wc(t_MT)
            for j in range(4 if a1_parts & 1 else 0):
                last_sc = None
                for ik in range(NCT):
                    deps = [bkT.w, bB_sb.w] + (bkT.rs[-1:] if bkT.rs else [])
                    last_sc = p.op("vector", (lambda j, ik: lambda e: e.tensor_mul(
                        kT[:, 512 * ik + 128 * j:512 * ik + 128 * (j + 1)],
                        kT[:, 512 * ik + 128 * j:512 * ik + 128 * (j + 1)],
                        B_sb[:, 128 * j:128 * (j + 1)]))(j, ik), deps=deps)
                bkT.wc(last_sc)
                bB_sb.rc(last_sc)
                transpose8(
                    (lambda j: lambda ik: kT[:, 512 * ik + 128 * j:512 * ik + 128 * (j + 1)])(j),
                    None, [last_sc], bkhat,
                    dst_view=khat.rearrange("p (j2 ik q) -> p j2 ik q", j2=4, q=128)[:, j, :, :])
                bkT.rc(bkhat.w)
            inc_ks = bool(a1_parts & 8)
            inc_x = bool(a1_parts & 16)
            inc_u = bool(a1_parts & 32)
            inc_sf = bool(a1_parts & 64)

            def rec_vks(j, close_group):
                bank, bankb = (PA, bPA) if j % 2 == 0 else (PB, bPB)
                mms = []
                for h in range(2):
                    for ck in range(NCT):
                        deps = [ba0nT.w, bwg.w]
                        if h == 0 and ck == 0:
                            deps += bankb.wdeps()
                        mms.append(p.op("tensor", (lambda bank, j, h, ck: lambda e: e.matmul(
                            bank[:, 512 * h:512 * (h + 1)],
                            a0nT[:, 512 * ck + 128 * j:512 * ck + 128 * (j + 1)],
                            wg_s[:, 3 * C * ck + 2 * C + 512 * h:3 * C * ck + 2 * C + 512 * (h + 1)],
                            start=(ck == 0), stop=False))(bank, j, h, ck), deps=deps))
                ba0nT.rc(mms[-1])
                for h in range(2):
                    mms.append(p.op("tensor", (lambda bank, h, st2: lambda e: e.matmul(
                        bank[:, 512 * h:512 * (h + 1)], onesb[0:1, :],
                        browv_b[0:1, 512 * h:512 * (h + 1)], start=False,
                        stop=st2))(bank, h, (not inc_ks) and close_group),
                        deps=[bbrowv.w, ld["ONESB"]]))
                if inc_ks:
                    for h in range(2):
                        for ik in range(NCT):
                            is_stop = close_group and h == 1 and ik == NCT - 1 and False
                            is_stop = close_group and ik == NCT - 1
                            deps = [bkT.w, bSb_t[ik].w]
                            mms.append(p.op("tensor", (lambda bank, j, h, ik, is_stop: lambda e: e.matmul(
                                bank[:, 512 * h:512 * (h + 1)],
                                kT[:, 512 * ik + 128 * j:512 * ik + 128 * (j + 1)],
                                S_b[:, C * ik + 512 * h:C * ik + 512 * (h + 1)],
                                start=False, stop=is_stop))(bank, j, h, ik, is_stop),
                                deps=deps))
                            bSb_t[ik].rc(mms[-1])
                    bkT.rc(mms[-1])
                bankb.wc(mms[-1])

            def rec_corr(j):
                bank, bankb = (PA, bPA) if j % 2 == 0 else (PB, bPB)
                t_xu = None
                for jj in range(j):
                    xms = []
                    for ik in range(NCT):
                        deps = [bkT.w]
                        if ik == 0:
                            deps += bPS.wdeps()
                        xms.append(p.op("tensor", (lambda j, jj, ik: lambda e: e.matmul(
                            PS[:, 0:128],
                            kT[:, 512 * ik + 128 * jj:512 * ik + 128 * (jj + 1)],
                            kT[:, 512 * ik + 128 * j:512 * ik + 128 * (j + 1)],
                            start=(ik == 0), stop=(ik == NCT - 1)))(j, jj, ik), deps=deps))
                    bPS.wc(xms[-1])
                    bkT.rc(xms[-1])
                    t_xt = p.op("scalar", lambda e: e.activation(
                        out=af_sc.bitcast(BF16)[:, 0:128], in_=PS[:, 0:128],
                        func=AF.Identity), deps=[xms[-1]] + baf.wdeps())
                    bPS.rc(t_xt)
                    baf.wc(t_xt)
                    for h in range(2):
                        t_xu = p.op("tensor", (lambda h, jj, j: lambda e: e.matmul(
                            bank[:, 512 * h:512 * (h + 1)], af_sc.bitcast(BF16)[:, 0:128],
                            ut[:, C * jj + 512 * h:C * jj + 512 * (h + 1)],
                            start=False, stop=(jj == j - 1)))(h, jj, j),
                            deps=[t_xt, but[jj].w])
                    baf.rc(t_xu)
                    but[jj].rc(t_xu)
                if t_xu is not None:
                    bankb.wc(t_xu)

            def rec_bu(j):
                bank, bankb = (PA, bPA) if j % 2 == 0 else (PB, bPB)
                t_b2e = p.op("scalar", (lambda bank: lambda e: e.activation(
                    out=b_sb[:, 0:1024], in_=bank[:, 0:1024], func=AF.Identity))(bank),
                    deps=[bankb.w] + bbsb.wdeps())
                bankb.rc(t_b2e)
                bbsb.wc(t_b2e)
                if not inc_u:
                    return
                ums = []
                for h in range(2):
                    ums.append(p.op("tensor", (lambda bank, j, h: lambda e: e.matmul(
                        bank[:, 512 * h:512 * (h + 1)], nb_MT[:, 128 * j:128 * (j + 1)],
                        b_sb[:, 512 * h:512 * (h + 1)], start=True, stop=True))(bank, j, h),
                        deps=[bnMT.w, t_b2e] + (bankb.wdeps() if h == 0 else [])))
                bankb.wc(ums[-1])
                bnMT.rc(ums[-1])
                bbsb.rc(ums[-1])
                t_u2 = p.op("scalar", (lambda bank, j: lambda e: e.activation(
                    out=ut[:, C * j:C * (j + 1)], in_=bank[:, 0:1024],
                    func=AF.Identity))(bank, j),
                    deps=[ums[-1]] + but[j].wdeps())
                but[j].wc(t_u2)
                t_w2 = p.op("vector", (lambda j: lambda e: e.tensor_scalar(
                    out=wt[:, C * j:C * (j + 1)], in0=ut[:, C * j:C * (j + 1)],
                    scalar1=lr_c[:, j:j + 1], scalar2=None, op0=ALU.mult))(j),
                    deps=[t_u2, blr.w] + (bwt.wdeps() if j == 0 else []))
                but[j].rc(t_w2)
                blr.rc(t_w2)
                if j == 3:
                    bwt.wc(t_w2)
                bankb.rc(t_w2)

            if a1_parts & 2:
                for j in range(4):
                    close = (j == 0) or not inc_x
                    rec_vks(j, close_group=close)
                    if j > 0 and inc_x:
                        rec_corr(j)
                    rec_bu(j)
                if inc_sf:
                    for ik in range(NCT):
                        bank2, bank2b = (PA, bPA) if ik % 2 == 0 else (PB, bPB)
                        fms = []
                        for jj in range(4):
                            for h in range(2):
                                deps = [bkhat.w, but[jj].w]
                                if jj == 0 and h == 0:
                                    deps += bank2b.wdeps()
                                fms.append(p.op("tensor",
                                    (lambda bank2, ik, jj, h: lambda e: e.matmul(
                                        bank2[:, 512 * h:512 * (h + 1)],
                                        khat[:, C * jj + 128 * ik:C * jj + 128 * (ik + 1)],
                                        ut[:, C * jj + 512 * h:C * jj + 512 * (h + 1)],
                                        start=(jj == 0), stop=(jj == 3)))(bank2, ik, jj, h),
                                    deps=deps))
                        bank2b.wc(fms[-1])
                        bkhat.rc(fms[-1])
                        for jj in range(4):
                            but[jj].rc(fms[-1])
                        t_sa = p.op("vector", (lambda bank2, ik: lambda e: e.tensor_add(
                            S_f[:, C * ik:C * (ik + 1)], S_f[:, C * ik:C * (ik + 1)],
                            bank2[:, :]))(bank2, ik),
                            deps=[fms[-1]] + bS.wdeps())
                        bank2b.rc(t_sa)
                        bS.wc(t_sa)
                        t_sc2 = p.op("scalar", (lambda ik: lambda e: e.activation(
                            out=S_b[:, C * ik:C * (ik + 1)], in_=S_f[:, C * ik:C * (ik + 1)],
                            func=AF.Identity))(ik),
                            deps=[t_sa] + bSb_t[ik].wdeps())
                        bS.rc(t_sc2)
                        bSb_t[ik].wc(t_sc2)
            for ik in range(NCT if a1_parts & 4 else 0):
                bank2, bank2b = (PB, bPB) if ik % 2 == 0 else (PA, bPA)
                gms = []
                for jj in range(4):
                    for h in range(2):
                        deps = [bkhat.w, bwt.w]
                        if jj == 0 and h == 0:
                            deps += bank2b.wdeps()
                        gms.append(p.op("tensor", (lambda bank2, ik, jj, h: lambda e: e.matmul(
                            bank2[:, 512 * h:512 * (h + 1)],
                            khat[:, C * jj + 128 * ik:C * jj + 128 * (ik + 1)],
                            wt[:, C * jj + 512 * h:C * jj + 512 * (h + 1)],
                            start=(jj == 0), stop=(jj == 3)))(bank2, ik, jj, h),
                            deps=deps))
                bank2b.wc(gms[-1])
                bkhat.rc(gms[-1])
                bwt.rc(gms[-1])
                t_ga = p.op("vector", (lambda bank2, ik: lambda e: e.tensor_add(
                    G_f[:, C * ik:C * (ik + 1)], G_f[:, C * ik:C * (ik + 1)],
                    bank2[:, :]))(bank2, ik),
                    deps=[gms[-1]] + bG.wdeps())
                bank2b.rc(t_ga)
                bG.wc(t_ga)
        if dbg:
            t_sd = p.dma("sync", "dbg", lambda e: e.dma_start(out=SD[:, :], in_=S_f[:, :]),
                         deps=bS.r())
            bS.rc(t_sd)
            t_gd = p.dma("sync", "dbg", lambda e: e.dma_start(out=GD[:, :], in_=G_f[:, :]),
                         deps=bG.r())
            bG.rc(t_gd)
        if stop_after == "a1":
            toks = []
            for ik in range(NCT):
                toks.append(p.dma("sync", "oo", (lambda ik: lambda e: e.dma_start(
                    out=OUT[128 * ik:128 * (ik + 1), :],
                    in_=S_f[:, C * ik:C * (ik + 1)]))(ik), deps=bS.r() + bG.r()))
            for t in toks:
                bS.rc(t)
            p.op("sync", lambda e: e.nop(), deps=[toks[-1]] +
                 ([("dbg", p.dma_counts["dbg"])] if dbg else []))
            a1stk.close()
            gstk.close()
            gfstk.close()
            p.build_and_emit(root)
            return nc
        bar_a1 = p.hard_barrier()
        a1stk.close()
        gstk.close()

        bstk = contextlib.ExitStack()
        G_b = SB(bstk, "G_b", [128, NCT * C], BF16)
        bGb = Buf(readonly=True)
        t_gc = None
        for ik in range(2 * NCT):
            t_gc = p.op("scalar", (lambda ik: lambda e: e.activation(
                out=G_b[:, 512 * ik:512 * (ik + 1)], in_=G_f[:, 512 * ik:512 * (ik + 1)],
                func=AF.Identity))(ik), deps=[bG.w] + (bar_a1 if ik == 0 else []))
            bG.rc(t_gc)
        bGb.w = t_gc
        gbar = p.hard_barrier()
        gfstk.close()

        wr_s2 = SB(bstk, "wr_s2", [128, NCT * C], BF16)
        wk_s2 = SB(bstk, "wk_s2", [128, NCT * C], BF16)
        wv_s2 = SB(bstk, "wv_s2", [128, NCT * C], BF16)
        outw_s = SB(bstk, "outw_s", [128, NCT * C], BF16)
        w1_s = SB(bstk, "w1_s", [128, NCT * C], BF16)
        w2_s = SB(bstk, "w2_s", [128, NCT * C], BF16)
        bwstk = contextlib.ExitStack()
        wfstage2 = SB(bwstk, "wfstage2", [128, 2 * C], F32)
        bwf2 = [Buf(), Buf()]
        bwr2, bwk2, bwv2, boutw, bw1, bw2 = (Buf(readonly=True) for _ in range(6))

        def load_w2(dst, bdst, srcd, scale_col=None, scale_cols=None, bar=None,
                    scale_tok=None):
            last = None
            for ck in range(NCT):
                sl = ck % 2
                deps = bwf2[sl].wdeps()
                if bar is not None and ck < 2:
                    deps = deps + bar
                dl = p.dma("scalar", "wl", (lambda ck, sl, srcd: lambda e: e.dma_start(
                    out=wfstage2[:, C * sl:C * (sl + 1)],
                    in_=srcd[128 * ck:128 * (ck + 1), :]))(ck, sl, srcd),
                    deps=deps)
                bwf2[sl].wc(dl)
                if scale_cols is not None:
                    sc_ap = scale_cols[:, ck:ck + 1]
                elif scale_col is not None:
                    sc_ap = scale_col
                else:
                    sc_ap = None
                if sc_ap is None:
                    tc = p.op("scalar", (lambda ck, sl, dst: lambda e: e.activation(
                        out=dst[:, C * ck:C * (ck + 1)],
                        in_=wfstage2[:, C * sl:C * (sl + 1)], func=AF.Identity))(ck, sl, dst),
                        deps=[dl])
                else:
                    tc = p.op("scalar", (lambda ck, sl, dst, sc_ap: lambda e: e.activation(
                        out=dst[:, C * ck:C * (ck + 1)],
                        in_=wfstage2[:, C * sl:C * (sl + 1)], func=AF.Identity,
                        scale=sc_ap))(ck, sl, dst, sc_ap),
                        deps=[dl] + ([scale_tok] if scale_tok else [cc]))
                bwf2[sl].rc(tc)
                last = tc
            bdst.w = last
            return last

        load_w2(wr_s2, bwr2, WRT, bar=gbar)
        load_w2(wk_s2, bwk2, WKT)
        load_w2(wv_s2, bwv2, WVT, scale_col=ccols[:, 2:3])
        load_w2(outw_s, boutw, OUTW, scale_col=ccols[:, 3:4])
        load_w2(w1_s, bw1, W1, scale_cols=mng_c, scale_tok=bmng.w)
        load_w2(w2_s, bw2, W2)
        wbar = p.hard_barrier()
        bwstk.close()

        x1i2 = SB(bstk, "x1i2", [128, 4 * C], F32)
        bx1i2 = Buf()
        qTin = SB(bstk, "qTin", [128, NCT * 512], BF16)
        bqT = Buf()
        kactT2 = qTin
        bkact2 = bqT
        a0b = SB(bstk, "a0b", [128, 2 * C], F32)
        ba0 = [Buf(), Buf()]
        outb_f = a0b
        boutf = ba0
        yT = SB(bstk, "yT", [128, NCT * 512], BF16)
        byT = Buf()
        xrT2 = yT
        bxr2 = byT
        a1b = SB(bstk, "a1b", [128, 3 * C], F32)
        ba1 = [Buf(), Buf(), Buf(), bqT]
        t3b2 = a1b
        bt32 = ba1

        def a1s(j):
            return a1b[:, C * j:C * (j + 1)] if j < 3 else qTin.bitcast(F32)[:, 0:C]
        a1nb = SB(bstk, "a1nb", [128, 2 * C], BF16)
        _ban = Buf()
        ba1nb = [_ban, _ban]
        ln3b = a1nb
        bln3b = ba1nb
        a1nT = SB(bstk, "a1nT", [128, NCT * 512], BF16)
        ba1nT = Buf()
        xkT2 = a1nT
        bxk2 = ba1nT
        hT = SB(bstk, "hT", [128, NCT * 512], BF16)
        bhT = Buf()
        ln3T = bstk.enter_context(nc.sbuf_tensor("ln3T", [128, NCT * 513], BF16, side="right"))
        bln3T = Buf()
        dscr_B = sqd.bitcast(BF16)[:, C:2 * C]
        bdscr_B = Buf()
        krelu_B = a1nb.bitcast(F32)
        bkrelu_B = _ban
        rsb_B = hT.bitcast(F32)

        tz4 = None
        for k in range(NCT):
            tz4 = p.op("vector", (lambda k: lambda e: e.memset(
                ln3T[:, 513 * k:513 * k + 1], 0.0))(k),
                deps=(bln3T.wdeps() + wbar) if k == 0 else [])
        bln3T.wc(tz4)

        for m in range(NMAC if b_parts & 1 else 0):
            t_x1 = p.dma("sync", "x1i", (lambda m: lambda e: e.dma_start(
                out=x1i2.rearrange("p (j c) -> p j c", j=4),
                in_=X1S.rearrange("(mm j p) c -> p mm j c", p=128, j=4)[:, m, :, :]))(m),
                deps=bx1i2.wdeps() + (wbar if m == 0 else []))
            bx1i2.wc(t_x1)
            t_qi = p.dma("scalar", "qi", (lambda m: lambda e: e.dma_start(
                out=qTin.rearrange("p (ik t) -> p ik t", ik=NCT),
                in_=QTS.rearrange("ik p t -> p ik t")[:, :, MAC * m:MAC * (m + 1)]))(m),
                deps=bqT.wdeps() + (wbar if m == 0 else []))
            bqT.wc(t_qi)
            y_last = None
            for jq in range(NCT):
                bank, bankb = (PW, bPW) if jq % 2 == 0 else (PX, bPX)
                mms = []
                for ik in range(NCT):
                    deps = [bGb.w, bqT.w]
                    if ik == 0:
                        deps += bankb.wdeps()
                    mms.append(p.op("tensor", (lambda bank, jq, ik: lambda e: e.matmul(
                        bank[:, :], G_b[:, C * ik + 128 * jq:C * ik + 128 * (jq + 1)],
                        qTin[:, 512 * ik:512 * (ik + 1)],
                        start=(ik == 0), stop=(ik == NCT - 1)))(bank, jq, ik), deps=deps))
                bankb.wc(mms[-1])
                bqT.rc(mms[-1])
                deps_e = [mms[-1]]
                if jq == 0:
                    deps_e += byT.wdeps()
                y_last = p.op("vector", (lambda bank, jq: lambda e: e.tensor_copy(
                    yT[:, 512 * jq:512 * (jq + 1)], bank[:, :]))(bank, jq), deps=deps_e)
                bankb.rc(y_last)
            byT.wc(y_last)
            if dbg:
                t_dmp3 = p.dma("sync", "dbg", (lambda m: lambda e: e.dma_start(
                    out=YD.rearrange("jq q t -> q jq t")[:, :, MAC * m:MAC * (m + 1)],
                    in_=yT.rearrange("p (jq t) -> p jq t", jq=NCT)))(m), deps=byT.r())
                byT.rc(t_dmp3)
            def emit_Fa(j, m=m):
                asl = j % 2
                stoks, sc_ap, nb_ap = ln_stats(x1i2[:, C * j:C * (j + 1)], [bx1i2.w],
                                               j % 4)
                t_a0 = p.op("scalar", (lambda j, asl, sc_ap, nb_ap: lambda e: e.activation(
                    out=a0b[:, C * asl:C * (asl + 1)], in_=x1i2[:, C * j:C * (j + 1)],
                    func=AF.Identity, scale=sc_ap, bias=nb_ap))(j, asl, sc_ap, nb_ap),
                    deps=stoks + [bx1i2.w] + ba0[asl].wdeps())
                ba0[asl].wc(t_a0)
                bstt[j % 4].rc(t_a0)
                bx1i2.rc(t_a0)
                mms = []
                for h in range(2):
                    for jv in range(NCT):
                        deps = [byT.w, boutw.w]
                        if h == 0 and jv == 0:
                            deps += bPA.wdeps()
                        mms.append(p.op("tensor", (lambda j, h, jv: lambda e: e.matmul(
                            PA[:, 512 * h:512 * (h + 1)],
                            yT[:, 512 * jv + 128 * j:512 * jv + 128 * (j + 1)],
                            outw_s[:, C * jv + 512 * h:C * jv + 512 * (h + 1)],
                            start=(jv == 0), stop=False))(j, h, jv), deps=deps))
                byT.rc(mms[-1])
                for h in range(2):
                    mms.append(p.op("tensor", (lambda h: lambda e: e.matmul(
                        PA[:, 512 * h:512 * (h + 1)], onesb[0:1, :],
                        outb_rb[0:1, 512 * h:512 * (h + 1)], start=False, stop=True))(h),
                        deps=[boutb.w, ld["ONESB"]]))
                bPA.wc(mms[-1])
                t_a1 = p.op("vector", (lambda j, asl: lambda e: e.tensor_add(
                    a1s(j), a0b[:, C * asl:C * (asl + 1)],
                    PA[:, :]))(j, asl),
                    deps=[mms[-1], t_a0] + ba1[j].wdeps())
                bPA.rc(t_a1)
                ba0[asl].rc(t_a1)
                ba1[j].wc(t_a1)
                if dbg:
                    t_dmp = p.dma("sync", "dbg", (lambda m, j: lambda e: e.dma_start(
                        out=A1D[MAC * m + MIC * j:MAC * m + MIC * (j + 1), :],
                        in_=a1s(j)))(m, j), deps=[t_a1])
                    ba1[j].rc(t_dmp)
                return t_a1

            def emit_Fb(j, t_a1):
                asl = j % 2
                stoks2, sc2, nb2 = ln_stats(a1s(j), [t_a1], j % 4)
                t_n2 = p.op("scalar", (lambda j, asl, sc2, nb2: lambda e: e.activation(
                    out=a1nb[:, C * asl:C * (asl + 1)], in_=a1s(j),
                    func=AF.Identity, scale=sc2, bias=nb2))(j, asl, sc2, nb2),
                    deps=stoks2 + [t_a1] + ba1nb[asl].wdeps())
                ba1nb[asl].wc(t_n2)
                bstt[j % 4].rc(t_n2)
                ba1[j].rc(t_n2)
                transpose8(
                    (lambda asl: lambda k: a1nb[:, C * asl + 128 * k:C * asl + 128 * (k + 1)])(asl),
                    None, [t_n2], ba1nT,
                    dst_view=a1nT.rearrange("p (k t) -> p k t", t=512)[:, :, 128 * j:128 * (j + 1)])
                ba1nb[asl].rc(ba1nT.w)

            def emit_H(half, m=m):
                h_last = None
                for dk in range(NCT):
                    bank, bankb = (PW, bPW) if dk % 2 == 0 else (PX, bPX)
                    mms = []
                    for ck in range(NCT):
                        deps = [ba1nT.w, bw1.w]
                        if ck == 0:
                            deps += bankb.wdeps()
                        mms.append(p.op("tensor", (lambda bank, dk, ck, half: lambda e: e.matmul(
                            bank[:, 0:256], w1_s[:, C * ck + 128 * dk:C * ck + 128 * (dk + 1)],
                            a1nT[:, 512 * ck + 256 * half:512 * ck + 256 * (half + 1)],
                            start=(ck == 0), stop=(ck == NCT - 1)))(bank, dk, ck, half),
                            deps=deps))
                    bankb.wc(mms[-1])
                    ba1nT.rc(mms[-1])
                    deps_e = [mms[-1], bb1c.w]
                    if dk == 0:
                        deps_e += bhT.wdeps()
                    h_last = p.op("scalar", (lambda bank, dk, half: lambda e: e.activation(
                        out=hT[:, 512 * dk + 256 * half:512 * dk + 256 * (half + 1)],
                        in_=bank[:, 0:256],
                        func=AF.Relu, bias=b1_c[:, dk:dk + 1]))(bank, dk, half), deps=deps_e)
                    bankb.rc(h_last)
                bhT.wc(h_last)
                if dbg:
                    t_dmp5 = p.dma("sync", "dbg", (lambda m, half: lambda e: e.dma_start(
                        out=HD.rearrange("dk q t -> q dk t")[:, :, MAC * m + 256 * half:MAC * m + 256 * (half + 1)],
                        in_=hT.rearrange("p (dk t) -> p dk t", dk=NCT)[:, :, 256 * half:256 * (half + 1)]))(m, half),
                        deps=bhT.r())
                    bhT.rc(t_dmp5)

            def emit_M(j, m=m):
                mms = []
                for h in range(2):
                    for dk in range(NCT):
                        deps = [bhT.w, bw2.w]
                        if h == 0 and dk == 0:
                            deps += bPB.wdeps()
                        mms.append(p.op("tensor", (lambda j, h, dk: lambda e: e.matmul(
                            PB[:, 512 * h:512 * (h + 1)],
                            hT[:, 512 * dk + 128 * j:512 * dk + 128 * (j + 1)],
                            w2_s[:, C * dk + 512 * h:C * dk + 512 * (h + 1)],
                            start=(dk == 0), stop=False))(j, h, dk), deps=deps))
                bhT.rc(mms[-1])
                for h in range(2):
                    mms.append(p.op("tensor", (lambda h: lambda e: e.matmul(
                        PB[:, 512 * h:512 * (h + 1)], onesb[0:1, :],
                        b2_rb[0:1, 512 * h:512 * (h + 1)], start=False, stop=True))(h),
                        deps=[bb2.w, ld["ONESB"]]))
                bPB.wc(mms[-1])
                t_a2 = p.op("vector", (lambda j: lambda e: e.tensor_add(
                    a1s(j), a1s(j),
                    PB[:, :]))(j),
                    deps=[mms[-1], ba1[j].w] + ba1[j].rs)
                bPB.rc(t_a2)
                ba1[j].wc(t_a2)
                if dbg:
                    t_dmp4 = p.dma("sync", "dbg", (lambda m, j: lambda e: e.dma_start(
                        out=A2D[MAC * m + MIC * j:MAC * m + MIC * (j + 1), :],
                        in_=a1s(j)))(m, j), deps=[t_a2])
                    ba1[j].rc(t_dmp4)
                t_x2 = p.op("vector", (lambda j: lambda e: e.tensor_add(
                    x1i2[:, C * j:C * (j + 1)], x1i2[:, C * j:C * (j + 1)],
                    a1s(j)))(j),
                    deps=[t_a2, bx1i2.w] + bx1i2.rs + ([t_dmp4] if dbg else []))
                ba1[j].rc(t_x2)
                bx1i2.wc(t_x2)
                if dbg:
                    t_dmp2 = p.dma("sync", "dbg", (lambda m, j: lambda e: e.dma_start(
                        out=X2D[MAC * m + MIC * j:MAC * m + MIC * (j + 1), :],
                        in_=x1i2[:, C * j:C * (j + 1)]))(m, j), deps=[bx1i2.w])
                    bx1i2.rc(t_dmp2)
            def emit_Mb(j):
                emit_lnT(x1i2[:, C * j:C * (j + 1)], [bx1i2.w], j, ln3T, bln3T,
                         ln3b, bln3b)
                bx1i2.rc(bln3T.w)

            ta1 = {}
            for j in (0, 1):
                ta1[j] = emit_Fa(j)
            for j in (0, 1):
                emit_Fb(j, ta1[j])
            for j in (2, 3):
                ta1[j] = emit_Fa(j)
            for j in (2, 3):
                emit_Fb(j, ta1[j])
            emit_H(0)
            emit_M(0)
            emit_M(1)
            emit_H(1)
            emit_Mb(0)
            emit_Mb(1)
            emit_M(2)
            emit_M(3)
            emit_Mb(2)
            emit_Mb(3)

            def b_out(j, rsl, vtok, sig_tok, rsb_ap, brsb_slot, m=m):
                osl = j % 2
                t_t3 = p.op("vector", (lambda rsb_ap, rsl: lambda e: e.scalar_tensor_tensor(
                    out=t3b2[:, C * rsl:C * (rsl + 1)], in0=PB[:, :], scalar=1.0,
                    in1=rsb_ap, op0=ALU.mult, op1=ALU.mult))(rsb_ap, rsl),
                    deps=[vtok, sig_tok] + bt32[rsl].wdeps())
                bPB.rc(t_t3)
                brsb_slot.rc(t_t3)
                bt32[rsl].wc(t_t3)
                t_o = p.op("vector", (lambda j, rsl, osl: lambda e: e.tensor_add(
                    outb_f[:, C * osl:C * (osl + 1)], x1i2[:, C * j:C * (j + 1)],
                    t3b2[:, C * rsl:C * (rsl + 1)]))(j, rsl, osl),
                    deps=[t_t3, bx1i2.w] + boutf[osl].wdeps())
                bt32[rsl].rc(t_o)
                bx1i2.rc(t_o)
                boutf[osl].wc(t_o)
                t_od = p.dma("sync", "oo", (lambda m, j, osl: lambda e: e.dma_start(
                    out=OUT[MAC * m + MIC * j:MAC * m + MIC * (j + 1), :],
                    in_=outb_f[:, C * osl:C * (osl + 1)]))(m, j, osl),
                    deps=[t_o])
                boutf[osl].rc(t_od)

            emit_ffn_mix(ln3T, bln3T, xrT2, bxr2, xkT2, bxk2, dscr_B, bdscr_B)
            emit_ffn_mms(xrT2, bxr2, xkT2, bxk2, kactT2, bkact2,
                         wr_s2, wk_s2, wv_s2, bwr2, bwk2, bwv2, b_out,
                         krelu_B, bkrelu_B, rsb_B, [bhT, bhT])

        p.op("sync", lambda e: e.nop(),
             deps=([("oo", p.dma_counts["oo"])] if p.dma_counts.get("oo") else p.barrier_tokens()) +
                  ([("dbg", p.dma_counts["dbg"])] if dbg else []))

        bstk.close()
        p.build_and_emit(root)
    return nc


def _prep_shared(inputs):
    f32 = np.float32
    ac = np.ascontiguousarray
    return {
        "WRT": ac(inputs["L0_ffn_W_r"].T.astype(f32)),
        "WKT": ac(inputs["L0_ffn_W_k"].T.astype(f32)),
        "WVT": ac(inputs["L0_ffn_W_v"].T.astype(f32)),
        "WG": ac(inputs["L0_attn_gates_w"].astype(f32)),
        "LRW": ac(inputs["L0_attn_lr_w"].reshape(1, -1).astype(f32)),
        "OUTW": ac(inputs["L0_attn_out_w"].astype(f32)),
        "W1": ac(inputs["L0_attn_w1"].astype(f32)),
        "W2": ac(inputs["L0_attn_w2"].astype(f32)),
        "DNG": ac(inputs["L0_attn_dn_g"].reshape(1, -1).astype(f32)),
        "DNB": ac(inputs["L0_attn_dn_b"].reshape(1, -1).astype(f32)),
        "MNG": ac(inputs["L0_attn_mn_g"].reshape(1, -1).astype(f32)),
        "MNB": ac(inputs["L0_attn_mn_b"].reshape(1, -1).astype(f32)),
        "MUR": ac(inputs["L0_ffn_mu_r"].reshape(1, -1).astype(f32)),
        "MUK": ac(inputs["L0_ffn_mu_k"].reshape(1, -1).astype(f32)),
        "GATESB": ac(inputs["L0_attn_gates_b"].reshape(1, -1).astype(f32)),
        "OUTB": ac(inputs["L0_attn_out_b"].reshape(1, -1).astype(f32)),
        "B1": ac(inputs["L0_attn_b1"].reshape(1, -1).astype(f32)),
        "B2": ac(inputs["L0_attn_b2"].reshape(1, -1).astype(f32)),
        "LRB": ac(inputs["L0_attn_lr_b"].reshape(1, -1).astype(f32)),
        "IDENT_F": np.eye(128, dtype=f32),
        "IDENT_B": np.eye(128).astype(ml_dtypes.bfloat16),
        "ONESB": np.ones((1, 128)).astype(ml_dtypes.bfloat16),
        "LMASK": np.tril(np.ones((128, 128), dtype=f32), -1),
        "TWOI": (2.0 * np.eye(128)).astype(ml_dtypes.bfloat16),
    }


def kernel(**inputs):
    x = np.ascontiguousarray(inputs["L0_attn_x"], dtype=np.float32)
    B = x.shape[0]
    shared = _prep_shared(inputs)
    nc = build(T=x.shape[1])
    in_maps = [dict(shared, X=np.ascontiguousarray(x[b])) for b in range(B)]
    res = run_bass_kernel_spmd(nc, in_maps, core_ids=list(range(B)))
    return np.stack([np.asarray(res.results[b]["OUT"], dtype=np.float32)
                     for b in range(B)], axis=0)
